# version 08 (final)
# speedup vs baseline: 1.5999x; 1.0345x over previous
import contextlib
import numpy as np
import ml_dtypes
import concourse.bass as bass
import concourse.mybir as mybir

F32 = mybir.dt.float32
BF16 = mybir.dt.bfloat16
SKIP_T_SELF = False
AF = mybir.ActivationFunctionType
OP = mybir.AluOpType
AX = mybir.AxisListType.X
BF = ml_dtypes.bfloat16

D = 1024
HV, HK, DK, DV = 16, 8, 64, 64
K_DIM, V_DIM, CONV_DIM = 512, 1024, 2048
KS = 4
NE, I_FF, NE1 = 8, 1408, 9
R_LORA = 32
CA, CB, SBOT = 128, 64, 8


class Eng:

    def __init__(self, name, sem, inc=1, skip_self=False):
        self.name, self.sem, self.inc = name, sem, inc
        self.skip_self = skip_self
        self.count = 0
        self.marks = {}
        self.thunks = []

    def dep(self, *tokens):
        for t in tokens:
            if t is None:
                continue
            sem, val = t
            if self.skip_self and sem is self.sem:
                continue
            key = id(sem)
            if self.marks.get(key, 0) >= val:
                continue
            self.marks[key] = val
            self.thunks.append(lambda e, s=sem, v=val: e.wait_ge(s, v))

    def emit(self, fn, *deps):
        self.dep(*deps)
        if self.skip_self and self.count % 8 == 0 and self.count > 32:
            self.thunks.append(
                lambda e, s=self.sem, v=self.count - 32: e.wait_ge(s, v))
        self.count += self.inc
        self.thunks.append(lambda e, f=fn, s=self.sem, i=self.inc: f(e).then_inc(s, i))
        return (self.sem, self.count)

    def tok(self):
        return (self.sem, self.count)

    def replay(self, engine):
        for th in self.thunks:
            th(engine)


class Buf:

    __slots__ = ("w", "r", "name", "dsem", "dcount")

    def __init__(self, name=""):
        self.name = name
        self.w = {}
        self.r = {}
        self.dsem = None
        self.dcount = 0

    def wtoks(self):
        return list(self.w.values())

    def rtoks(self):
        return list(self.r.values())

    def set_w(self, tok):
        self.w = {id(tok[0]): tok}
        self.r = {}

    def add_w(self, tok):
        self.w[id(tok[0])] = tok

    def add_r(self, tok):
        self.r[id(tok[0])] = tok


class Builder:
    def __init__(self, nc, TPC, stages, n_cores=8, parity=0):
        self.nc = nc
        self.parity = parity
        self.TPC = TPC
        self.NT = TPC // 128
        self.NTE = self.NT + 1
        self.stages = stages
        self.n_cores = n_cores
        self.inputs = {}
        self.dram = {}

    def din(self, name, shape, dtype=F32):
        t = self.nc.dram_tensor(name, list(shape), dtype, kind="ExternalInput").ap()
        self.dram[name] = t
        return t

    def dout(self, name, shape, dtype=F32):
        t = self.nc.dram_tensor(name, list(shape), dtype, kind="ExternalOutput").ap()
        return t

    def sbuf(self, name, shape, dtype=F32, stack=None):
        t = (stack or self.ctx).enter_context(
            self.nc.sbuf_tensor(name, list(shape), dtype))
        return t, Buf(name)

    def op(self, eng, fn, reads=(), writes=()):
        deps = []
        for b in reads:
            deps += b.wtoks()
        for b in writes:
            deps += b.wtoks() + b.rtoks()
        t = eng.emit(fn, *deps)
        for b in reads:
            b.add_r(t)
        for b in writes:
            b.set_w(t)
        return t

    def dma(self, eng, fn, dst, reads=()):
        if dst.dsem is None:
            dst.dsem = self.ctx.enter_context(
                self.nc.semaphore("dsem_" + dst.name))
            self.dma_bufs.append(dst)
        deps = []
        for b in reads:
            deps += b.wtoks()
        deps += dst.wtoks() + dst.rtoks()
        eng.dep(*deps)
        dst.dcount += 16
        eng.thunks.append(lambda e, f=fn, s=dst.dsem: f(e).then_inc(s, 16))
        tok = (dst.dsem, dst.dcount)
        for b in reads:
            b.add_r(tok)
        dst.set_w(tok)
        return tok

    def input_sbuf(self, name, arr, stack=None, cast=False):
        arr = np.ascontiguousarray(arr)
        hdt = {np.dtype(np.float32): F32, np.dtype(BF): BF16,
               np.dtype(np.int32): mybir.dt.int32}[arr.dtype]
        sdt = BF16 if cast else hdt
        d = self.din(name, arr.shape, hdt)
        s, b = self.sbuf(name + "_s", arr.shape, sdt, stack=stack)
        eng = self.GD if cast else self.SY
        self.dma(eng, lambda e, d=d, s=s: e.dma_start(out=s[:], in_=d[:]), b)
        self.inputs[name] = arr
        return s, b

    def barrier(self):
        toks = [e.tok() for e in self.engs]
        toks += [(b.dsem, b.dcount) for b in self.dma_bufs]
        for e in self.engs:
            e.dep(*toks)

    def build(self):
        nc = self.nc
        self.ctx = contextlib.ExitStack()
        with self.ctx:
            sems = {s: self.ctx.enter_context(nc.semaphore("sem_" + s))
                    for s in ("t", "v", "s", "g", "sy", "gd", "sd")}
            self.T = Eng("T", sems["t"], skip_self=SKIP_T_SELF)
            self.V = Eng("V", sems["v"])
            self.S = Eng("S", sems["s"])
            self.G = Eng("G", sems["g"])
            self.SY = Eng("SY", sems["sy"], inc=16)
            self.SD = self.S
            self.GD = self.G
            self.engs = [self.T, self.V, self.S, self.G, self.SY]
            self.dma_bufs = []
            self.psum = []
            self.psb = []
            for i in range(7):
                self.psum.append(self.ctx.enter_context(
                    nc.psum_tensor(f"pb{i}", [128, 512], F32)))
                self.psb.append(Buf(f"pb{i}"))
            self.pt8 = None
            self.pt8b = None
            self.declare_static()
            self.program()
            with nc.Block() as block:
                block.tensor(lambda e: self.T.replay(e))
                block.vector(lambda e: self.V.replay(e))
                block.scalar(lambda e: self.S.replay(e))
                block.gpsimd(lambda e: self.G.replay(e))
                block.sync(lambda e: self.SY.replay(e))
        return nc

    def declare_static(self):
        NTE = self.NTE
        self.x_tm, self.xb = self.sbuf("x_tm", [128, self.NT * 1024], F32)
        self.xov_stack = contextlib.ExitStack()
        self.xov = self.xov_stack.enter_context(
            self.nc.sbuf_tensor("x_ov", [128, 1024], F32, side="right"))
        self.lnT, self.lnb = self.sbuf("lnT", [128, 8 * NTE * 128], BF16)
        self.cst, self.cstb = self.sbuf("cstv", [128, 4], F32)
        self.cpar, self.cparb = self.input_sbuf(
            "c_par", np.full((128, 1), float(self.parity), np.float32))
        self.op(self.V, lambda e: e.memset(self.cst[:, 0:1], 1e-6),
                writes=[self.cstb])
        self.identf, self.identf_b = self.input_sbuf(
            "identf", np.eye(128, dtype=np.float32))
        self.identb, self.identb_b = self.input_sbuf(
            "identb", np.eye(128, dtype=np.float32).astype(BF))
        self.ones1b, self.ones1b_b = self.input_sbuf(
            "ones1b", np.ones((1, 128), np.float32).astype(BF))
        sel = np.zeros((16, 9 * 128), np.float32)
        for e in range(9):
            sel[e, e * 128:(e + 1) * 128] = 1.0
        self.selb, self.selb_b = self.input_sbuf("selb", sel.astype(BF))
        xin = self.din("x0ext", [128, NTE * 1024], F32)
        self.dma(self.SY, lambda e: e.dma_start(out=self.xov[:],
                                                in_=xin[:, 0:1024]), self.xb)
        self.dma(self.SY, lambda e: e.dma_start(
            out=self.x_tm[:], in_=xin[:, 1024:NTE * 1024]), self.xb)

    def lnT_ap(self, k, tile, cols=128):
        o = k * self.NTE * 128 + tile * 128
        return self.lnT[:, o:o + cols]

    def lnT_cols(self, k, c0, c1):
        o = k * self.NTE * 128
        return self.lnT[:, o + c0:o + c1]

    def x_ap(self, tile, c0=0, c1=1024):
        if tile == 0:
            return self.xov[:, c0:c1]
        return self.x_tm[:, (tile - 1) * 1024 + c0:(tile - 1) * 1024 + c1]

    def free_xov(self):
        if self.xov_stack is not None:
            self.barrier()
            self.xov_stack.close()
            self.xov_stack = None

    def emit_ln(self, t0, t1):
        V, S, T = self.V, self.S, self.T
        self._ln_no = getattr(self, "_ln_no", 0) + 1
        pfx = f"ln{self._ln_no}_"
        with contextlib.ExitStack() as ph:
            st, stb = self.sbuf(pfx + "st", [128, 16], F32, stack=ph)
            sc, scb = self.sbuf(pfx + "sc", [128, 2 * 1024], F32, stack=ph)
            tmp, tmpb = self.sbuf(pfx + "tmp", [128, 2 * 1024], F32, stack=ph)
            stbs = [Buf(f"st{p}") for p in range(2)]
            scbs = [Buf(f"sc{p}") for p in range(2)]
            tmpbs = [Buf(f"tmp{p}") for p in range(2)]
            for i in range(t0, t1):
                p = i % 2
                xap = self.x_ap(i)
                sb = stbs[p]
                c = lambda a, b_, p=p: st[:, p * 8 + a: p * 8 + b_]
                self.op(V, lambda e, xap=xap, c=c: e.tensor_reduce(
                    c(0, 1), xap, AX, OP.add), reads=[self.xb], writes=[sb])
                self.op(V, lambda e, c=c: e.tensor_scalar_mul(
                    c(1, 2), c(0, 1), -1.0 / 1024.0), reads=[sb], writes=[sb])
                self.op(S, lambda e, xap=xap, c=c, p=p: e.activation(
                    sc[:, p * 1024:(p + 1) * 1024], xap, AF.Square,
                    bias=c(1, 2), scale=1.0, accum_out=c(2, 3)),
                    reads=[self.xb, sb], writes=[scbs[p], sb])
                self.op(V, lambda e, c=c: e.tensor_scalar(
                    c(3, 4), c(2, 3), 1.0 / 1024.0, 1e-5, op0=OP.mult, op1=OP.add),
                    reads=[sb], writes=[sb])
                self.op(V, lambda e, c=c: e.reciprocal(c(4, 5), c(3, 4)),
                        reads=[sb], writes=[sb])
                self.op(S, lambda e, c=c: e.activation(c(5, 6), c(4, 5), AF.Sqrt),
                        reads=[sb], writes=[sb])
                self.op(V, lambda e, xap=xap, c=c, p=p: e.tensor_scalar(
                    tmp[:, p * 1024:(p + 1) * 1024], xap, c(1, 2), c(5, 6),
                    op0=OP.add, op1=OP.mult),
                    reads=[self.xb, sb], writes=[tmpbs[p]])
                for half in range(2):
                    pj = 5 + half
                    for kk in range(4):
                        k = half * 4 + kk
                        self.op(T, lambda e, k=k, kk=kk, pj=pj, p=p: e.transpose(
                            self.psum[pj][:, kk * 128:(kk + 1) * 128],
                            tmp[:, p * 1024 + k * 128:p * 1024 + (k + 1) * 128],
                            self.identf[:]),
                            reads=[tmpbs[p], self.identf_b], writes=[self.psb[pj]])
                    for kk in range(4):
                        k = half * 4 + kk
                        self.op(S, lambda e, k=k, kk=kk, pj=pj, i=i: e.copy(
                            self.lnT_ap(k, i), self.psum[pj][:, kk * 128:(kk + 1) * 128]),
                            reads=[self.psb[pj]], writes=[self.lnb])
        self.barrier()

    def prep_ffn_weights(self, p, base):
        gw = np.concatenate([np.asarray(p[base + "gate_w"]),
                             np.asarray(p[base + "shared_gate_w"])], 0).T
        gw = np.concatenate([gw, np.zeros((1024, 7), np.float32)], 1)
        self.inputs["ffn_gatewT"] = tile_w(gw)
        wg, wu, wd = (np.asarray(p[base + w]) for w in ("w_gate", "w_up", "w_down"))
        for e in range(NE):
            self.inputs[f"ffn_e{e}_g"] = tile_w(wg[e])
            self.inputs[f"ffn_e{e}_u"] = tile_w(wu[e])
            self.inputs[f"ffn_e{e}_d"] = tile_w(pad_rows(wd[e], 1536))
        self.inputs["ffn_e8_g"] = tile_w(np.asarray(p[base + "sw_gate"]))
        self.inputs["ffn_e8_u"] = tile_w(np.asarray(p[base + "sw_up"]))
        self.inputs["ffn_e8_d"] = tile_w(pad_rows(np.asarray(p[base + "sw_down"]), 1536))

    def declare_ffn_weights(self):
        self.ffn_w = {"gatewT": self.din("ffn_gatewT", [128, 8 * 16], F32)}
        self.ffn_wc = {}
        self.ffn_wcb = {}
        for e in range(NE1):
            for j, wdt in (("g", 8 * 1408), ("u", 8 * 1408), ("d", 12 * 1024)):
                self.ffn_w[(e, j)] = self.din(f"ffn_e{e}_{j}", [128, wdt], F32)
                self.ffn_wc[(e, j)] = self.nc.dram_tensor(
                    f"ffnc_e{e}_{j}", [128, wdt], BF16, kind="Internal").ap()
        cb = Buf("ffnc_all")
        for e in range(NE1):
            for j in ("g", "u", "d"):
                self.ffn_wcb[(e, j)] = cb

    def emit_ffn_cache(self):
        for e in range(NE1):
            for j in ("g", "u", "d"):
                self.dma(self.GD, lambda en, e=e, j=j: en.dma_start(
                    out=self.ffn_wc[(e, j)][:], in_=self.ffn_w[(e, j)][:]),
                    self.ffn_wcb[(e, j)])

    def emit_ffn(self, tag):
        T, V, S, G, GD, SY = self.T, self.V, self.S, self.G, self.GD, self.SY
        NT = self.NT
        ps, pb = self.psum, self.psb
        with contextlib.ExitStack() as ph:
            ps7 = ph.enter_context(nc_psum(self.nc, tag + "p7"))
            pb7 = Buf(tag + "p7")
            gwt, gwtb = self.sbuf(tag + "gwt", [128, 8 * 16], BF16, stack=ph)
            self.dma(GD, lambda e: e.dma_start(out=gwt[:], in_=self.ffn_w["gatewT"][:]),
                     gwtb)
            rl, rlb = self.sbuf(tag + "rl", [128, 16], F32, stack=ph)
            rs, rsb = self.sbuf(tag + "rs", [128, 8], F32, stack=ph)
            comb_all, combb = self.sbuf(tag + "comb", [128, NT * 16], F32, stack=ph)
            scr, scrb = self.sbuf(tag + "scr", [128, 16], F32, stack=ph)
            for i in range(1, NT + 1):
                cm = lambda a, b_, i=i: comb_all[:, (i - 1) * 16 + a:(i - 1) * 16 + b_]
                for k in range(8):
                    self.op(T, lambda e, k=k, i=i: e.matmul(
                        ps[0][:, 0:16], self.lnT_ap(k, i), gwt[:, k * 16:(k + 1) * 16],
                        start=(k == 0), stop=(k == 7)),
                        reads=[self.lnb, gwtb], writes=[pb[0]])
                self.op(V, lambda e: e.tensor_copy(rl[:], ps[0][:, 0:16]),
                        reads=[pb[0]], writes=[rlb])
                self.op(V, lambda e: e.reduce_max(rs[:, 0:1], rl[:, 0:8], AX),
                        reads=[rlb], writes=[rsb])
                self.op(V, lambda e: e.tensor_scalar_mul(rs[:, 1:2], rs[:, 0:1], -1.0),
                        reads=[rsb], writes=[rsb])
                self.op(S, lambda e: e.activation(rl[:, 0:8], rl[:, 0:8], AF.Exp,
                                                  bias=rs[:, 1:2], scale=1.0),
                        reads=[rsb, rlb], writes=[rlb])
                self.op(V, lambda e: e.reduce_max(rs[:, 3:4], rl[:, 0:8], AX),
                        reads=[rlb], writes=[rsb])
                self.op(V, lambda e, cm=cm: e.tensor_single_scalar(
                    cm(0, 8), rl[:, 0:8], rs[:, 3:4], OP.is_lt),
                    reads=[rlb, rsb], writes=[combb])
                self.op(V, lambda e, cm=cm: e.tensor_mul(scr[:, 0:8], rl[:, 0:8],
                                                         cm(0, 8)),
                        reads=[rlb, combb], writes=[scrb])
                self.op(V, lambda e: e.reduce_max(rs[:, 4:5], scr[:, 0:8], AX),
                        reads=[scrb], writes=[rsb])
                self.op(V, lambda e: e.tensor_single_scalar(
                    scr[:, 0:8], rl[:, 0:8], rs[:, 4:5], OP.is_ge),
                    reads=[rlb, rsb], writes=[scrb])
                self.op(V, lambda e: e.tensor_add(rs[:, 5:6], rs[:, 3:4], rs[:, 4:5]),
                        reads=[rsb], writes=[rsb])
                self.op(V, lambda e: e.reciprocal(rs[:, 6:7], rs[:, 5:6]),
                        reads=[rsb], writes=[rsb])
                self.op(V, lambda e, cm=cm: e.scalar_tensor_tensor(
                    cm(0, 8), rl[:, 0:8], rs[:, 6:7], scr[:, 0:8],
                    op0=OP.mult, op1=OP.mult),
                    reads=[rlb, rsb, scrb], writes=[combb])
                self.op(S, lambda e, cm=cm: e.activation(cm(8, 9), rl[:, 8:9],
                                                         AF.Sigmoid),
                        reads=[rlb], writes=[combb])
            self.barrier()

            wgs, wgb = self.sbuf(tag + "wgs", [128, 8 * 1408], BF16, stack=ph)
            wus, wub = self.sbuf(tag + "wus", [128, 8 * 1408], BF16, stack=ph)
            wds, wdb = self.sbuf(tag + "wds", [128, 12 * 1024], BF16, stack=ph)
            h_buf, _ = self.sbuf(tag + "h", [128, 11 * 512], BF16, stack=ph)
            hb = [Buf(f"h{m}") for m in range(11)]
            sg_buf, sgb = self.sbuf(tag + "sg", [128, 1024], F32, stack=ph)
            sgb2 = [Buf("sg0"), Buf("sg1")]
            NCH = NT * 128 // 512

            def down_mms(e, ch, m, tts):
                for tt in tts:
                    for dc in range(2):
                        bk = (tt % 2) * 2 + dc
                        self.op(T, lambda en, m=m, tt=tt, dc=dc, bk=bk:
                                en.matmul(
                            ps[bk][:, 0:512],
                            h_buf[:, m * 512 + tt * 128:m * 512 + (tt + 1) * 128],
                            wds[:, m * 1024 + dc * 512:m * 1024 + (dc + 1) * 512],
                            start=(m == 0), stop=(m == 10)),
                            reads=[hb[m], wdb], writes=[pb[bk]])

            def down_evac(e, ch, tts):
                for tt in tts:
                    tile = 1 + 4 * ch + tt
                    for dc in range(2):
                        bk = (tt % 2) * 2 + dc
                        xap = self.x_ap(tile, dc * 512, (dc + 1) * 512)
                        cm = comb_all[:, (tile - 1) * 16 + e:(tile - 1) * 16 + e + 1]
                        self.op(V, lambda en, xap=xap, bk=bk, cm=cm:
                                en.scalar_tensor_tensor(
                            xap, ps[bk][:, 0:512], cm, xap,
                            op0=OP.mult, op1=OP.add),
                            reads=[pb[bk], combb, self.xb], writes=[self.xb])

            for e in range(NE1):
                self.dma(SY, lambda en, e=e: en.dma_start(
                    out=wgs[:], in_=self.ffn_wc[(e, "g")][:]), wgb,
                    reads=[self.ffn_wcb[(e, "g")]])
                self.dma(SY, lambda en, e=e: en.dma_start(
                    out=wus[:], in_=self.ffn_wc[(e, "u")][:]), wub,
                    reads=[self.ffn_wcb[(e, "u")]])
                self.dma(SY, lambda en, e=e: en.dma_start(
                    out=wds[:], in_=self.ffn_wc[(e, "d")][:]), wdb,
                    reads=[self.ffn_wcb[(e, "d")]])
                for ch in range(NCH):
                    for m in range(11):
                        hs = m % 2
                        pgx, pgxb = (ps[4], pb[4]) if hs == 0 else (ps[5], pb[5])
                        pux, puxb = (ps[6], pb[6]) if hs == 0 else (ps7, pb7)
                        for k in range(8):
                            self.op(T, lambda en, k=k, m=m, ch=ch, pgx=pgx:
                                    en.matmul(
                                pgx[:, 0:512],
                                wgs[:, k * 1408 + m * 128:k * 1408 + (m + 1) * 128],
                                self.lnT_cols(k, 128 + ch * 512,
                                              128 + (ch + 1) * 512),
                                start=(k == 0), stop=(k == 7)),
                                reads=[wgb, self.lnb], writes=[pgxb])
                        for k in range(8):
                            self.op(T, lambda en, k=k, m=m, ch=ch, pux=pux:
                                    en.matmul(
                                pux[:, 0:512],
                                wus[:, k * 1408 + m * 128:k * 1408 + (m + 1) * 128],
                                self.lnT_cols(k, 128 + ch * 512,
                                              128 + (ch + 1) * 512),
                                start=(k == 0), stop=(k == 7)),
                                reads=[wub, self.lnb], writes=[puxb])
                        self.op(S, lambda en, hs=hs, pgx=pgx: en.activation(
                            sg_buf[:, hs * 512:(hs + 1) * 512], pgx[:, 0:512],
                            AF.Silu),
                            reads=[pgxb], writes=[sgb2[hs]])
                        self.op(V, lambda en, m=m, hs=hs, pux=pux: en.tensor_mul(
                            h_buf[:, m * 512:(m + 1) * 512],
                            sg_buf[:, hs * 512:(hs + 1) * 512],
                            pux[:, 0:512]),
                            reads=[sgb2[hs], puxb], writes=[hb[m]])
                        if m >= 1:
                            down_mms(e, ch, m - 1, (0, 1))
                    down_mms(e, ch, 10, (0, 1))
                    down_evac(e, ch, (0, 1))
                    for m in range(11):
                        down_mms(e, ch, m, (2, 3))
                    down_evac(e, ch, (2, 3))
        self.barrier()

    def prep_attn_a_weights(self, p, base):
        self.inputs["aa_wqkv"] = tile_w(np.asarray(p[base + "W_qkv"]))
        self.inputs["aa_wz"] = tile_w(np.asarray(p[base + "W_z"]))
        wba = np.concatenate([np.asarray(p[base + "W_b"]),
                              np.asarray(p[base + "W_a"])], 1)
        self.inputs["aa_wba"] = tile_w(wba)
        self.inputs["aa_wout"] = tile_w(np.asarray(p[base + "W_out"]))
        cw = np.asarray(p[base + "conv_w"], np.float32)
        self.inputs["aa_convw"] = np.ascontiguousarray(
            cw.reshape(16, 128, 4).transpose(1, 0, 2).reshape(128, 64))
        self.inputs["aa_dtb"] = np.tile(np.asarray(p[base + "dt_bias"],
                                                   np.float32), (128, 1))
        self.inputs["aa_alog"] = np.tile(np.asarray(p[base + "A_log"],
                                                    np.float32), (128, 1))
        self.inputs["aa_normw"] = np.tile(np.asarray(p[base + "norm_w"],
                                                     np.float32), (128, 16))

    def declare_attn_a_weights(self):
        self.aa_w = {}
        for n, sh in (("aa_wqkv", [128, 8 * 2048]), ("aa_wz", [128, 8 * 1024]),
                      ("aa_wba", [128, 8 * 32]), ("aa_wout", [128, 8 * 1024]),
                      ("aa_convw", [128, 64]), ("aa_dtb", [128, 16]),
                      ("aa_alog", [128, 16]), ("aa_normw", [128, 1024])):
            self.aa_w[n] = self.din(n, sh, F32)

    def emit_attn_a(self, tag, use_cc):
        T, V, S, G, SY, GD = self.T, self.V, self.S, self.G, self.SY, self.GD
        NT, NTE, nc = self.NT, self.NTE, self.nc
        ps, pb = self.psum, self.psb
        op = self.op
        SCW = 2
        NSC = (NT + SCW - 1) // SCW
        with contextlib.ExitStack() as ph:
            self.pt8 = ph.enter_context(
                nc.psum_tensor(tag + "pt8", [128, 1024], BF16))
            self.pt8b = Buf(tag + "pt8")
            sb = lambda n, sh, dt=F32, st=None: self.sbuf(tag + n, sh, dt,
                                                          stack=st or ph)
            wslot, wslotb = sb("wslot", [128, 8 * 2048], BF16)
            self.dma(GD, lambda e: e.dma_start(out=wslot[:],
                                               in_=self.aa_w["aa_wqkv"][:]), wslotb)
            wba, wbab = sb("wba", [128, 8 * 32], BF16)
            self.dma(GD, lambda e: e.dma_start(out=wba[:],
                                               in_=self.aa_w["aa_wba"][:]), wbab)
            cvw, cvwb = sb("cvw", [128, 64])
            self.dma(SY, lambda e: e.dma_start(out=cvw[:],
                                               in_=self.aa_w["aa_convw"][:]), cvwb)
            dtb, dtbb = sb("dtb", [128, 16])
            self.dma(SY, lambda e: e.dma_start(out=dtb[:],
                                               in_=self.aa_w["aa_dtb"][:]), dtbb)
            alog, alogb = sb("alog", [128, 16])
            self.dma(SY, lambda e: e.dma_start(out=alog[:],
                                               in_=self.aa_w["aa_alog"][:]), alogb)

            bg, bgb = sb("bg", [128, NT * 32])
            beta, betab = sb("beta", [128, NT * 16])
            gbuf, gbufb = sb("g", [128, NT * 16])
            lnbta, lnbtab = sb("lnb", [128, NT * 16])
            Gc, Gcb = sb("G", [128, NT * 16])
            negG, negGb = sb("negG", [128, NT * 16])
            glb, glbb = sb("glb", [128, NT * 16])
            eglb, eglbb = sb("eglb", [128, NT * 16])
            nexpa, nexpab = sb("nexpa", [128, 16])
            spsc, spscb = sb("spsc", [128, 48])

            def softplus(dst_ap, dstb, src_ap, srcb, neg=False):
                op = self.op
                op(self.S, lambda e: e.activation(spsc[:, 0:16], src_ap, AF.Abs),
                   reads=[srcb], writes=[spscb])
                op(self.S, lambda e: e.activation(spsc[:, 16:32], spsc[:, 0:16],
                                                  AF.Exp, scale=-1.0),
                   reads=[spscb], writes=[spscb])
                op(self.S, lambda e: e.activation(spsc[:, 16:32], spsc[:, 16:32],
                                                  AF.Ln, bias=1.0),
                   reads=[spscb], writes=[spscb])
                op(self.S, lambda e, sc=-1.0 if neg else 1.0: e.activation(
                    spsc[:, 32:48], src_ap, AF.Relu, scale=sc),
                   reads=[srcb], writes=[spscb])
                op(self.V, lambda e: e.tensor_add(dst_ap, spsc[:, 16:32],
                                                  spsc[:, 32:48]),
                   reads=[spscb], writes=[dstb])
            egcrow, egcrowb = sb("egcrow", [1, NT * 16])
            egc, egcb = sb("egc", [128, NT * 16])
            tril_i = np.triu(np.ones((128, 128), np.float32))
            ctril, ctrilb = self.input_sbuf("c_tril", tril_i, stack=ph)
            sel16 = np.zeros((16, 16 * 128), np.float32)
            for h in range(16):
                sel16[h, h * 128:(h + 1) * 128] = 1.0
            csel, cselb = self.input_sbuf("c_sel16", sel16, stack=ph)
            ones1f, ones1fb = self.input_sbuf("c_ones1f",
                                              np.ones((1, 128), np.float32), stack=ph)
            cmst, cmstb = self.input_sbuf(
                "c_mst", 1e30 * np.triu(np.ones((128, 128), np.float32), 0), stack=ph)
            cminc, cmincb = self.input_sbuf(
                "c_minc", -1e30 * np.tril(np.ones((128, 128), np.float32), -1),
                stack=ph)
            sel2 = np.zeros((128, 2), np.float32)
            sel2[:64, 0] = 1.0
            sel2[64:, 1] = 1.0
            csel2, csel2b = self.input_sbuf("c_sel2", sel2, stack=ph)
            ex2 = np.zeros((2, 128), np.float32)
            ex2[0, :64] = 1.0
            ex2[1, 64:] = 1.0
            cex2, cex2b = self.input_sbuf("c_ex2", ex2.astype(BF), stack=ph)
            cex2q, cex2qb = self.input_sbuf(
                "c_ex2q", (ex2 * (DK ** -0.5)).astype(BF), stack=ph)
            cpar, cparb = self.cpar, self.cparb

            Sst, Sstb = sb("Sst", [128, 2 * 8 * 64], BF16)
            NI = SCW * 16
            dO0, dO0b = sb("dO0", [128, NI * 64], BF16)
            dWY, dWYb = sb("dWY", [128, (NI // 2) * 128], BF16)
            dQg, dQgb = sb("dQg", [128, (NI // 2) * 128], BF16)
            dMn, dMnb = sb("dMn", [128, (NI // 2) * 64], BF16)
            dIc, dIcb = sb("dIc", [128, (NI // 2) * 64], BF16)
            spl = {}
            splb = {}
            for n, w in (("O0", NI * 64), ("WY", (NI // 2) * 128),
                         ("Qg", (NI // 2) * 128), ("Mn", (NI // 2) * 64),
                         ("Ic", (NI // 2) * 64)):
                spl[n] = nc.dram_tensor(tag + "sp" + n, [128, NSC * w], BF16,
                                        kind="Internal").ap()
                splb[n] = Buf("sp" + n)
            dlocal = {"O0": (dO0, dO0b, NI * 64), "WY": (dWY, dWYb, (NI // 2) * 128),
                      "Qg": (dQg, dQgb, (NI // 2) * 128),
                      "Mn": (dMn, dMnb, (NI // 2) * 64),
                      "Ic": (dIc, dIcb, (NI // 2) * 64)}

            def pk(base, hv, w):
                return (64 * (hv % 2), base * w + 0)

            op(S, lambda e: e.activation(nexpa[:], alog[:], AF.Exp),
               reads=[alogb], writes=[nexpab])
            op(V, lambda e: e.tensor_scalar_mul(nexpa[:], nexpa[:], -1.0),
               reads=[nexpab], writes=[nexpab])
            for i in range(NT):
                tl = i + 1
                for k in range(8):
                    op(T, lambda e, k=k, tl=tl: e.matmul(
                        ps[6][:, 0:32], self.lnT_ap(k, tl),
                        wba[:, k * 32:(k + 1) * 32],
                        start=(k == 0), stop=(k == 7)),
                        reads=[self.lnb, wbab], writes=[pb[6]])
                op(V, lambda e, i=i: e.tensor_copy(bg[:, i * 32:(i + 1) * 32],
                                                   ps[6][:, 0:32]),
                   reads=[pb[6]], writes=[bgb])
                op(S, lambda e, i=i: e.activation(
                    beta[:, i * 16:(i + 1) * 16], bg[:, i * 32:i * 32 + 16],
                    AF.Sigmoid), reads=[bgb], writes=[betab])
                softplus(lnbta[:, i * 16:(i + 1) * 16], lnbtab,
                         bg[:, i * 32:i * 32 + 16], bgb, neg=True)
                op(V, lambda e, i=i: e.tensor_scalar_mul(
                    lnbta[:, i * 16:(i + 1) * 16], lnbta[:, i * 16:(i + 1) * 16],
                    -1.0), reads=[lnbtab], writes=[lnbtab])
                op(V, lambda e, i=i: e.tensor_add(
                    gbuf[:, i * 16:(i + 1) * 16], bg[:, i * 32 + 16:i * 32 + 32],
                    dtb[:]), reads=[bgb, dtbb], writes=[gbufb])
                softplus(gbuf[:, i * 16:(i + 1) * 16], gbufb,
                         gbuf[:, i * 16:(i + 1) * 16], gbufb, neg=False)
                op(V, lambda e, i=i: e.tensor_mul(
                    gbuf[:, i * 16:(i + 1) * 16], gbuf[:, i * 16:(i + 1) * 16],
                    nexpa[:]), reads=[gbufb, nexpab], writes=[gbufb])
                op(T, lambda e, i=i: e.matmul(
                    ps[6][:, 0:16], ctril[:], gbuf[:, i * 16:(i + 1) * 16],
                    start=True, stop=True),
                    reads=[gbufb, ctrilb], writes=[pb[6]])
                op(V, lambda e, i=i: e.tensor_copy(Gc[:, i * 16:(i + 1) * 16],
                                                   ps[6][:, 0:16]),
                   reads=[pb[6]], writes=[Gcb])
            op(V, lambda e: e.tensor_scalar_mul(negG[:], Gc[:], -1.0),
               reads=[Gcb], writes=[negGb])
            op(V, lambda e: e.tensor_add(glb[:], Gc[:], lnbta[:]),
               reads=[Gcb, lnbtab], writes=[glbb])
            op(S, lambda e: e.activation(eglb[:], glb[:], AF.Exp),
               reads=[glbb], writes=[eglbb])
            self.dma(SY, lambda e: e.dma_start(out=egcrow[:],
                                               in_=Gc[127:128, :]), egcrowb,
                     reads=[Gcb])
            op(S, lambda e: e.activation(egcrow[:], egcrow[:], AF.Exp),
               reads=[egcrowb], writes=[egcrowb])
            for i in range(NT):
                op(T, lambda e, i=i: e.matmul(
                    ps[6][:, 0:16], ones1f[:],
                    egcrow[:, i * 16:(i + 1) * 16], start=True, stop=True),
                    reads=[egcrowb, ones1fb], writes=[pb[6]])
                op(V, lambda e, i=i: e.tensor_copy(egc[:, i * 16:(i + 1) * 16],
                                                   ps[6][:, 0:16]),
                   reads=[pb[6]], writes=[egcb])
            op(V, lambda e: e.memset(Sst[:], 0.0), writes=[Sstb])
            self.barrier()

            with contextlib.ExitStack() as pa:
                TW = SCW * 128
                pre, preb = sb("pre", [128, 16 * (TW + 3)], BF16, st=pa)
                qkv, qkvb = pre, preb
                acc, accb = sb("acc", [128, TW], st=pa)
                tmp7, tmp7b = sb("tmp7", [128, 16 * 8], BF16, st=pa)
                ssq, ssqb = sb("ssq", [2, 8 * SCW * 128], BF16, st=pa)
                ktok, ktokb = sb("ktok", [128, SCW * 512], BF16, st=pa)
                vtok, vtokb = sb("vtok", [128, SCW * 1024], BF16, st=pa)
                GT, GTb = sb("GT", [16, 128], st=pa)
                eGT, eGTb = sb("eGT", [16, 128], st=pa)
                arg1, arg1b = sb("arg1", [128, 128], st=pa)
                prex, prexb = sb("prex", [128, 128], st=pa)
                Anb_, Anbb = sb("An", [128, 128], BF16, st=pa)
                Atb_, Atbb = sb("At", [128, 128], BF16, st=pa)
                Wtb_, Wtbb = sb("Wt", [128, 128], BF16, st=pa)
                wpre, wpreb = sb("wpre", [128, 128], st=pa)
                kbar, kbarb = sb("kbar", [128, 64], BF16, st=pa)
                rhs, rhsb = sb("rhs", [128, 128], BF16, st=pa)
                X, Xb = sb("X", [128, 128], BF16, st=pa)
                pw = {}
                pwb = {}
                for pname in ("t2", "t4", "t8", "t16", "t32", "t64", "n_a", "n_b"):
                    pw[pname], pwb[pname] = sb("P" + pname, [128, 128], BF16, st=pa)
                for m in range(16):
                    op(V, lambda e, m=m: e.memset(
                        pre[:, m * (TW + 3):m * (TW + 3) + 3], 0.0), writes=[preb])

                for sc in range(NSC):
                    c0 = sc * SCW
                    nch = min(SCW, NT - c0)
                    tw = nch * 128
                    PW = TW + 3
                    for m in range(16):
                        for k in range(8):
                            op(T, lambda e, m=m, k=k, tw=tw, c0=c0: e.matmul(
                                ps[6][:, 0:tw],
                                wslot[:, k * 2048 + m * 128:k * 2048 + (m + 1) * 128],
                                self.lnT_cols(k, 128 + c0 * 128,
                                              128 + c0 * 128 + tw),
                                start=(k == 0), stop=(k == 7)),
                                reads=[self.lnb, wslotb], writes=[pb[6]])
                        op(V, lambda e, m=m, tw=tw, PW=PW: e.tensor_copy(
                            pre[:, m * PW + 3:m * PW + 3 + tw], ps[6][:, 0:tw]),
                           reads=[pb[6]], writes=[preb])
                    for m in range(16):
                        for k in range(4):
                            fn = (lambda e, m=m, k=k, tw=tw, PW=PW:
                                  e.tensor_scalar_mul(
                                      acc[:, 3:tw],
                                      pre[:, m * PW + 3 + k:m * PW + k + tw],
                                      cvw[:, m * 4 + k:m * 4 + k + 1]))                                 if k == 0 else                                 (lambda e, m=m, k=k, tw=tw, PW=PW:
                                 e.scalar_tensor_tensor(
                                     acc[:, 3:tw],
                                     pre[:, m * PW + 3 + k:m * PW + k + tw],
                                     cvw[:, m * 4 + k:m * 4 + k + 1], acc[:, 3:tw],
                                     op0=OP.mult, op1=OP.add))
                            op(V, fn, reads=[preb, cvwb, accb], writes=[accb])
                        op(V, lambda e, m=m, PW=PW: e.tensor_copy(
                            tmp7[:, m * 8:m * 8 + 3], pre[:, m * PW:m * PW + 3]),
                            reads=[preb], writes=[tmp7b])
                        op(V, lambda e, m=m, PW=PW: e.tensor_copy(
                            tmp7[:, m * 8 + 3:m * 8 + 6],
                            pre[:, m * PW + 3:m * PW + 6]),
                            reads=[preb], writes=[tmp7b])
                        for k in range(4):
                            fn = (lambda e, m=m, k=k: e.tensor_scalar_mul(
                                acc[:, 0:3], tmp7[:, m * 8 + k:m * 8 + k + 3],
                                cvw[:, m * 4 + k:m * 4 + k + 1])) if k == 0 else                                 (lambda e, m=m, k=k: e.scalar_tensor_tensor(
                                    acc[:, 0:3], tmp7[:, m * 8 + k:m * 8 + k + 3],
                                    cvw[:, m * 4 + k:m * 4 + k + 1], acc[:, 0:3],
                                    op0=OP.mult, op1=OP.add))
                            op(V, fn, reads=[tmp7b, cvwb, accb], writes=[accb])
                        op(V, lambda e, m=m, tw=tw, PW=PW: e.tensor_copy(
                            pre[:, m * PW:m * PW + 3],
                            pre[:, m * PW + tw:m * PW + tw + 3]),
                            reads=[preb], writes=[preb])
                        op(S, lambda e, m=m, tw=tw: e.activation(
                            pre[:, m * PW + 3:m * PW + 3 + tw], acc[:, 0:tw], AF.Silu),
                            reads=[accb], writes=[qkvb])
                    for m in range(8):
                        op(S, lambda e, m=m, tw=tw: e.activation(
                            acc[:, 0:tw], pre[:, m * PW + 3:m * PW + 3 + tw], AF.Square),
                            reads=[qkvb], writes=[accb])
                        for ch in range(nch):
                            op(T, lambda e, ch=ch: e.matmul(
                                ps[6][:2, 0:128], csel2[:],
                                acc[:, ch * 128:(ch + 1) * 128],
                                start=True, stop=True),
                                reads=[accb, csel2b], writes=[pb[6]])
                            op(V, lambda e, m=m, ch=ch: e.tensor_copy(
                                ssq[:, (m * SCW + ch) * 128:
                                    (m * SCW + ch) * 128 + 128],
                                ps[6][:2, 0:128]),
                               reads=[pb[6]], writes=[ssqb])
                    nss = 8 * tw
                    op(S, lambda e, nss=nss: e.activation(
                        ssq[:, 0:nss], ssq[:, 0:nss], AF.Ln,
                        bias=self.cst[0:2, 0:1]),
                       reads=[ssqb, self.cstb], writes=[ssqb])
                    op(S, lambda e, nss=nss: e.activation(
                        ssq[:, 0:nss], ssq[:, 0:nss], AF.Exp, scale=-0.5),
                       reads=[ssqb], writes=[ssqb])
                    for m in range(8):
                        exc, excb = (cex2q, cex2qb) if m < 4 else (cex2, cex2b)
                        for ch in range(nch):
                            op(T, lambda e, m=m, ch=ch, exc=exc: e.matmul(
                                ps[6][:, 0:128], exc[:],
                                ssq[:, (m * SCW + ch) * 128:
                                    (m * SCW + ch) * 128 + 128],
                                start=True, stop=True),
                                reads=[ssqb, excb], writes=[pb[6]])
                            op(V, lambda e, m=m, ch=ch: e.tensor_mul(
                                pre[:, m * PW + 3 + ch * 128:m * PW + 3 + (ch + 1) * 128],
                                pre[:, m * PW + 3 + ch * 128:m * PW + 3 + (ch + 1) * 128],
                                ps[6][:, 0:128]),
                               reads=[pb[6], qkvb], writes=[qkvb])
                    for ch in range(nch):
                        for m in range(4, 16):
                            dst, dstb, doff = (ktok, ktokb,
                                               ch * 512 + (m - 4) * 128)                                 if m < 8 else (vtok, vtokb,
                                               ch * 1024 + (m - 8) * 128)
                            op(T, lambda e, m=m, ch=ch: e.transpose(
                                self.pt8[:, 0:128],
                                pre[:, m * PW + 3 + ch * 128:m * PW + 3 + (ch + 1) * 128],
                                self.identb[:]),
                                reads=[qkvb, self.identb_b], writes=[self.pt8b])
                            op(V, lambda e, dst=dst, doff=doff: e.tensor_copy(
                                dst[:, doff:doff + 128], self.pt8[:, 0:128]),
                               reads=[self.pt8b], writes=[dstb])
                    for ch in range(nch):
                        ci = c0 + ch
                        op(T, lambda e, ci=ci: e.transpose(
                            ps[6][:16, 0:128], Gc[:, ci * 16:(ci + 1) * 16],
                            self.identf[:]),
                            reads=[Gcb, self.identf_b], writes=[pb[6]])
                        op(V, lambda e: e.tensor_copy(GT[:], ps[6][:16, 0:128]),
                           reads=[pb[6]], writes=[GTb])
                        op(S, lambda e: e.activation(eGT[:], GT[:], AF.Exp),
                           reads=[GTb], writes=[eGTb])
                        for hv in range(16):
                            hk = hv // 2
                            kq_m = 4 + hk // 2
                            q_m = hk // 2
                            hko = 64 * (hk % 2)
                            kd = pre[hko:hko + 64,
                                     kq_m * PW + 3 + ch * 128:
                                     kq_m * PW + 3 + (ch + 1) * 128]
                            qd = pre[hko:hko + 64,
                                     q_m * PW + 3 + ch * 128:
                                     q_m * PW + 3 + (ch + 1) * 128]
                            gof = ci * 16 + hv
                            gc1 = lambda b_, o=gof: b_[:, o:o + 1]
                            op(T, lambda e, kd=kd: e.matmul(
                                ps[0][:, 0:128], kd, kd, start=True, stop=True),
                                reads=[qkvb], writes=[pb[0]])
                            op(T, lambda e, kd=kd, qd=qd: e.matmul(
                                ps[1][:, 0:128], kd, qd, start=True, stop=True),
                                reads=[qkvb], writes=[pb[1]])
                            op(T, lambda e, hv=hv: e.matmul(
                                ps[2][:, 0:128], csel[:, hv * 128:(hv + 1) * 128],
                                GT[:], start=True, stop=True),
                                reads=[GTb, cselb], writes=[pb[2]])
                            op(V, lambda e: e.tensor_add(arg1[:], ps[2][:, 0:128],
                                                         cmst[:]),
                               reads=[pb[2], cmstb], writes=[arg1b])
                            op(S, lambda e, gof=gof: e.activation(
                                prex[:], arg1[:], AF.Exp,
                                bias=glb[:, gof:gof + 1], scale=-1.0),
                                reads=[arg1b, glbb], writes=[prexb])
                            op(V, lambda e: e.tensor_mul(Anb_[:], prex[:],
                                                         ps[0][:, 0:128]),
                               reads=[prexb, pb[0]], writes=[Anbb])
                            op(V, lambda e: e.tensor_add(arg1[:], ps[2][:, 0:128],
                                                         cminc[:]),
                               reads=[pb[2], cmincb], writes=[arg1b])
                            op(S, lambda e, gof=gof: e.activation(
                                wpre[:], arg1[:], AF.Exp,
                                bias=negG[:, gof:gof + 1], scale=1.0),
                                reads=[arg1b, negGb], writes=[wpreb])
                            op(V, lambda e: e.tensor_mul(Wtb_[:], wpre[:],
                                                         ps[1][:, 0:128]),
                               reads=[wpreb, pb[1]], writes=[Wtbb])
                            op(T, lambda e: e.transpose(self.pt8[:, 0:128],
                                                        Anb_[:], self.identb[:]),
                               reads=[Anbb, self.identb_b], writes=[self.pt8b])
                            op(V, lambda e: e.tensor_copy(Atb_[:],
                                                          self.pt8[:, 0:128]),
                               reads=[self.pt8b], writes=[Atbb])
                            op(V, lambda e, ch=ch, hk=hk: e.tensor_scalar_mul(
                                kbar[:], ktok[:, ch * 512 + hk * 64:
                                              ch * 512 + (hk + 1) * 64],
                                wpre[:, 127:128]),
                               reads=[ktokb, wpreb], writes=[kbarb])
                            op(V, lambda e, ch=ch, hv=hv, gof=gof:
                               e.tensor_scalar_mul(
                                rhs[:, 0:64], vtok[:, ch * 1024 + hv * 64:
                                                   ch * 1024 + (hv + 1) * 64],
                                beta[:, gof:gof + 1]),
                               reads=[vtokb, betab], writes=[rhsb])
                            op(V, lambda e, ch=ch, hk=hk, gof=gof:
                               e.tensor_scalar_mul(
                                rhs[:, 64:128], ktok[:, ch * 512 + hk * 64:
                                                     ch * 512 + (hk + 1) * 64],
                                eglb[:, gof:gof + 1]),
                               reads=[ktokb, eglbb], writes=[rhsb])
                            op(V, lambda e: e.tensor_copy(X[:], rhs[:]),
                               reads=[rhsb], writes=[Xb])
                            cn, ct, cnt_, ctt_ = Anbb, Atbb, Anb_, Atb_
                            for r, pnm in enumerate(("t2", "t4", "t8", "t16",
                                                     "t32", "t64")):
                                op(T, lambda e, cnt_=cnt_, ctt_=ctt_: e.matmul(
                                    ps[4][:, 0:128], cnt_[:], ctt_[:],
                                    start=True, stop=True),
                                    reads=[cn, ct], writes=[pb[4]])
                                op(V, lambda e, pnm=pnm: e.tensor_copy(
                                    pw[pnm][:], ps[4][:, 0:128]),
                                   reads=[pb[4]], writes=[pwb[pnm]])
                                if pnm != "t64":
                                    nb = "n_a" if r % 2 == 0 else "n_b"
                                    op(T, lambda e, cnt_=cnt_, ctt_=ctt_: e.matmul(
                                        ps[5][:, 0:128], ctt_[:], cnt_[:],
                                        start=True, stop=True),
                                        reads=[cn, ct], writes=[pb[5]])
                                    op(V, lambda e, nb=nb: e.tensor_copy(
                                        pw[nb][:], ps[5][:, 0:128]),
                                       reads=[pb[5]], writes=[pwb[nb]])
                                    cn, ct = pwb[nb], pwb[pnm]
                                    cnt_, ctt_ = pw[nb], pw[pnm]
                            for pnm in ("t64", "t32", "t16", "t8", "t4", "t2"):
                                op(T, lambda e, pnm=pnm: e.matmul(
                                    ps[4][:, 0:128], pw[pnm][:], X[:],
                                    start=True, stop=True),
                                    reads=[pwb[pnm], Xb], writes=[pb[4]])
                                op(V, lambda e: e.tensor_add(X[:], X[:],
                                                             ps[4][:, 0:128]),
                                   reads=[pb[4], Xb], writes=[Xb])
                            op(T, lambda e: e.matmul(ps[4][:, 0:128], Atb_[:],
                                                     X[:], start=True, stop=True),
                               reads=[Atbb, Xb], writes=[pb[4]])
                            op(V, lambda e: e.tensor_sub(X[:], X[:],
                                                         ps[4][:, 0:128]),
                               reads=[pb[4], Xb], writes=[Xb])
                            po = 64 * ((hv // 2) % 2)
                            ii = ch * 16 + hv
                            ip = ch * 8 + (hv // 4) * 2 + hv % 2
                            op(T, lambda e: e.matmul(ps[4][:, 0:64], Wtb_[:],
                                                     X[:, 0:64],
                                                     start=True, stop=True),
                               reads=[Wtbb, Xb], writes=[pb[4]])
                            op(V, lambda e, ii=ii: e.tensor_copy(
                                dO0[:, ii * 64:(ii + 1) * 64], ps[4][:, 0:64]),
                               reads=[pb[4]], writes=[dO0b])
                            op(T, lambda e, po=po: e.matmul(
                                ps[4][po:po + 64, 0:128], X[:, 64:128], Wtb_[:],
                                start=True, stop=True),
                               reads=[Wtbb, Xb], writes=[pb[4]])
                            op(V, lambda e, po=po, ip=ip: e.tensor_scalar_mul(
                                dWY[po:po + 64, ip * 128:(ip + 1) * 128],
                                ps[4][po:po + 64, 0:128], -1.0),
                               reads=[pb[4]], writes=[dWYb])
                            op(T, lambda e, hv=hv, po=po: e.matmul(
                                ps[4][po:po + 64, 0:128],
                                csel[:, hv * 128:hv * 128 + 64], eGT[:],
                                start=True, stop=True),
                                reads=[eGTb, cselb], writes=[pb[4]])
                            op(V, lambda e, po=po, ip=ip, qd=qd: e.tensor_mul(
                                dQg[po:po + 64, ip * 128:(ip + 1) * 128], qd,
                                ps[4][po:po + 64, 0:128]),
                               reads=[pb[4], qkvb], writes=[dQgb])
                            op(T, lambda e, po=po: e.matmul(
                                ps[4][po:po + 64, 0:64], kbar[:], X[:, 64:128],
                                start=True, stop=True),
                               reads=[kbarb, Xb], writes=[pb[4]])
                            op(V, lambda e, po=po, ip=ip: e.tensor_scalar_mul(
                                dMn[po:po + 64, ip * 64:(ip + 1) * 64],
                                ps[4][po:po + 64, 0:64], -1.0),
                               reads=[pb[4]], writes=[dMnb])
                            op(T, lambda e, po=po: e.matmul(
                                ps[4][po:po + 64, 0:64], kbar[:], X[:, 0:64],
                                start=True, stop=True),
                               reads=[kbarb, Xb], writes=[pb[4]])
                            op(V, lambda e, po=po, ip=ip: e.tensor_copy(
                                dIc[po:po + 64, ip * 64:(ip + 1) * 64],
                                ps[4][po:po + 64, 0:64]),
                               reads=[pb[4]], writes=[dIcb])
                    for n in ("O0", "WY", "Qg", "Mn", "Ic"):
                        dloc, dlb, w = dlocal[n]
                        self.dma(SY, lambda e, n=n, sc=sc, dloc=dloc, w=w:
                                 e.dma_start(out=spl[n][:, sc * w:(sc + 1) * w],
                                             in_=dloc[:, 0:w]),
                                 splb[n], reads=[dlb])
                    for ch in range(nch):
                        ci = c0 + ch
                        for hv in range(16):
                            po = 64 * ((hv // 2) % 2)
                            ip = ch * 8 + (hv // 4) * 2 + hv % 2
                            gof = ci * 16 + hv
                            sof = ((hv // 4) * 2 + hv % 2) * 64
                            op(T, lambda e, po=po, ip=ip, sof=sof: e.matmul(
                                ps[5][po:po + 64, 0:64],
                                dMn[po:po + 64, ip * 64:(ip + 1) * 64],
                                Sst[po:po + 64, sof:sof + 64],
                                start=True, stop=False),
                                reads=[dMnb, Sstb], writes=[pb[5]])
                            op(T, lambda e, po=po, ip=ip: e.matmul(
                                ps[5][po:po + 64, 0:64],
                                self.identb[po:po + 64, po:po + 64],
                                dIc[po:po + 64, ip * 64:(ip + 1) * 64],
                                start=False, stop=True),
                                reads=[dIcb, self.identb_b], writes=[pb[5]])
                            op(V, lambda e, po=po, gof=gof, sof=sof:
                               e.scalar_tensor_tensor(
                                Sst[po:po + 64, sof:sof + 64],
                                Sst[po:po + 64, sof:sof + 64],
                                egc[po:po + 64, gof:gof + 1],
                                ps[5][po:po + 64, 0:64], op0=OP.mult, op1=OP.add),
                               reads=[pb[5], egcb, Sstb], writes=[Sstb])
                self.barrier()

            recvS, recvSb = sb("recvS", [128, 8 * 64], BF16)
            if use_cc:
                cin = nc.dram_tensor(tag + "ccin", [128 * 512], BF16,
                                     kind="Internal").ap()
                cout = nc.dram_tensor(tag + "ccout", [2 * 128 * 512], BF16,
                                      kind="Internal").ap()
                cinb, coutb = Buf("ccin"), Buf("ccout")
                self.dma(SY, lambda e: e.dma_start(
                    out=cin.rearrange("(p f) -> p f", p=128),
                    in_=Sst[:, 0:512]), cinb, reads=[Sstb])
                op(G, lambda e: e.collective_compute(
                    "AllGather", OP.bypass, ins=[cin[:]], outs=[cout[:]],
                    replica_groups=[[2 * i, 2 * i + 1]
                                    for i in range(self.n_cores // 2)]),
                   reads=[cinb], writes=[coutb])
                self.dma(SY, lambda e: e.dma_start(
                    out=recvS[:, 0:512], in_=cout[0:128 * 512].rearrange(
                        "(p f) -> p f", p=128)), recvSb, reads=[coutb])
                op(V, lambda e: e.tensor_scalar_mul(
                    recvS[:, 0:512], recvS[:, 0:512], cpar[:, 0:1]),
                   reads=[recvSb, cparb], writes=[recvSb])
            else:
                op(V, lambda e: e.memset(recvS[:], 0.0), writes=[recvSb])
            op(V, lambda e: e.tensor_copy(Sst[:, 0:512], recvS[:, 0:512]),
               reads=[recvSb], writes=[Sstb])

            self.dma(GD, lambda e: e.dma_start(
                out=wslot[:, 0:8 * 1024], in_=self.aa_w["aa_wz"][:]), wslotb)
            self.dma(GD, lambda e: e.dma_start(
                out=wslot[:, 8 * 1024:16 * 1024], in_=self.aa_w["aa_wout"][:]),
                wslotb)
            nrmw, nrmwb = sb("nrmw", [128, 1024])
            self.dma(SY, lambda e: e.dma_start(out=nrmw[:],
                                               in_=self.aa_w["aa_normw"][:]), nrmwb)
            with contextlib.ExitStack() as pbx:
                o_t, o_tb = sb("o_t", [128, 1024], st=pbx)
                zb_, zbb = sb("zsil", [128, 1024], st=pbx)
                on, onb = sb("on", [128, 1024], BF16, st=pbx)
                rmsb, rmsbb = sb("rms", [128, 16], st=pbx)
                oT, oTb = sb("oT", [128, 8 * 128], BF16, st=pbx)
                for sc in range(NSC):
                    c0 = sc * SCW
                    nch = min(SCW, NT - c0)
                    for n in ("O0", "WY", "Qg", "Mn", "Ic"):
                        dloc, dlb, w = dlocal[n]
                        self.dma(SY, lambda e, n=n, sc=sc, dloc=dloc, w=w:
                                 e.dma_start(out=dloc[:, 0:w],
                                             in_=spl[n][:, sc * w:(sc + 1) * w]),
                                 dlb, reads=[splb[n]])
                    for ch in range(nch):
                        ci = c0 + ch
                        pcur, pnxt = ci % 2, 1 - (ci % 2)
                        for half in range(2):
                            pj = half
                            for hh in range(8):
                                hv = half * 8 + hh
                                po = 64 * ((hv // 2) % 2)
                                ip = ch * 8 + (hv // 4) * 2 + hv % 2
                                ii = ch * 16 + hv
                                sof = pcur * 512 + ((hv // 4) * 2 + hv % 2) * 64
                                op(T, lambda e, po=po, ip=ip, hh=hh, pj=pj,
                                   sof=sof: e.matmul(
                                    ps[pj][:, hh * 64:(hh + 1) * 64],
                                    dQg[po:po + 64, ip * 128:(ip + 1) * 128],
                                    Sst[po:po + 64, sof:sof + 64],
                                    start=True, stop=False),
                                   reads=[dQgb, Sstb], writes=[pb[pj]])
                                op(T, lambda e, po=po, ip=ip, hh=hh, pj=pj,
                                   sof=sof: e.matmul(
                                    ps[pj][:, hh * 64:(hh + 1) * 64],
                                    dWY[po:po + 64, ip * 128:(ip + 1) * 128],
                                    Sst[po:po + 64, sof:sof + 64],
                                    start=False, stop=True),
                                   reads=[dWYb, Sstb], writes=[pb[pj]])
                            op(V, lambda e, half=half, ch=ch, pj=pj: e.tensor_add(
                                o_t[:, half * 512:(half + 1) * 512],
                                ps[pj][:, 0:512],
                                dO0[:, (ch * 16 + half * 8) * 64:
                                    (ch * 16 + half * 8 + 8) * 64]),
                               reads=[pb[pj], dO0b], writes=[o_tb])
                        for hv in range(16):
                            po = 64 * ((hv // 2) % 2)
                            ip = ch * 8 + (hv // 4) * 2 + hv % 2
                            gof = ci * 16 + hv
                            scur = pcur * 512 + ((hv // 4) * 2 + hv % 2) * 64
                            snxt = pnxt * 512 + ((hv // 4) * 2 + hv % 2) * 64
                            op(T, lambda e, po=po, ip=ip, scur=scur: e.matmul(
                                ps[5][po:po + 64, 0:64],
                                dMn[po:po + 64, ip * 64:(ip + 1) * 64],
                                Sst[po:po + 64, scur:scur + 64],
                                start=True, stop=False),
                                reads=[dMnb, Sstb], writes=[pb[5]])
                            op(T, lambda e, po=po, ip=ip: e.matmul(
                                ps[5][po:po + 64, 0:64],
                                self.identb[po:po + 64, po:po + 64],
                                dIc[po:po + 64, ip * 64:(ip + 1) * 64],
                                start=False, stop=True),
                                reads=[dIcb, self.identb_b], writes=[pb[5]])
                            op(V, lambda e, po=po, gof=gof, scur=scur, snxt=snxt:
                               e.scalar_tensor_tensor(
                                Sst[po:po + 64, snxt:snxt + 64],
                                Sst[po:po + 64, scur:scur + 64],
                                egc[po:po + 64, gof:gof + 1],
                                ps[5][po:po + 64, 0:64], op0=OP.mult, op1=OP.add),
                               reads=[pb[5], egcb, Sstb], writes=[Sstb])
                        tl = ci + 1
                        op(S, lambda e: e.activation(on[:], o_t[:], AF.Square),
                           reads=[o_tb], writes=[onb])
                        for hv in range(16):
                            op(V, lambda e, hv=hv: e.tensor_reduce(
                                rmsb[:, hv:hv + 1], on[:, hv * 64:(hv + 1) * 64],
                                AX, OP.add),
                               reads=[onb], writes=[rmsbb])
                        op(V, lambda e: e.tensor_scalar(
                            rmsb[:], rmsb[:], 1.0 / 64.0, 1e-6,
                            op0=OP.mult, op1=OP.add),
                           reads=[rmsbb], writes=[rmsbb])
                        op(V, lambda e: e.reciprocal(rmsb[:], rmsb[:]),
                           reads=[rmsbb], writes=[rmsbb])
                        op(S, lambda e: e.activation(rmsb[:], rmsb[:], AF.Sqrt),
                           reads=[rmsbb], writes=[rmsbb])
                        for dc in range(2):
                            for k in range(8):
                                op(T, lambda e, k=k, dc=dc, tl=tl: e.matmul(
                                    ps[2 + dc][:, 0:512], self.lnT_ap(k, tl),
                                    wslot[:, k * 1024 + dc * 512:
                                          k * 1024 + (dc + 1) * 512],
                                    start=(k == 0), stop=(k == 7)),
                                    reads=[self.lnb, wslotb], writes=[pb[2 + dc]])
                            op(S, lambda e, dc=dc: e.activation(
                                zb_[:, dc * 512:(dc + 1) * 512],
                                ps[2 + dc][:, 0:512], AF.Silu),
                               reads=[pb[2 + dc]], writes=[zbb])
                        op(V, lambda e: e.tensor_mul(zb_[:], zb_[:], nrmw[:]),
                           reads=[zbb, nrmwb], writes=[zbb])
                        for hv in range(16):
                            op(V, lambda e, hv=hv: e.tensor_scalar_mul(
                                o_t[:, hv * 64:(hv + 1) * 64],
                                o_t[:, hv * 64:(hv + 1) * 64],
                                rmsb[:, hv:hv + 1]),
                               reads=[rmsbb, o_tb], writes=[o_tb])
                        op(V, lambda e: e.tensor_mul(on[:], o_t[:], zb_[:]),
                           reads=[o_tb, zbb], writes=[onb])
                        for half in range(2):
                            for kk in range(4):
                                k = half * 4 + kk
                                op(T, lambda e, k=k, kk=kk, half=half: e.transpose(
                                    self.pt8[:, half * 512 + kk * 128:
                                             half * 512 + (kk + 1) * 128],
                                    on[:, k * 128:(k + 1) * 128], self.identb[:]),
                                    reads=[onb, self.identb_b],
                                    writes=[self.pt8b])
                            for kk in range(4):
                                k = half * 4 + kk
                                op(S, lambda e, k=k, kk=kk, half=half: e.copy(
                                    oT[:, k * 128:(k + 1) * 128],
                                    self.pt8[:, half * 512 + kk * 128:
                                             half * 512 + (kk + 1) * 128]),
                                   reads=[self.pt8b], writes=[oTb])
                        for m in range(8):
                            for k in range(8):
                                op(T, lambda e, m=m, k=k: e.matmul(
                                    ps[6][:, 0:128],
                                    wslot[:, 8192 + k * 1024 + m * 128:
                                          8192 + k * 1024 + (m + 1) * 128],
                                    oT[:, k * 128:(k + 1) * 128],
                                    start=(k == 0), stop=(k == 7)),
                                    reads=[oTb, wslotb], writes=[pb[6]])
                            op(V, lambda e, m=m: e.tensor_copy(
                                on[:, m * 128:(m + 1) * 128], ps[6][:, 0:128]),
                               reads=[pb[6]], writes=[onb])
                        for half in range(2):
                            for kk in range(4):
                                m = half * 4 + kk
                                op(T, lambda e, m=m, kk=kk, half=half: e.transpose(
                                    self.pt8[:, half * 512 + kk * 128:
                                             half * 512 + (kk + 1) * 128],
                                    on[:, m * 128:(m + 1) * 128], self.identb[:]),
                                    reads=[onb, self.identb_b],
                                    writes=[self.pt8b])
                            op(V, lambda e, half=half, tl=tl: e.tensor_add(
                                self.x_ap(tl, half * 512, (half + 1) * 512),
                                self.x_ap(tl, half * 512, (half + 1) * 512),
                                self.pt8[:, half * 512:half * 512 + 512]),
                               reads=[self.pt8b], writes=[self.xb])
        self.barrier()

    def prep_attn_b_weights(self, p, base):
        g = lambda n: np.asarray(p[base + n], np.float32)
        self.inputs["ab_mux"] = np.ascontiguousarray(g("mu_x").reshape(8, 128).T)
        A = g("lora_A_w")
        self.inputs["ab_lA"] = tile_w(np.concatenate([A[i] for i in range(6)], 1))
        Ab = g("lora_A_b")
        lab = np.zeros((64, 3), np.float32)
        for j in range(3):
            lab[0:32, j] = Ab[2 * j]
            lab[32:64, j] = Ab[2 * j + 1]
        self.inputs["ab_lAb"] = lab
        B = g("lora_B_w")
        lb = np.zeros((64, 3 * 1024), np.float32)
        for j in range(3):
            lb[0:32, j * 1024:(j + 1) * 1024] = B[2 * j]
            lb[32:64, j * 1024:(j + 1) * 1024] = B[2 * j + 1]
        self.inputs["ab_lB"] = lb
        lam = np.zeros((128, 6 * 8), np.float32)
        for gi in range(6):
            lam[:, gi * 8:(gi + 1) * 8] += g("lora_lambda")[gi].reshape(8, 128).T
            lam[:, gi * 8:(gi + 1) * 8] += 0
        self.inputs["ab_lam"] = lam
        lbb = np.zeros((128, 6 * 8), np.float32)
        for gi in range(6):
            lbb[:, gi * 8:(gi + 1) * 8] = g("lora_B_b")[gi].reshape(8, 128).T
        self.inputs["ab_lbb"] = lbb
        mh = g("mhd_w")
        mimg = np.zeros((128, 5 * 16 * 64), np.float32)
        for sl in range(5):
            for h in range(16):
                blk = mh[sl, h]
                mimg[0:64, (sl * 16 + h) * 64:(sl * 16 + h + 1) * 64] = blk
                mimg[64:128, (sl * 16 + h) * 64:(sl * 16 + h + 1) * 64] = blk
        self.inputs["ab_mhd"] = mimg
        for nm, w in (("ab_wA", g("wkv_A")), ("ab_wB", g("wkv_B"))):
            img = np.zeros((128, 16 * 64), np.float32)
            for h in range(16):
                img[0:64, h * 64:(h + 1) * 64] = w[h]
                img[64:128, h * 64:(h + 1) * 64] = w[h]
            self.inputs[nm] = img
        lamw = np.zeros((128, 8), np.float32)
        for j in range(8):
            lamw[0:64, j] = g("wkv_lamb")[2 * j]
            lamw[64:128, j] = g("wkv_lamb")[2 * j + 1]
        self.inputs["ab_lamb"] = lamw
        self.inputs["ab_gatew"] = tile_w(g("gate_w"))
        self.inputs["ab_outw"] = tile_w(g("out_w"))
        self.inputs["ab_outb"] = np.ascontiguousarray(g("out_b").reshape(8, 128).T)
        self.inputs["ab_gng"] = np.tile(g("gn_gamma"), (128, 1))
        self.inputs["ab_gnb"] = np.tile(g("gn_beta"), (128, 1))
        msk = np.zeros((128, 4 * 64 * 0 + 4 * 64), np.float32)
        full = np.zeros((64, 4 * 64), np.float32)
        jj, tt = np.meshgrid(np.arange(64), np.arange(64), indexing="ij")
        lv0 = ((jj // 8 == tt // 8) & (jj < tt)).astype(np.float32)
        lv1 = ((jj // 16 == tt // 16) & (jj % 16 < 8) & (tt % 16 >= 8)).astype(np.float32)
        lv2 = ((jj // 32 == tt // 32) & (jj % 32 < 16) & (tt % 32 >= 16)).astype(np.float32)
        lv3 = ((jj < 32) & (tt >= 32)).astype(np.float32)
        for i, lv in enumerate((lv0, lv1, lv2, lv3)):
            full[:, i * 64:(i + 1) * 64 * 0 + (i + 1) * 64] = lv
        self.inputs["ab_msk"] = np.ascontiguousarray(
            np.vstack([full, full]).astype(BF))

    def declare_attn_b_weights(self):
        self.ab_w = {}
        for n, sh in (("ab_mux", [128, 8]), ("ab_lA", [128, 8 * 192]),
                      ("ab_lAb", [64, 3]), ("ab_lB", [64, 3 * 1024]),
                      ("ab_lam", [128, 48]), ("ab_lbb", [128, 48]),
                      ("ab_mhd", [128, 5 * 16 * 64]), ("ab_wA", [128, 16 * 64]),
                      ("ab_wB", [128, 16 * 64]), ("ab_lamb", [128, 8]),
                      ("ab_gatew", [128, 8 * 1024]), ("ab_outw", [128, 8 * 1024]),
                      ("ab_outb", [128, 8]), ("ab_gng", [128, 1024]),
                      ("ab_gnb", [128, 1024])):
            self.ab_w[n] = self.din(n, sh, F32)
        self.ab_w["ab_msk"] = self.din("ab_msk", [128, 256], BF16)

    def emit_attn_b(self, tag, use_cc):
        T, V, S, G, SY, GD = self.T, self.V, self.S, self.G, self.SY, self.GD
        NT, NTE, nc = self.NT, self.NTE, self.nc
        ps, pb = self.psum, self.psb
        op = self.op
        with contextlib.ExitStack() as ph:
            self.pt8 = ph.enter_context(
                nc.psum_tensor(tag + "pt8", [128, 1024], BF16))
            self.pt8b = Buf(tag + "pt8")
            pt8bs = [Buf(f"{tag}pt8_{i}") for i in range(8)]
            _ptc = [0]

            def nxt():
                i = _ptc[0] % 8
                _ptc[0] += 1
                return i * 128, self.pt8b
            sb = lambda n, sh, dt=F32, st=None: self.sbuf(tag + n, sh, dt,
                                                          stack=st or ph)
            shp = {"ab_mux": ([128, 8], 0), "ab_lA": ([128, 8 * 192], 1),
                   "ab_lAb": ([64, 3], 0), "ab_lB": ([64, 3 * 1024], 1),
                   "ab_lam": ([128, 48], 0), "ab_lbb": ([128, 48], 0),
                   "ab_mhd": ([128, 5 * 16 * 64], 1), "ab_wA": ([128, 16 * 64], 1),
                   "ab_wB": ([128, 16 * 64], 1), "ab_lamb": ([128, 8], 0),
                   "ab_outb": ([128, 8], 0), "ab_msk": ([128, 256], 2)}
            W, Wb = {}, {}
            for n, (sh, cast) in shp.items():
                dt = BF16 if cast else F32
                W[n], Wb[n] = sb(n[3:], sh, dt)
                eng = GD if cast == 1 else SY
                self.dma(eng, lambda e, n=n, t_=W[n]: e.dma_start(
                    out=t_[:], in_=self.ab_w[n][:]), Wb[n])
            wslot, wslotb = sb("wslot", [128, 8 * 1024], BF16)
            self.dma(GD, lambda e: e.dma_start(out=wslot[:],
                                               in_=self.ab_w["ab_gatew"][:]), wslotb)
            onesd, onesdb = self.input_sbuf(
                "c_onesd", np.ones((128, 128), np.float32).astype(BF), stack=ph)
            lamc, lamcb = sb("lamc", [128, 48])
            op(V, lambda e: e.tensor_add(lamc[:], W["ab_lam"][:], W["ab_lbb"][:]),
               reads=[Wb["ab_lam"], Wb["ab_lbb"]], writes=[lamcb])

            if use_cc:
                bin_ = nc.dram_tensor(tag + "bcin", [128 * 8], BF16,
                                      kind="Internal").ap()
                bout = nc.dram_tensor(tag + "bcout", [2 * 128 * 8], BF16,
                                      kind="Internal").ap()
                binb, boutb = Buf("bcin"), Buf("bcout")
                lastc = 128 + NT * 128 - 1
                self.dma(SY, lambda e, lastc=lastc: e.dma_start(
                    out=bin_.rearrange("(p f) -> p f", p=128),
                    in_=self.lnT[:, lastc::NTE * 128],
                    allow_slow_non_contiguous=True), binb, reads=[self.lnb])
                op(G, lambda e: e.collective_compute(
                    "AllGather", OP.bypass, ins=[bin_[:]], outs=[bout[:]],
                    replica_groups=[[2 * i, 2 * i + 1]
                                    for i in range(self.n_cores // 2)]),
                   reads=[binb], writes=[boutb])
                bcol, bcolb = sb("bcol", [128, 8], BF16)
                self.dma(SY, lambda e: e.dma_start(
                    out=bcol[:], in_=bout[0:128 * 8].rearrange(
                        "(p f) -> p f", p=128)), bcolb, reads=[boutb])
                op(V, lambda e: e.tensor_scalar_mul(
                    bcol[:], bcol[:], self.cpar[:, 0:1]),
                   reads=[bcolb, self.cparb], writes=[bcolb])
                op(V, lambda e: e.tensor_copy(self.lnT[:, 127::NTE * 128],
                                              bcol[:]),
                   reads=[bcolb], writes=[self.lnb])
            else:
                op(V, lambda e: e.memset(self.lnT[:, 127::NTE * 128], 0.0),
                   writes=[self.lnb])

            Sst, Sstb = sb("Sst", [128, 2 * 512], BF16)
            op(V, lambda e: e.memset(Sst[:], 0.0), writes=[Sstb])
            spn = ("op", "rd", "ic", "ew", "ep", "gs")
            spl, splb = {}, {}
            for n in spn:
                spl[n] = nc.dram_tensor(tag + "sp" + n, [128, NT * 1024], BF16,
                                        kind="Internal").ap()
                splb[n] = Buf(tag + n)

            with contextlib.ExitStack() as pa:
                diff, diffb = sb("diff", [128, 1024], BF16, st=pa)
                zc, zcb = sb("zc", [128, 1024], BF16, st=pa)
                hdn, hdnb = sb("hdn", [64, 3 * 128], BF16, st=pa)
                dl, dlb = sb("dl", [128, 1024], BF16, st=pa)
                dd5, dd5b = sb("dd5", [128, 1024], BF16, st=pa)
                kv5, kv5b = {}, {}
                for n in ("kk", "vv", "rr", "ww", "uu"):
                    kv5[n], kv5b[n] = sb(n, [128, 1024], BF16, st=pa)
                wh, whb = sb("wh", [128, 1024], BF16, st=pa)
                lnw, lnwb = sb("lnw", [128, 1024], BF16, st=pa)
                bcum, bcumb = sb("bcum", [128, 1024], st=pa)
                ktk, ktkb = sb("ktk", [64, 16 * 128], BF16, st=pa)
                vtk, vtkb = sb("vtk", [64, 16 * 128], BF16, st=pa)
                utk, utkb = sb("utk", [64, 16 * 128], BF16, st=pa)
                lnwT, lnwTb = sb("lnwT", [64, 16 * 128], BF16, st=pa)
                tmpd, tmpdb = sb("tmpd", [64, 16 * 128], BF16, st=pa)
                khd, khdb = sb("khd", [128, 64], BF16, st=pa)
                khT, khTb = sb("khT", [64, 128], BF16, st=pa)
                sca, scab = sb("sca", [128, 64], st=pa)
                scl, sclb = sb("scl", [128, 4 * 64], st=pa)
                ptl, ptlb = sb("ptl", [64, 2 * 4 * 64], BF16, st=pa)
                opar, oparb = sb("opar", [128, 1024], st=pa)
                vrt, vrtb = sb("vrt", [64, 32], st=pa)
                icb_, icbb = sb("icb", [128, 2 * 512], BF16, st=pa)
                ewb_, ewbb = sb("ewb", [128, 2 * 512], BF16, st=pa)
                eprev, eprevb = sb("eprev", [128, 1024], BF16, st=pa)

                for ti in range(NT):
                    ec0 = 128 + ti * 128
                    for k in range(8):
                        op(V, lambda e, k=k, ec0=ec0: e.tensor_sub(
                            diff[:, k * 128:(k + 1) * 128],
                            self.lnT_cols(k, ec0 - 1, ec0 + 127),
                            self.lnT_cols(k, ec0, ec0 + 128)),
                           reads=[self.lnb], writes=[diffb])
                        op(V, lambda e, k=k, ec0=ec0: e.scalar_tensor_tensor(
                            zc[:, k * 128:(k + 1) * 128],
                            diff[:, k * 128:(k + 1) * 128],
                            W["ab_mux"][:, k:k + 1],
                            self.lnT_cols(k, ec0, ec0 + 128),
                            op0=OP.mult, op1=OP.add),
                           reads=[diffb, Wb["ab_mux"], self.lnb], writes=[zcb])
                        op(V, lambda e, k=k: e.tensor_mul(
                            zc[:, k * 128:(k + 1) * 128],
                            zc[:, k * 128:(k + 1) * 128],
                            diff[:, k * 128:(k + 1) * 128]),
                           reads=[diffb, zcb], writes=[zcb])
                        op(V, lambda e, k=k, ec0=ec0: e.tensor_add(
                            zc[:, k * 128:(k + 1) * 128],
                            zc[:, k * 128:(k + 1) * 128],
                            self.lnT_cols(k, ec0, ec0 + 128)),
                           reads=[zcb, self.lnb], writes=[zcb])
                    for p3 in range(3):
                        bA = (0, 2)[p3 % 2]
                        for k in range(8):
                            op(T, lambda e, p3=p3, k=k, bA=bA: e.matmul(
                                ps[bA][:64, 0:128],
                                W["ab_lA"][:, k * 192 + p3 * 64:
                                           k * 192 + (p3 + 1) * 64],
                                zc[:, k * 128:(k + 1) * 128],
                                start=(k == 0), stop=(k == 7)),
                                reads=[Wb["ab_lA"], zcb], writes=[pb[bA]])
                        op(S, lambda e, p3=p3, bA=bA: e.activation(
                            hdn[:, p3 * 128:(p3 + 1) * 128], ps[bA][:64, 0:128],
                            AF.Tanh, bias=W["ab_lAb"][:, p3:p3 + 1]),
                           reads=[pb[bA], Wb["ab_lAb"]], writes=[hdnb])
                    for g in range(6):
                        p3, gh = g // 2, g % 2
                        for k in range(8):
                            bB = (0, 2)[k % 2]
                            op(T, lambda e, p3=p3, gh=gh, k=k, bB=bB: e.matmul(
                                ps[bB][:, 0:128],
                                W["ab_lB"][gh * 32:gh * 32 + 32,
                                           p3 * 1024 + k * 128:
                                           p3 * 1024 + (k + 1) * 128],
                                hdn[gh * 32:gh * 32 + 32,
                                    p3 * 128:(p3 + 1) * 128],
                                start=True, stop=True),
                                reads=[Wb["ab_lB"], hdnb], writes=[pb[bB]])
                            op(V, lambda e, g=g, k=k, bB=bB: e.scalar_tensor_tensor(
                                dl[:, k * 128:(k + 1) * 128], ps[bB][:, 0:128],
                                lamc[:, g * 8 + k:g * 8 + k + 1],
                                diff[:, k * 128:(k + 1) * 128],
                                op0=OP.add, op1=OP.mult),
                               reads=[pb[bB], lamcb, diffb], writes=[dlb])
                        if g == 5:
                            for k in range(8):
                                op(V, lambda e, k=k, ec0=ec0: e.tensor_add(
                                    dd5[:, k * 128:(k + 1) * 128],
                                    dl[:, k * 128:(k + 1) * 128],
                                    self.lnT_cols(k, ec0, ec0 + 128)),
                                   reads=[dlb, self.lnb], writes=[dd5b])
                            continue
                        dstn = ("kk", "vv", "rr", "ww", "uu")[g]
                        dst, dstb = kv5[dstn], kv5b[dstn]
                        for pr in range(8):
                            bC = (1, 3)[pr % 2]
                            for hh in range(2):
                                h = pr * 2 + hh
                                po = 64 * hh
                                kt = h // 2
                                mof = (g * 16 + h) * 64
                                op(T, lambda e, po=po, kt=kt, mof=mof, ec0=ec0,
                                   bC=bC: e.matmul(
                                    ps[bC][po:po + 64, 0:128],
                                    W["ab_mhd"][po:po + 64, mof:mof + 64],
                                    self.lnT[po:po + 64,
                                             kt * NTE * 128 + ec0:
                                             kt * NTE * 128 + ec0 + 128],
                                    start=True, stop=False),
                                   reads=[Wb["ab_mhd"], self.lnb], writes=[pb[bC]])
                                op(T, lambda e, po=po, kt=kt, mof=mof, bC=bC:
                                   e.matmul(
                                    ps[bC][po:po + 64, 0:128],
                                    W["ab_mhd"][po:po + 64, mof:mof + 64],
                                    dl[po:po + 64, kt * 128:kt * 128 + 128],
                                    start=False, stop=True),
                                   reads=[Wb["ab_mhd"], dlb], writes=[pb[bC]])
                            op(V, lambda e, pr=pr, dst=dst, bC=bC: e.tensor_copy(
                                dst[:, pr * 128:(pr + 1) * 128], ps[bC][:, 0:128]),
                               reads=[pb[bC]], writes=[dstb])
                    for pr in range(8):
                        bD = (0, 2)[pr % 2]
                        bE = (1, 3)[pr % 2]
                        for hh in range(2):
                            h = pr * 2 + hh
                            po = 64 * hh
                            op(T, lambda e, po=po, h=h, pr=pr, bD=bD: e.matmul(
                                ps[bD][po:po + 64, 0:128],
                                W["ab_wA"][po:po + 64, h * 64:(h + 1) * 64],
                                kv5["ww"][po:po + 64, pr * 128:(pr + 1) * 128],
                                start=True, stop=True),
                                reads=[Wb["ab_wA"], kv5b["ww"]], writes=[pb[bD]])
                        op(S, lambda e, pr=pr, bD=bD: e.activation(
                            wh[:, pr * 128:(pr + 1) * 128], ps[bD][:, 0:128],
                            AF.Tanh),
                           reads=[pb[bD]], writes=[whb])
                        for hh in range(2):
                            h = pr * 2 + hh
                            po = 64 * hh
                            op(T, lambda e, po=po, h=h, pr=pr, bE=bE: e.matmul(
                                ps[bE][po:po + 64, 0:128],
                                W["ab_wB"][po:po + 64, h * 64:(h + 1) * 64],
                                wh[po:po + 64, pr * 128:(pr + 1) * 128],
                                start=True, stop=True),
                                reads=[Wb["ab_wB"], whb], writes=[pb[bE]])
                        op(S, lambda e, pr=pr, bE=bE: e.activation(
                            lnw[:, pr * 128:(pr + 1) * 128], ps[bE][:, 0:128],
                            AF.Exp, bias=W["ab_lamb"][:, pr:pr + 1]),
                           reads=[pb[bE], Wb["ab_lamb"]], writes=[lnwb])
                        op(V, lambda e, pr=pr: e.tensor_scalar(
                            lnw[:, pr * 128:(pr + 1) * 128],
                            lnw[:, pr * 128:(pr + 1) * 128], -1.0, -30.0,
                            op0=OP.mult, op1=OP.max),
                           reads=[lnwb], writes=[lnwb])
                        for c2 in range(2):
                            cof = pr * 128 + c2 * 64
                            op(V, lambda e, cof=cof: e.tensor_tensor_scan(
                                bcum[:, cof:cof + 64], lnw[:, cof:cof + 64],
                                lnw[:, cof:cof + 64], 0.0,
                                op0=OP.add, op1=OP.bypass),
                               reads=[lnwb], writes=[bcumb])
                    for pr in range(8):
                        for c2 in range(2):
                            dof = (c2 * 8 + pr) * 128
                            for src, srcb, dst, dstb2 in (
                                    (lnw, lnwb, lnwT, lnwTb),
                                    (kv5["kk"], kv5b["kk"], ktk, ktkb),
                                    (kv5["vv"], kv5b["vv"], vtk, vtkb),
                                    (kv5["uu"], kv5b["uu"], utk, utkb)):
                                pof, pob = nxt()
                                op(T, lambda e, pr=pr, c2=c2, src=src, pof=pof:
                                   e.transpose(
                                    self.pt8[:64, pof:pof + 128],
                                    src[:, pr * 128 + c2 * 64:
                                        pr * 128 + c2 * 64 + 64],
                                    self.identb[:]),
                                    reads=[srcb, self.identb_b],
                                    writes=[pob])
                                op(V, lambda e, dof=dof, dst=dst, pof=pof:
                                   e.tensor_copy(
                                    dst[:, dof:dof + 128],
                                    self.pt8[:64, pof:pof + 128]),
                                   reads=[pob], writes=[dstb2])
                    for pr in range(8):
                        for c2 in range(2):
                            dof = (c2 * 8 + pr) * 128
                            pof, pob = nxt()
                            op(T, lambda e, pr=pr, c2=c2, pof=pof: e.transpose(
                                self.pt8[:64, pof:pof + 128],
                                kv5["rr"][:, pr * 128 + c2 * 64:
                                          pr * 128 + c2 * 64 + 64],
                                self.identb[:]),
                                reads=[kv5b["rr"], self.identb_b],
                                writes=[pob])
                            op(V, lambda e, dof=dof, pof=pof: e.tensor_mul(
                                tmpd[:, dof:dof + 128],
                                self.pt8[:64, pof:pof + 128],
                                vtk[:, dof:dof + 128]),
                               reads=[pob, vtkb], writes=[tmpdb])
                    for h in range(16):
                        pr, hh = h // 2, h % 2
                        for c2 in range(2):
                            dof = (c2 * 8 + pr) * 128 + hh * 64
                            op(V, lambda e, dof=dof, h=h, c2=c2: e.tensor_reduce(
                                vrt[:64, h:h + 1] if c2 == 0 else
                                vrt[:64, h:h + 1],
                                tmpd[:, dof:dof + 64], AX, OP.add),
                               reads=[tmpdb], writes=[vrtb]) if False else None
                    for h in range(16):
                        pr, hh = h // 2, h % 2
                        for c2 in range(2):
                            dof = (c2 * 8 + pr) * 128 + hh * 64
                            op(V, lambda e, dof=dof, h=h, c2=c2: e.tensor_reduce(
                                vrt[:64, h * 2 + c2:h * 2 + c2 + 1],
                                tmpd[:, dof:dof + 64], AX, OP.add),
                               reads=[tmpdb], writes=[vrtb])
                    op(V, lambda e: e.tensor_mul(tmpd[:], utk[:], ktk[:]),
                       reads=[utkb, ktkb], writes=[tmpdb])
                    for h in range(16):
                        pr, hh = h // 2, h % 2
                        for c2 in range(2):
                            dof = (c2 * 8 + pr) * 128 + hh * 64
                            op(V, lambda e, dof=dof, h=h, c2=c2:
                               e.tensor_scalar_mul(
                                tmpd[:, dof:dof + 64], tmpd[:, dof:dof + 64],
                                vrt[:64, h * 2 + c2:h * 2 + c2 + 1]),
                               reads=[vrtb, tmpdb], writes=[tmpdb])
                    for pr in range(8):
                        for c2 in range(2):
                            dof = (c2 * 8 + pr) * 128
                            pof, pob = nxt()
                            op(T, lambda e, dof=dof, pof=pof: e.transpose(
                                self.pt8[:, pof:pof + 64],
                                tmpd[:64, dof:dof + 128],
                                self.identb[0:64, 0:64]),
                                reads=[tmpdb, self.identb_b], writes=[pob])
                            op(V, lambda e, pr=pr, c2=c2, pof=pof: e.tensor_copy(
                                opar[:, pr * 128 + c2 * 64:
                                     pr * 128 + c2 * 64 + 64],
                                self.pt8[:, pof:pof + 64]),
                               reads=[pob], writes=[oparb])
                    for c2 in range(2):
                        poc = 0
                        for pr in range(8):
                            cof = pr * 128 + c2 * 64
                            tof = (c2 * 8 + pr) * 128
                            for hh in range(2):
                                op(T, lambda e, hh=hh, cof=cof: e.matmul(
                                    ps[2][0:64, hh * 64:hh * 64 + 64],
                                    kv5["vv"][64 * hh:64 * hh + 64, cof:cof + 64],
                                    kv5["rr"][64 * hh:64 * hh + 64, cof:cof + 64],
                                    start=True, stop=True),
                                   reads=[kv5b["vv"], kv5b["rr"]], writes=[pb[2]])
                            for hh in range(2):
                                op(V, lambda e, hh=hh: e.tensor_mul(
                                    ptl[0:64, hh * 256:(hh + 1) * 256]
                                    .rearrange("p (l f) -> p l f", l=4),
                                    ps[2][0:64, hh * 64:hh * 64 + 64]
                                    .unsqueeze(1).broadcast_to([64, 4, 64]),
                                    W["ab_msk"][0:64, :]
                                    .rearrange("p (l f) -> p l f", l=4)),
                                   reads=[pb[2], Wb["ab_msk"]], writes=[ptlb])
                            for lv in range(4):
                                sblk = 8 * (2 ** lv)
                                if lv == 0:
                                    op(V, lambda e, cof=cof: e.tensor_scalar_mul(
                                        sca[:, 0:8], bcum[:, cof:cof + 8], -1.0),
                                       reads=[bcumb], writes=[scab])
                                    op(V, lambda e, cof=cof: e.tensor_sub(
                                        sca[:, 8:64].rearrange(
                                            "p (b r) -> p b r", b=7, r=8),
                                        bcum[:, cof + 7:cof + 63:8]
                                        .unsqueeze(2).broadcast_to([128, 7, 8]),
                                        bcum[:, cof + 8:cof + 64].rearrange(
                                            "p (b r) -> p b r", b=7, r=8)),
                                       reads=[bcumb], writes=[scab])
                                else:
                                    nb = 64 // sblk
                                    half = sblk // 2
                                    op(V, lambda e, cof=cof, nb=nb, half=half,
                                       sblk=sblk: e.tensor_sub(
                                        sca[:, 0:64].rearrange(
                                            "p (b r) -> p b r", b=nb, r=sblk),
                                        bcum[:, cof + half - 1:
                                             cof + half - 1 + (nb - 1) * sblk + 1:
                                             sblk]
                                        .unsqueeze(2).broadcast_to([128, nb, sblk]),
                                        bcum[:, cof:cof + 64].rearrange(
                                            "p (b r) -> p b r", b=nb, r=sblk)),
                                       reads=[bcumb], writes=[scab])
                                op(V, lambda e: e.tensor_scalar_min(
                                    sca[:, 0:64], sca[:, 0:64], 60.0),
                                   reads=[scab], writes=[scab])
                                op(S, lambda e: e.activation(
                                    khd[:, 0:64], sca[:, 0:64], AF.Exp),
                                   reads=[scab], writes=[khdb])
                                op(V, lambda e, cof=cof: e.tensor_mul(
                                    khd[:, 0:64], khd[:, 0:64],
                                    kv5["kk"][:, cof:cof + 64]),
                                   reads=[khdb, kv5b["kk"]], writes=[khdb])
                                pof, pob = nxt()
                                op(T, lambda e, pof=pof: e.transpose(
                                    self.pt8[:64, pof:pof + 128], khd[:, 0:64],
                                    self.identb[:]),
                                    reads=[khdb, self.identb_b],
                                    writes=[pob])
                                op(V, lambda e, pof=pof: e.tensor_copy(
                                    khT[0:64, 0:128],
                                    self.pt8[:64, pof:pof + 128]),
                                   reads=[pob], writes=[khTb])
                                for hh in range(2):
                                    op(T, lambda e, hh=hh, lv=lv: e.matmul(
                                        ps[3][hh * 64:hh * 64 + 64,
                                              lv * 64:lv * 64 + 64],
                                        khT[0:64, hh * 64:hh * 64 + 64],
                                        ptl[0:64,
                                            hh * 256 + lv * 64:
                                            hh * 256 + lv * 64 + 64],
                                        start=True, stop=True),
                                       reads=[khTb, ptlb], writes=[pb[3]])
                                op(V, lambda e, cof=cof: e.tensor_sub(
                                    sca[:, 0:64], bcum[:, cof:cof + 64],
                                    lnw[:, cof:cof + 64]),
                                   reads=[bcumb, lnwb], writes=[scab])
                                if lv == 0:
                                    op(V, lambda e, cof=cof: e.tensor_sub(
                                        sca[:, 8:64].rearrange(
                                            "p (b r) -> p b r", b=7, r=8),
                                        sca[:, 8:64].rearrange(
                                            "p (b r) -> p b r", b=7, r=8),
                                        bcum[:, cof + 7:cof + 63:8]
                                        .unsqueeze(2).broadcast_to([128, 7, 8])),
                                       reads=[bcumb, scab], writes=[scab])
                                else:
                                    nb = 64 // sblk
                                    half = sblk // 2
                                    op(V, lambda e, cof=cof, nb=nb, half=half,
                                       sblk=sblk: e.tensor_sub(
                                        sca[:, 0:64].rearrange(
                                            "p (b r) -> p b r", b=nb, r=sblk),
                                        sca[:, 0:64].rearrange(
                                            "p (b r) -> p b r", b=nb, r=sblk),
                                        bcum[:, cof + half - 1:
                                             cof + half - 1 + (nb - 1) * sblk + 1:
                                             sblk]
                                        .unsqueeze(2).broadcast_to([128, nb, sblk])),
                                       reads=[bcumb, scab], writes=[scab])
                                op(V, lambda e: e.tensor_scalar_min(
                                    sca[:, 0:64], sca[:, 0:64], 0.0),
                                   reads=[scab], writes=[scab])
                                op(S, lambda e, lv=lv: e.activation(
                                    scl[:, lv * 64:(lv + 1) * 64], sca[:, 0:64],
                                    AF.Exp),
                                   reads=[scab], writes=[sclb])
                            op(V, lambda e: e.tensor_mul(
                                scl[:].rearrange("p (l f) -> p l f", l=4),
                                scl[:].rearrange("p (l f) -> p l f", l=4),
                                ps[3][:, 0:256].rearrange(
                                    "p (l f) -> p l f", l=4)),
                               reads=[pb[3], sclb], writes=[sclb])
                            op(V, lambda e: e.tensor_add(
                                scl[:, 0:128], scl[:, 0:128], scl[:, 128:256]),
                               reads=[sclb], writes=[sclb])
                            op(V, lambda e, cof=cof: e.scalar_tensor_tensor(
                                opar[:, cof:cof + 64], scl[:, 0:64], 1.0,
                                scl[:, 64:128], op0=OP.mult, op1=OP.add),
                               reads=[sclb], writes=[sclb]) if False else None
                            op(V, lambda e: e.tensor_add(
                                scl[:, 0:64], scl[:, 0:64], scl[:, 64:128]),
                               reads=[sclb], writes=[sclb])
                            op(V, lambda e, cof=cof: e.tensor_add(
                                opar[:, cof:cof + 64], opar[:, cof:cof + 64],
                                scl[:, 0:64]),
                               reads=[sclb, oparb], writes=[oparb])
                            op(V, lambda e, cof=cof: e.tensor_sub(
                                sca[:, 0:64],
                                bcum[:, cof + 63:cof + 64].broadcast_to([128, 64]),
                                bcum[:, cof:cof + 64]),
                               reads=[bcumb], writes=[scab])
                            op(S, lambda e: e.activation(khd[:, 0:64],
                                                         sca[:, 0:64], AF.Exp),
                               reads=[scab], writes=[khdb])
                            op(V, lambda e, cof=cof: e.tensor_mul(
                                khd[:, 0:64], khd[:, 0:64],
                                kv5["kk"][:, cof:cof + 64]),
                               reads=[khdb, kv5b["kk"]], writes=[khdb])
                            pof, pob = nxt()
                            op(T, lambda e, pof=pof: e.transpose(
                                self.pt8[:64, pof:pof + 128], khd[:, 0:64],
                                self.identb[:]),
                                reads=[khdb, self.identb_b], writes=[pob])
                            op(V, lambda e, pof=pof: e.tensor_copy(
                                khT[0:64, 0:128], self.pt8[:64, pof:pof + 128]),
                               reads=[pob], writes=[khTb])
                            for hh in range(2):
                                op(T, lambda e, hh=hh, tof=tof: e.matmul(
                                    ps[4][hh * 64:hh * 64 + 64, 0:64],
                                    vtk[0:64, tof + hh * 64:tof + hh * 64 + 64],
                                    khT[0:64, hh * 64:hh * 64 + 64],
                                    start=True, stop=True),
                                   reads=[vtkb, khTb], writes=[pb[4]])
                            op(V, lambda e, c2=c2, pr=pr: e.tensor_copy(
                                icb_[:, c2 * 512 + pr * 64:c2 * 512 + pr * 64 + 64],
                                ps[4][:, 0:64]),
                               reads=[pb[4]], writes=[icbb])
                            for hh in range(2):
                                op(T, lambda e, hh=hh, tof=tof: e.matmul(
                                    ps[4][hh * 64:hh * 64 + 64, 0:64],
                                    onesd[0:64, 0:64],
                                    lnwT[0:64, tof + hh * 64:tof + hh * 64 + 64],
                                    start=True, stop=True),
                                   reads=[lnwTb, onesdb], writes=[pb[4]])
                            op(S, lambda e, c2=c2, pr=pr: e.activation(
                                ewb_[:, c2 * 512 + pr * 64:
                                     c2 * 512 + pr * 64 + 64],
                                ps[4][:, 0:64], AF.Exp),
                               reads=[pb[4]], writes=[ewbb])

                    for pr in range(8):
                        op(V, lambda e, pr=pr: e.tensor_sub(
                            scl[:, 0:128], bcum[:, pr * 128:(pr + 1) * 128],
                            lnw[:, pr * 128:(pr + 1) * 128]),
                           reads=[bcumb, lnwb], writes=[sclb])
                        op(S, lambda e, pr=pr: e.activation(
                            eprev[:, pr * 128:(pr + 1) * 128], scl[:, 0:128],
                            AF.Exp),
                           reads=[sclb], writes=[eprevb])
                    for dc in range(2):
                        for k in range(8):
                            op(T, lambda e, k=k, dc=dc: e.matmul(
                                ps[4 + dc][:, 0:512],
                                dd5[:, k * 128:(k + 1) * 128],
                                wslot[:, k * 1024 + dc * 512:
                                      k * 1024 + (dc + 1) * 512],
                                start=(k == 0), stop=(k == 7)),
                                reads=[dd5b, wslotb], writes=[pb[4 + dc]])
                    for dc in range(2):
                        op(S, lambda e, dc=dc: e.activation(
                            dd5[:, dc * 512:(dc + 1) * 512],
                            ps[4 + dc][:, 0:512], AF.Silu),
                           reads=[pb[4 + dc]], writes=[dd5b])
                    for n, src, srcb in (("op", opar, oparb),
                                         ("rd", kv5["rr"], kv5b["rr"]),
                                         ("ic", icb_, icbb), ("ew", ewb_, ewbb),
                                         ("ep", eprev, eprevb),
                                         ("gs", dd5, dd5b)):
                        if n == "op":
                            self.dma(GD, lambda e, ti=ti, src=src: e.dma_start(
                                out=spl["op"][:, ti * 1024:(ti + 1) * 1024],
                                in_=src[:]), splb["op"], reads=[srcb])
                        else:
                            self.dma(SY, lambda e, n=n, ti=ti, src=src:
                                     e.dma_start(
                                out=spl[n][:, ti * 1024:(ti + 1) * 1024],
                                in_=src[:]), splb[n], reads=[srcb])
                    for c2 in range(2):
                        op(V, lambda e, c2=c2: e.tensor_mul(
                            Sst[:, 0:512], Sst[:, 0:512],
                            ewb_[:, c2 * 512:(c2 + 1) * 512]),
                           reads=[ewbb, Sstb], writes=[Sstb])
                        op(V, lambda e, c2=c2: e.tensor_add(
                            Sst[:, 0:512], Sst[:, 0:512],
                            icb_[:, c2 * 512:(c2 + 1) * 512]),
                           reads=[icbb, Sstb], writes=[Sstb])
                self.barrier()

            recvS, recvSb = sb("recvS", [128, 512], BF16)
            if use_cc:
                cin = nc.dram_tensor(tag + "ccin", [128 * 512], BF16,
                                     kind="Internal").ap()
                cout = nc.dram_tensor(tag + "ccout", [2 * 128 * 512], BF16,
                                      kind="Internal").ap()
                cinb, coutb = Buf(tag + "ccin"), Buf(tag + "ccout")
                self.dma(SY, lambda e: e.dma_start(
                    out=cin.rearrange("(p f) -> p f", p=128),
                    in_=Sst[:, 0:512]), cinb, reads=[Sstb])
                op(G, lambda e: e.collective_compute(
                    "AllGather", OP.bypass, ins=[cin[:]], outs=[cout[:]],
                    replica_groups=[[2 * i, 2 * i + 1]
                                    for i in range(self.n_cores // 2)]),
                   reads=[cinb], writes=[coutb])
                self.dma(SY, lambda e: e.dma_start(
                    out=recvS[:], in_=cout[0:128 * 512].rearrange(
                        "(p f) -> p f", p=128)), recvSb, reads=[coutb])
                op(V, lambda e: e.tensor_scalar_mul(
                    recvS[:], recvS[:], self.cpar[:, 0:1]),
                   reads=[recvSb, self.cparb], writes=[recvSb])
            else:
                op(V, lambda e: e.memset(recvS[:], 0.0), writes=[recvSb])
            op(V, lambda e: e.tensor_copy(Sst[:, 0:512], recvS[:]),
               reads=[recvSb], writes=[Sstb])

            self.dma(GD, lambda e: e.dma_start(
                out=wslot[:], in_=self.ab_w["ab_outw"][:]), wslotb)

            with contextlib.ExitStack() as pbx:
                for n in ("ab_gng", "ab_gnb"):
                    W[n], Wb[n] = sb(n[3:], [128, 1024], BF16, st=pbx)
                    self.dma(GD, lambda e, n=n, t_=W[n]: e.dma_start(
                        out=t_[:], in_=self.ab_w[n][:]), Wb[n])
                ro, rob = sb("ro", [128, 1024], BF16, st=pbx)
                oprr, oprrb = sb("oprr", [128, 1024], BF16, st=pbx)
                icr, icrb = sb("icr", [128, 1024], BF16, st=pbx)
                ewr, ewrb = sb("ewr", [128, 1024], BF16, st=pbx)
                epr, eprb = sb("epr", [128, 1024], BF16, st=pbx)
                gsr, gsrb = sb("gsr", [128, 1024], BF16, st=pbx)
                od, odb = sb("od", [128, 1024], st=pbx)
                otk, otkb = sb("otk", [128, 1024], st=pbx)
                on2, on2b = sb("on2", [128, 1024], BF16, st=pbx)
                st2, st2b = sb("st2", [128, 32], st=pbx)
                oT2, oT2b = sb("oT2", [128, 1024], BF16, st=pbx)
                for ti in range(NT):
                    tl = ti + 1
                    for n, dst, dstb2 in (("op", oprr, oprrb), ("rd", ro, rob),
                                          ("ic", icr, icrb), ("ew", ewr, ewrb),
                                          ("ep", epr, eprb), ("gs", gsr, gsrb)):
                        self.dma(SY, lambda e, n=n, ti=ti, dst=dst: e.dma_start(
                            out=dst[:], in_=spl[n][:, ti * 1024:(ti + 1) * 1024]),
                            dstb2, reads=[splb[n]])
                    gci = ti * 2
                    for c2 in range(2):
                        ci = gci + c2
                        pcur, pnxt = ci % 2, 1 - ci % 2
                        bI = c2 % 2
                        for pr in range(8):
                            for hh in range(2):
                                op(T, lambda e, pr=pr, hh=hh, c2=c2, pcur=pcur,
                                   bI=bI: e.matmul(
                                    ps[bI][hh * 64:hh * 64 + 64,
                                           pr * 64:pr * 64 + 64],
                                    Sst[hh * 64:hh * 64 + 64,
                                        pcur * 512 + pr * 64:
                                        pcur * 512 + pr * 64 + 64],
                                    ro[hh * 64:hh * 64 + 64,
                                       pr * 128 + c2 * 64:pr * 128 + c2 * 64 + 64],
                                    start=True, stop=True),
                                   reads=[Sstb, rob], writes=[pb[bI]])
                        op(V, lambda e, c2=c2, bI=bI: e.tensor_mul(
                            od.rearrange("p (b r) -> p b r", b=8, r=128)
                            [:, :, c2 * 64:c2 * 64 + 64],
                            ps[bI][:, 0:512].rearrange("p (b r) -> p b r",
                                                       b=8, r=64),
                            epr.rearrange("p (b r) -> p b r", b=8, r=128)
                            [:, :, c2 * 64:c2 * 64 + 64]),
                           reads=[pb[bI], eprb], writes=[odb])
                        op(V, lambda e, c2=c2: e.tensor_add(
                            od.rearrange("p (b r) -> p b r", b=8, r=128)
                            [:, :, c2 * 64:c2 * 64 + 64],
                            od.rearrange("p (b r) -> p b r", b=8, r=128)
                            [:, :, c2 * 64:c2 * 64 + 64],
                            oprr.rearrange("p (b r) -> p b r", b=8, r=128)
                            [:, :, c2 * 64:c2 * 64 + 64]),
                           reads=[oprrb, odb], writes=[odb])
                        op(V, lambda e, c2=c2, pcur=pcur, pnxt=pnxt: e.tensor_mul(
                            Sst[:, pnxt * 512:pnxt * 512 + 512],
                            Sst[:, pcur * 512:pcur * 512 + 512],
                            ewr[:, c2 * 512:(c2 + 1) * 512]),
                           reads=[ewrb, Sstb], writes=[Sstb])
                        op(V, lambda e, c2=c2, pnxt=pnxt: e.tensor_add(
                            Sst[:, pnxt * 512:pnxt * 512 + 512],
                            Sst[:, pnxt * 512:pnxt * 512 + 512],
                            icr[:, c2 * 512:(c2 + 1) * 512]),
                           reads=[icrb, Sstb], writes=[Sstb])
                    for pr in range(8):
                        bJ = 6
                        op(T, lambda e, pr=pr, bJ=bJ: e.transpose(
                            ps[bJ][:, 0:128], od[:, pr * 128:(pr + 1) * 128],
                            self.identf[:]),
                            reads=[odb, self.identf_b], writes=[pb[bJ]])
                        op(V, lambda e, pr=pr, bJ=bJ: e.tensor_copy(
                            otk[:, pr * 128:(pr + 1) * 128], ps[bJ][:, 0:128]),
                           reads=[pb[bJ]], writes=[otkb])
                    for h in range(16):
                        pr, hh = h // 2, h % 2
                        hof = pr * 128 + hh * 64
                        op(V, lambda e, hof=hof, h=h: e.tensor_reduce(
                            st2[:, h:h + 1], otk[:, hof:hof + 64], AX, OP.add),
                           reads=[otkb], writes=[st2b])
                    op(V, lambda e: e.tensor_scalar_mul(st2[:, 0:16],
                                                        st2[:, 0:16], -1.0 / 64),
                       reads=[st2b], writes=[st2b])
                    for h in range(16):
                        pr, hh = h // 2, h % 2
                        hof = pr * 128 + hh * 64
                        op(S, lambda e, hof=hof, h=h: e.activation(
                            on2[:, hof:hof + 64], otk[:, hof:hof + 64],
                            AF.Square, bias=st2[:, h:h + 1],
                            accum_out=st2[:, 16 + h:17 + h]),
                           reads=[otkb, st2b], writes=[on2b, st2b])
                    op(V, lambda e: e.tensor_scalar(
                        st2[:, 16:32], st2[:, 16:32], 1.0 / 64, 0.001,
                        op0=OP.mult, op1=OP.add),
                       reads=[st2b], writes=[st2b])
                    op(S, lambda e: e.activation(st2[:, 16:32], st2[:, 16:32],
                                                 AF.Ln),
                       reads=[st2b], writes=[st2b])
                    op(S, lambda e: e.activation(st2[:, 16:32], st2[:, 16:32],
                                                 AF.Exp, scale=-0.5),
                       reads=[st2b], writes=[st2b])
                    for h in range(16):
                        pr, hh = h // 2, h % 2
                        hof = pr * 128 + hh * 64
                        op(V, lambda e, hof=hof, h=h: e.tensor_scalar(
                            otk[:, hof:hof + 64], otk[:, hof:hof + 64],
                            st2[:, h:h + 1], st2[:, 16 + h:17 + h],
                            op0=OP.add, op1=OP.mult),
                           reads=[st2b, otkb], writes=[otkb])
                    op(V, lambda e: e.scalar_tensor_tensor(
                        otk[:], otk[:], 1.0, W["ab_gng"][:],
                        op0=OP.mult, op1=OP.mult),
                       reads=[Wb["ab_gng"], otkb], writes=[otkb])
                    op(V, lambda e: e.tensor_add(otk[:], otk[:], W["ab_gnb"][:]),
                       reads=[Wb["ab_gnb"], otkb], writes=[otkb])
                    op(V, lambda e: e.tensor_mul(on2[:], otk[:], gsr[:]),
                       reads=[otkb, gsrb], writes=[on2b])
                    for half in range(2):
                        for kk in range(4):
                            k = half * 4 + kk
                            op(T, lambda e, k=k, kk=kk, half=half: e.transpose(
                                self.pt8[:, half * 512 + kk * 128:
                                         half * 512 + (kk + 1) * 128],
                                on2[:, k * 128:(k + 1) * 128], self.identb[:]),
                                reads=[on2b, self.identb_b],
                                writes=[self.pt8b])
                        for kk in range(4):
                            k = half * 4 + kk
                            op(S, lambda e, k=k, half=half, kk=kk: e.copy(
                                oT2[:, k * 128:(k + 1) * 128],
                                self.pt8[:, half * 512 + kk * 128:
                                         half * 512 + (kk + 1) * 128]),
                               reads=[self.pt8b], writes=[oT2b])
                    for m in range(8):
                        bK = (6, 5)[m % 2]
                        for k in range(8):
                            op(T, lambda e, m=m, k=k, bK=bK: e.matmul(
                                ps[bK][:, 0:128],
                                wslot[:, k * 1024 + m * 128:
                                      k * 1024 + (m + 1) * 128],
                                oT2[:, k * 128:(k + 1) * 128],
                                start=(k == 0), stop=(k == 7)),
                                reads=[oT2b, wslotb], writes=[pb[bK]])
                        op(V, lambda e, m=m, bK=bK: e.tensor_single_scalar(
                            on2[:, m * 128:(m + 1) * 128], ps[bK][:, 0:128],
                            W["ab_outb"][:, m:m + 1], OP.add),
                           reads=[pb[bK], Wb["ab_outb"]], writes=[on2b])
                    for half in range(2):
                        for kk in range(4):
                            m = half * 4 + kk
                            op(T, lambda e, m=m, kk=kk, half=half: e.transpose(
                                self.pt8[:, half * 512 + kk * 128:
                                         half * 512 + (kk + 1) * 128],
                                on2[:, m * 128:(m + 1) * 128], self.identb[:]),
                                reads=[on2b, self.identb_b],
                                writes=[self.pt8b])
                        op(V, lambda e, half=half, tl=tl: e.tensor_add(
                            self.x_ap(tl, half * 512, (half + 1) * 512),
                            self.x_ap(tl, half * 512, (half + 1) * 512),
                            self.pt8[:, half * 512:half * 512 + 512]),
                           reads=[self.pt8b],
                           writes=[self.xb])
        self.barrier()

    def program(self):
        if "ffn1" in self.stages or "ffn2" in self.stages:
            self.declare_ffn_weights()
            self.emit_ffn_cache()
        if "ln1" in self.stages:
            self.emit_ln(0, self.NTE)
        if "attn_a" in self.stages:
            self.free_xov()
            self.declare_attn_a_weights()
            self.emit_attn_a("aa", use_cc="aa_cc" in self.stages)
        if "ffn1" in self.stages:
            if "ln2" in self.stages:
                self.emit_ln(1, self.NTE)
            self.emit_ffn("f1")
        if "attn_b" in self.stages:
            if "ln3" in self.stages:
                self.emit_ln(1, self.NTE)
            self.declare_attn_b_weights()
            self.emit_attn_b("ab", use_cc="ab_cc" in self.stages)
        if "ffn2" in self.stages:
            if "ln4" in self.stages:
                self.emit_ln(1, self.NTE)
            self.emit_ffn("f2")
        if "out_x" in self.stages:
            xo = self.dout("x_out", [128, self.NT * 1024], F32)
            ob = Buf("xout")
            self.dma(self.SY, lambda e: e.dma_start(
                out=xo[:], in_=self.x_tm[:, 0:self.NT * 1024]),
                ob, reads=[self.xb])
            self.SY.dep(ob.wtoks()[0])


def nc_psum(nc, name):
    return nc.psum_tensor(name, [128, 512], F32)


def tile_w(w):
    w = np.ascontiguousarray(np.asarray(w, np.float32))
    din, dout = w.shape
    assert din % 128 == 0, din
    return np.ascontiguousarray(
        w.reshape(din // 128, 128, dout).transpose(1, 0, 2).reshape(128, -1))


def pad_rows(w, rows):
    w = np.asarray(w, np.float32)
    out = np.zeros((rows, w.shape[1]), np.float32)
    out[:w.shape[0]] = w
    return out


def make_x0ext(x_shard, prev3):
    TPC = x_shard.shape[0]
    NTE = TPC // 128 + 1
    full = np.zeros((NTE * 128, 1024), np.float32)
    full[125:128] = prev3
    full[128:] = x_shard
    return np.ascontiguousarray(
        full.reshape(NTE, 128, 1024).transpose(1, 0, 2).reshape(128, -1))


def unmake_tm(img, TPC):
    NT = TPC // 128
    return np.ascontiguousarray(img).reshape(128, NT, 1024).transpose(1, 0, 2
                                                                      ).reshape(TPC, 1024)


FULL_STAGES = ("ln1", "attn_a", "aa_cc", "ln2", "ffn1", "ln3", "attn_b",
               "ab_cc", "ln4", "ffn2", "out_x")


def kernel(**inputs):
    from concourse.bass_utils import run_bass_kernel_spmd
    TPC = 2048
    x = np.asarray(inputs["L0_attn_a_hidden_states"], np.float32)
    B_, T_, D_ = x.shape
    nc = bass.Bass("TRN2", target_bir_lowering=False, debug=False)
    b = Builder(nc, TPC, stages=FULL_STAGES, n_cores=8, parity=0)
    b.prep_attn_a_weights(inputs, "L0_attn_a_")
    b.prep_ffn_weights(inputs, "L0_ffn_")
    b.prep_attn_b_weights(inputs, "L0_attn_b_")
    nc = b.build()

    in_maps = []
    for c in range(8):
        bb, half = c // 2, c % 2
        shard = x[bb, half * TPC:(half + 1) * TPC]
        prev3 = (x[bb, half * TPC - 3:half * TPC] if half == 1
                 else np.zeros((3, 1024), np.float32))
        m = dict(b.inputs)
        m["x0ext"] = make_x0ext(np.ascontiguousarray(shard),
                                np.ascontiguousarray(prev3))
        m["c_par"] = np.full((128, 1), float(half), np.float32)
        in_maps.append(m)
    res = run_bass_kernel_spmd(nc, in_maps, core_ids=list(range(8)))
    out = np.zeros((B_, T_, D_), np.float32)
    for c in range(8):
        bb, half = c // 2, c % 2
        out[bb, half * TPC:(half + 1) * TPC] = unmake_tm(
            res.results[c]["x_out"], TPC)
    return out


# version 09
# speedup vs baseline: 1.6196x; 1.0123x over previous
import contextlib
import numpy as np
import ml_dtypes
import concourse.bass as bass
import concourse.mybir as mybir

F32 = mybir.dt.float32
BF16 = mybir.dt.bfloat16
SKIP_T_SELF = False
AF = mybir.ActivationFunctionType
OP = mybir.AluOpType
AX = mybir.AxisListType.X
BF = ml_dtypes.bfloat16

D = 1024
HV, HK, DK, DV = 16, 8, 64, 64
K_DIM, V_DIM, CONV_DIM = 512, 1024, 2048
KS = 4
NE, I_FF, NE1 = 8, 1408, 9
R_LORA = 32
CA, CB, SBOT = 128, 64, 8


class Eng:

    def __init__(self, name, sem, inc=1, skip_self=False):
        self.name, self.sem, self.inc = name, sem, inc
        self.skip_self = skip_self
        self.count = 0
        self.marks = {}
        self.thunks = []

    def dep(self, *tokens):
        for t in tokens:
            if t is None:
                continue
            sem, val = t
            if self.skip_self and sem is self.sem:
                continue
            key = id(sem)
            if self.marks.get(key, 0) >= val:
                continue
            self.marks[key] = val
            self.thunks.append(lambda e, s=sem, v=val: e.wait_ge(s, v))

    def emit(self, fn, *deps):
        self.dep(*deps)
        if self.skip_self and self.count % 8 == 0 and self.count > 32:
            self.thunks.append(
                lambda e, s=self.sem, v=self.count - 32: e.wait_ge(s, v))
        self.count += self.inc
        self.thunks.append(lambda e, f=fn, s=self.sem, i=self.inc: f(e).then_inc(s, i))
        return (self.sem, self.count)

    def tok(self):
        return (self.sem, self.count)

    def replay(self, engine):
        for th in self.thunks:
            th(engine)


class Buf:

    __slots__ = ("w", "r", "name", "dsem", "dcount")

    def __init__(self, name=""):
        self.name = name
        self.w = {}
        self.r = {}
        self.dsem = None
        self.dcount = 0

    def wtoks(self):
        return list(self.w.values())

    def rtoks(self):
        return list(self.r.values())

    def set_w(self, tok):
        self.w = {id(tok[0]): tok}
        self.r = {}

    def add_w(self, tok):
        self.w[id(tok[0])] = tok

    def add_r(self, tok):
        self.r[id(tok[0])] = tok


class Builder:
    def __init__(self, nc, TPC, stages, n_cores=8, parity=0):
        self.nc = nc
        self.parity = parity
        self.TPC = TPC
        self.NT = TPC // 128
        self.NTE = self.NT + 1
        self.stages = stages
        self.n_cores = n_cores
        self.inputs = {}
        self.dram = {}

    def din(self, name, shape, dtype=F32):
        t = self.nc.dram_tensor(name, list(shape), dtype, kind="ExternalInput").ap()
        self.dram[name] = t
        return t

    def dout(self, name, shape, dtype=F32):
        t = self.nc.dram_tensor(name, list(shape), dtype, kind="ExternalOutput").ap()
        return t

    def sbuf(self, name, shape, dtype=F32, stack=None):
        t = (stack or self.ctx).enter_context(
            self.nc.sbuf_tensor(name, list(shape), dtype))
        return t, Buf(name)

    def op(self, eng, fn, reads=(), writes=()):
        deps = []
        for b in reads:
            deps += b.wtoks()
        for b in writes:
            deps += b.wtoks() + b.rtoks()
        t = eng.emit(fn, *deps)
        for b in reads:
            b.add_r(t)
        for b in writes:
            b.set_w(t)
        return t

    def dma(self, eng, fn, dst, reads=()):
        if dst.dsem is None:
            dst.dsem = self.ctx.enter_context(
                self.nc.semaphore("dsem_" + dst.name))
            self.dma_bufs.append(dst)
        deps = []
        for b in reads:
            deps += b.wtoks()
        deps += dst.wtoks() + dst.rtoks()
        eng.dep(*deps)
        dst.dcount += 16
        eng.thunks.append(lambda e, f=fn, s=dst.dsem: f(e).then_inc(s, 16))
        tok = (dst.dsem, dst.dcount)
        for b in reads:
            b.add_r(tok)
        dst.set_w(tok)
        return tok

    def input_sbuf(self, name, arr, stack=None, cast=False):
        arr = np.ascontiguousarray(arr)
        hdt = {np.dtype(np.float32): F32, np.dtype(BF): BF16,
               np.dtype(np.int32): mybir.dt.int32}[arr.dtype]
        sdt = BF16 if cast else hdt
        d = self.din(name, arr.shape, hdt)
        s, b = self.sbuf(name + "_s", arr.shape, sdt, stack=stack)
        eng = self.GD if cast else self.SY
        self.dma(eng, lambda e, d=d, s=s: e.dma_start(out=s[:], in_=d[:]), b)
        self.inputs[name] = arr
        return s, b

    def barrier(self):
        toks = [e.tok() for e in self.engs]
        toks += [(b.dsem, b.dcount) for b in self.dma_bufs]
        for e in self.engs:
            e.dep(*toks)

    def build(self):
        nc = self.nc
        self.ctx = contextlib.ExitStack()
        with self.ctx:
            sems = {s: self.ctx.enter_context(nc.semaphore("sem_" + s))
                    for s in ("t", "v", "s", "g", "sy", "gd", "sd")}
            self.T = Eng("T", sems["t"], skip_self=SKIP_T_SELF)
            self.V = Eng("V", sems["v"])
            self.S = Eng("S", sems["s"])
            self.G = Eng("G", sems["g"])
            self.SY = Eng("SY", sems["sy"], inc=16)
            self.SD = self.S
            self.GD = self.G
            self.engs = [self.T, self.V, self.S, self.G, self.SY]
            self.dma_bufs = []
            self.psum = []
            self.psb = []
            for i in range(7):
                self.psum.append(self.ctx.enter_context(
                    nc.psum_tensor(f"pb{i}", [128, 512], F32)))
                self.psb.append(Buf(f"pb{i}"))
            self.pt8 = None
            self.pt8b = None
            self.declare_static()
            self.program()
            with nc.Block() as block:
                block.tensor(lambda e: self.T.replay(e))
                block.vector(lambda e: self.V.replay(e))
                block.scalar(lambda e: self.S.replay(e))
                block.gpsimd(lambda e: self.G.replay(e))
                block.sync(lambda e: self.SY.replay(e))
        return nc

    def declare_static(self):
        NTE = self.NTE
        self.x_tm, self.xb = self.sbuf("x_tm", [128, self.NT * 1024], F32)
        self.xov_stack = contextlib.ExitStack()
        self.xov = self.xov_stack.enter_context(
            self.nc.sbuf_tensor("x_ov", [128, 1024], F32, side="right"))
        self.lnT, self.lnb = self.sbuf("lnT", [128, 8 * NTE * 128], BF16)
        self.cst, self.cstb = self.sbuf("cstv", [128, 4], F32)
        self.cpar, self.cparb = self.input_sbuf(
            "c_par", np.full((128, 1), float(self.parity), np.float32))
        self.op(self.V, lambda e: e.memset(self.cst[:, 0:1], 1e-6),
                writes=[self.cstb])
        self.identf, self.identf_b = self.input_sbuf(
            "identf", np.eye(128, dtype=np.float32))
        self.identb, self.identb_b = self.input_sbuf(
            "identb", np.eye(128, dtype=np.float32).astype(BF))
        self.ones1b, self.ones1b_b = self.input_sbuf(
            "ones1b", np.ones((1, 128), np.float32).astype(BF))
        sel = np.zeros((16, 9 * 128), np.float32)
        for e in range(9):
            sel[e, e * 128:(e + 1) * 128] = 1.0
        self.selb, self.selb_b = self.input_sbuf("selb", sel.astype(BF))
        xin = self.din("x0ext", [128, NTE * 1024], F32)
        self.dma(self.SY, lambda e: e.dma_start(out=self.xov[:],
                                                in_=xin[:, 0:1024]), self.xb)
        self.dma(self.SY, lambda e: e.dma_start(
            out=self.x_tm[:], in_=xin[:, 1024:NTE * 1024]), self.xb)

    def lnT_ap(self, k, tile, cols=128):
        o = k * self.NTE * 128 + tile * 128
        return self.lnT[:, o:o + cols]

    def lnT_cols(self, k, c0, c1):
        o = k * self.NTE * 128
        return self.lnT[:, o + c0:o + c1]

    def x_ap(self, tile, c0=0, c1=1024):
        if tile == 0:
            return self.xov[:, c0:c1]
        return self.x_tm[:, (tile - 1) * 1024 + c0:(tile - 1) * 1024 + c1]

    def free_xov(self):
        if self.xov_stack is not None:
            self.barrier()
            self.xov_stack.close()
            self.xov_stack = None

    def emit_ln(self, t0, t1):
        V, S, T = self.V, self.S, self.T
        self._ln_no = getattr(self, "_ln_no", 0) + 1
        pfx = f"ln{self._ln_no}_"
        with contextlib.ExitStack() as ph:
            st, stb = self.sbuf(pfx + "st", [128, 16], F32, stack=ph)
            sc, scb = self.sbuf(pfx + "sc", [128, 2 * 1024], F32, stack=ph)
            tmp, tmpb = self.sbuf(pfx + "tmp", [128, 2 * 1024], F32, stack=ph)
            stbs = [Buf(f"st{p}") for p in range(2)]
            scbs = [Buf(f"sc{p}") for p in range(2)]
            tmpbs = [Buf(f"tmp{p}") for p in range(2)]
            for i in range(t0, t1):
                p = i % 2
                xap = self.x_ap(i)
                sb = stbs[p]
                c = lambda a, b_, p=p: st[:, p * 8 + a: p * 8 + b_]
                self.op(V, lambda e, xap=xap, c=c: e.tensor_reduce(
                    c(0, 1), xap, AX, OP.add), reads=[self.xb], writes=[sb])
                self.op(V, lambda e, c=c: e.tensor_scalar_mul(
                    c(1, 2), c(0, 1), -1.0 / 1024.0), reads=[sb], writes=[sb])
                self.op(S, lambda e, xap=xap, c=c, p=p: e.activation(
                    sc[:, p * 1024:(p + 1) * 1024], xap, AF.Square,
                    bias=c(1, 2), scale=1.0, accum_out=c(2, 3)),
                    reads=[self.xb, sb], writes=[scbs[p], sb])
                self.op(V, lambda e, c=c: e.tensor_scalar(
                    c(3, 4), c(2, 3), 1.0 / 1024.0, 1e-5, op0=OP.mult, op1=OP.add),
                    reads=[sb], writes=[sb])
                self.op(V, lambda e, c=c: e.reciprocal(c(4, 5), c(3, 4)),
                        reads=[sb], writes=[sb])
                self.op(S, lambda e, c=c: e.activation(c(5, 6), c(4, 5), AF.Sqrt),
                        reads=[sb], writes=[sb])
                self.op(V, lambda e, xap=xap, c=c, p=p: e.tensor_scalar(
                    tmp[:, p * 1024:(p + 1) * 1024], xap, c(1, 2), c(5, 6),
                    op0=OP.add, op1=OP.mult),
                    reads=[self.xb, sb], writes=[tmpbs[p]])
                for half in range(2):
                    pj = 5 + half
                    for kk in range(4):
                        k = half * 4 + kk
                        self.op(T, lambda e, k=k, kk=kk, pj=pj, p=p: e.transpose(
                            self.psum[pj][:, kk * 128:(kk + 1) * 128],
                            tmp[:, p * 1024 + k * 128:p * 1024 + (k + 1) * 128],
                            self.identf[:]),
                            reads=[tmpbs[p], self.identf_b], writes=[self.psb[pj]])
                    for kk in range(4):
                        k = half * 4 + kk
                        self.op(S, lambda e, k=k, kk=kk, pj=pj, i=i: e.copy(
                            self.lnT_ap(k, i), self.psum[pj][:, kk * 128:(kk + 1) * 128]),
                            reads=[self.psb[pj]], writes=[self.lnb])
        self.barrier()

    def prep_ffn_weights(self, p, base):
        gw = np.concatenate([np.asarray(p[base + "gate_w"]),
                             np.asarray(p[base + "shared_gate_w"])], 0).T
        gw = np.concatenate([gw, np.zeros((1024, 7), np.float32)], 1)
        self.inputs["ffn_gatewT"] = tile_w(gw)
        wg, wu, wd = (np.asarray(p[base + w]) for w in ("w_gate", "w_up", "w_down"))
        for e in range(NE):
            self.inputs[f"ffn_e{e}_g"] = tile_w(wg[e])
            self.inputs[f"ffn_e{e}_u"] = tile_w(wu[e])
            self.inputs[f"ffn_e{e}_d"] = tile_w(pad_rows(wd[e], 1536))
        self.inputs["ffn_e8_g"] = tile_w(np.asarray(p[base + "sw_gate"]))
        self.inputs["ffn_e8_u"] = tile_w(np.asarray(p[base + "sw_up"]))
        self.inputs["ffn_e8_d"] = tile_w(pad_rows(np.asarray(p[base + "sw_down"]), 1536))

    def declare_ffn_weights(self):
        self.ffn_w = {"gatewT": self.din("ffn_gatewT", [128, 8 * 16], F32)}
        self.ffn_wc = {}
        self.ffn_wcb = {}
        for e in range(NE1):
            for j, wdt in (("g", 8 * 1408), ("u", 8 * 1408), ("d", 12 * 1024)):
                self.ffn_w[(e, j)] = self.din(f"ffn_e{e}_{j}", [128, wdt], F32)
                self.ffn_wc[(e, j)] = self.nc.dram_tensor(
                    f"ffnc_e{e}_{j}", [128, wdt], BF16, kind="Internal").ap()
        cb = Buf("ffnc_all")
        for e in range(NE1):
            for j in ("g", "u", "d"):
                self.ffn_wcb[(e, j)] = cb

    def emit_ffn_cache(self):
        for e in range(NE1):
            for j in ("g", "u", "d"):
                self.dma(self.GD, lambda en, e=e, j=j: en.dma_start(
                    out=self.ffn_wc[(e, j)][:], in_=self.ffn_w[(e, j)][:]),
                    self.ffn_wcb[(e, j)])

    def emit_ffn(self, tag):
        T, V, S, G, GD, SY = self.T, self.V, self.S, self.G, self.GD, self.SY
        NT = self.NT
        ps, pb = self.psum, self.psb
        with contextlib.ExitStack() as ph:
            ps7 = ph.enter_context(nc_psum(self.nc, tag + "p7"))
            pb7 = Buf(tag + "p7")
            gwt, gwtb = self.sbuf(tag + "gwt", [128, 8 * 16], BF16, stack=ph)
            self.dma(GD, lambda e: e.dma_start(out=gwt[:], in_=self.ffn_w["gatewT"][:]),
                     gwtb)
            rl, rlb = self.sbuf(tag + "rl", [128, 16], F32, stack=ph)
            rs, rsb = self.sbuf(tag + "rs", [128, 8], F32, stack=ph)
            comb_all, combb = self.sbuf(tag + "comb", [128, NT * 16], F32, stack=ph)
            scr, scrb = self.sbuf(tag + "scr", [128, 16], F32, stack=ph)
            for i in range(1, NT + 1):
                cm = lambda a, b_, i=i: comb_all[:, (i - 1) * 16 + a:(i - 1) * 16 + b_]
                for k in range(8):
                    self.op(T, lambda e, k=k, i=i: e.matmul(
                        ps[0][:, 0:16], self.lnT_ap(k, i), gwt[:, k * 16:(k + 1) * 16],
                        start=(k == 0), stop=(k == 7)),
                        reads=[self.lnb, gwtb], writes=[pb[0]])
                self.op(V, lambda e: e.tensor_copy(rl[:], ps[0][:, 0:16]),
                        reads=[pb[0]], writes=[rlb])
                self.op(V, lambda e: e.reduce_max(rs[:, 0:1], rl[:, 0:8], AX),
                        reads=[rlb], writes=[rsb])
                self.op(V, lambda e: e.tensor_scalar_mul(rs[:, 1:2], rs[:, 0:1], -1.0),
                        reads=[rsb], writes=[rsb])
                self.op(S, lambda e: e.activation(rl[:, 0:8], rl[:, 0:8], AF.Exp,
                                                  bias=rs[:, 1:2], scale=1.0),
                        reads=[rsb, rlb], writes=[rlb])
                self.op(V, lambda e: e.reduce_max(rs[:, 3:4], rl[:, 0:8], AX),
                        reads=[rlb], writes=[rsb])
                self.op(V, lambda e, cm=cm: e.tensor_single_scalar(
                    cm(0, 8), rl[:, 0:8], rs[:, 3:4], OP.is_lt),
                    reads=[rlb, rsb], writes=[combb])
                self.op(V, lambda e, cm=cm: e.tensor_mul(scr[:, 0:8], rl[:, 0:8],
                                                         cm(0, 8)),
                        reads=[rlb, combb], writes=[scrb])
                self.op(V, lambda e: e.reduce_max(rs[:, 4:5], scr[:, 0:8], AX),
                        reads=[scrb], writes=[rsb])
                self.op(V, lambda e: e.tensor_single_scalar(
                    scr[:, 0:8], rl[:, 0:8], rs[:, 4:5], OP.is_ge),
                    reads=[rlb, rsb], writes=[scrb])
                self.op(V, lambda e: e.tensor_add(rs[:, 5:6], rs[:, 3:4], rs[:, 4:5]),
                        reads=[rsb], writes=[rsb])
                self.op(V, lambda e: e.reciprocal(rs[:, 6:7], rs[:, 5:6]),
                        reads=[rsb], writes=[rsb])
                self.op(V, lambda e, cm=cm: e.scalar_tensor_tensor(
                    cm(0, 8), rl[:, 0:8], rs[:, 6:7], scr[:, 0:8],
                    op0=OP.mult, op1=OP.mult),
                    reads=[rlb, rsb, scrb], writes=[combb])
                self.op(S, lambda e, cm=cm: e.activation(cm(8, 9), rl[:, 8:9],
                                                         AF.Sigmoid),
                        reads=[rlb], writes=[combb])
            self.barrier()

            wgs, wgb = self.sbuf(tag + "wgs", [128, 8 * 1408], BF16, stack=ph)
            wus, wub = self.sbuf(tag + "wus", [128, 8 * 1408], BF16, stack=ph)
            wds, wdb = self.sbuf(tag + "wds", [128, 12 * 1024], BF16, stack=ph)
            h_buf, _ = self.sbuf(tag + "h", [128, 11 * 512], BF16, stack=ph)
            hb = [Buf(f"h{m}") for m in range(11)]
            sg_buf, sgb = self.sbuf(tag + "sg", [128, 1024], F32, stack=ph)
            sgb2 = [Buf("sg0"), Buf("sg1")]
            NCH = NT * 128 // 512

            def down_mms(e, ch, m, tts):
                for tt in tts:
                    for dc in range(2):
                        bk = (tt % 2) * 2 + dc
                        self.op(T, lambda en, m=m, tt=tt, dc=dc, bk=bk:
                                en.matmul(
                            ps[bk][:, 0:512],
                            h_buf[:, m * 512 + tt * 128:m * 512 + (tt + 1) * 128],
                            wds[:, m * 1024 + dc * 512:m * 1024 + (dc + 1) * 512],
                            start=(m == 0), stop=(m == 10)),
                            reads=[hb[m], wdb], writes=[pb[bk]])

            def down_evac(e, ch, tts):
                for tt in tts:
                    tile = 1 + 4 * ch + tt
                    for dc in range(2):
                        bk = (tt % 2) * 2 + dc
                        xap = self.x_ap(tile, dc * 512, (dc + 1) * 512)
                        cm = comb_all[:, (tile - 1) * 16 + e:(tile - 1) * 16 + e + 1]
                        self.op(V, lambda en, xap=xap, bk=bk, cm=cm:
                                en.scalar_tensor_tensor(
                            xap, ps[bk][:, 0:512], cm, xap,
                            op0=OP.mult, op1=OP.add),
                            reads=[pb[bk], combb, self.xb], writes=[self.xb])

            for e in range(NE1):
                self.dma(SY, lambda en, e=e: en.dma_start(
                    out=wgs[:], in_=self.ffn_wc[(e, "g")][:]), wgb,
                    reads=[self.ffn_wcb[(e, "g")]])
                self.dma(SY, lambda en, e=e: en.dma_start(
                    out=wus[:], in_=self.ffn_wc[(e, "u")][:]), wub,
                    reads=[self.ffn_wcb[(e, "u")]])
                self.dma(SY, lambda en, e=e: en.dma_start(
                    out=wds[:], in_=self.ffn_wc[(e, "d")][:]), wdb,
                    reads=[self.ffn_wcb[(e, "d")]])
                for ch in range(NCH):
                    for m in range(11):
                        hs = m % 2
                        pgx, pgxb = (ps[4], pb[4]) if hs == 0 else (ps[5], pb[5])
                        pux, puxb = (ps[6], pb[6]) if hs == 0 else (ps7, pb7)
                        for k in range(8):
                            self.op(T, lambda en, k=k, m=m, ch=ch, pgx=pgx:
                                    en.matmul(
                                pgx[:, 0:512],
                                wgs[:, k * 1408 + m * 128:k * 1408 + (m + 1) * 128],
                                self.lnT_cols(k, 128 + ch * 512,
                                              128 + (ch + 1) * 512),
                                start=(k == 0), stop=(k == 7)),
                                reads=[wgb, self.lnb], writes=[pgxb])
                        for k in range(8):
                            self.op(T, lambda en, k=k, m=m, ch=ch, pux=pux:
                                    en.matmul(
                                pux[:, 0:512],
                                wus[:, k * 1408 + m * 128:k * 1408 + (m + 1) * 128],
                                self.lnT_cols(k, 128 + ch * 512,
                                              128 + (ch + 1) * 512),
                                start=(k == 0), stop=(k == 7)),
                                reads=[wub, self.lnb], writes=[puxb])
                        self.op(S, lambda en, hs=hs, pgx=pgx: en.activation(
                            sg_buf[:, hs * 512:(hs + 1) * 512], pgx[:, 0:512],
                            AF.Silu),
                            reads=[pgxb], writes=[sgb2[hs]])
                        self.op(V, lambda en, m=m, hs=hs, pux=pux: en.tensor_mul(
                            h_buf[:, m * 512:(m + 1) * 512],
                            sg_buf[:, hs * 512:(hs + 1) * 512],
                            pux[:, 0:512]),
                            reads=[sgb2[hs], puxb], writes=[hb[m]])
                        if m >= 1:
                            down_mms(e, ch, m - 1, (0, 1))
                    down_mms(e, ch, 10, (0, 1))
                    down_evac(e, ch, (0, 1))
                    for m in range(11):
                        down_mms(e, ch, m, (2, 3))
                    down_evac(e, ch, (2, 3))
        self.barrier()

    def prep_attn_a_weights(self, p, base):
        self.inputs["aa_wqkv"] = tile_w(np.asarray(p[base + "W_qkv"]))
        self.inputs["aa_wz"] = tile_w(np.asarray(p[base + "W_z"]))
        wba = np.concatenate([np.asarray(p[base + "W_b"]),
                              np.asarray(p[base + "W_a"])], 1)
        self.inputs["aa_wba"] = tile_w(wba)
        self.inputs["aa_wout"] = tile_w(np.asarray(p[base + "W_out"]))
        cw = np.asarray(p[base + "conv_w"], np.float32)
        self.inputs["aa_convw"] = np.ascontiguousarray(
            cw.reshape(16, 128, 4).transpose(1, 0, 2).reshape(128, 64))
        self.inputs["aa_dtb"] = np.tile(np.asarray(p[base + "dt_bias"],
                                                   np.float32), (128, 1))
        self.inputs["aa_alog"] = np.tile(np.asarray(p[base + "A_log"],
                                                    np.float32), (128, 1))
        self.inputs["aa_normw"] = np.tile(np.asarray(p[base + "norm_w"],
                                                     np.float32), (128, 16))

    def declare_attn_a_weights(self):
        self.aa_w = {}
        for n, sh in (("aa_wqkv", [128, 8 * 2048]), ("aa_wz", [128, 8 * 1024]),
                      ("aa_wba", [128, 8 * 32]), ("aa_wout", [128, 8 * 1024]),
                      ("aa_convw", [128, 64]), ("aa_dtb", [128, 16]),
                      ("aa_alog", [128, 16]), ("aa_normw", [128, 1024])):
            self.aa_w[n] = self.din(n, sh, F32)

    def emit_attn_a(self, tag, use_cc):
        T, V, S, G, SY, GD = self.T, self.V, self.S, self.G, self.SY, self.GD
        NT, NTE, nc = self.NT, self.NTE, self.nc
        ps, pb = self.psum, self.psb
        op = self.op
        SCW = 2
        NSC = (NT + SCW - 1) // SCW
        with contextlib.ExitStack() as ph:
            self.pt8 = ph.enter_context(
                nc.psum_tensor(tag + "pt8", [128, 1024], BF16))
            self.pt8b = Buf(tag + "pt8")
            sb = lambda n, sh, dt=F32, st=None: self.sbuf(tag + n, sh, dt,
                                                          stack=st or ph)
            wslot, wslotb = sb("wslot", [128, 8 * 2048], BF16)
            self.dma(GD, lambda e: e.dma_start(out=wslot[:],
                                               in_=self.aa_w["aa_wqkv"][:]), wslotb)
            wba, wbab = sb("wba", [128, 8 * 32], BF16)
            self.dma(GD, lambda e: e.dma_start(out=wba[:],
                                               in_=self.aa_w["aa_wba"][:]), wbab)
            cvw, cvwb = sb("cvw", [128, 64])
            self.dma(SY, lambda e: e.dma_start(out=cvw[:],
                                               in_=self.aa_w["aa_convw"][:]), cvwb)
            dtb, dtbb = sb("dtb", [128, 16])
            self.dma(SY, lambda e: e.dma_start(out=dtb[:],
                                               in_=self.aa_w["aa_dtb"][:]), dtbb)
            alog, alogb = sb("alog", [128, 16])
            self.dma(SY, lambda e: e.dma_start(out=alog[:],
                                               in_=self.aa_w["aa_alog"][:]), alogb)

            bg, bgb = sb("bg", [128, NT * 32])
            beta, betab = sb("beta", [128, NT * 16])
            gbuf, gbufb = sb("g", [128, NT * 16])
            lnbta, lnbtab = sb("lnb", [128, NT * 16])
            Gc, Gcb = sb("G", [128, NT * 16])
            negG, negGb = sb("negG", [128, NT * 16])
            glb, glbb = sb("glb", [128, NT * 16])
            eglb, eglbb = sb("eglb", [128, NT * 16])
            nexpa, nexpab = sb("nexpa", [128, 16])
            spsc, spscb = sb("spsc", [128, 48])

            def softplus(dst_ap, dstb, src_ap, srcb, neg=False):
                op = self.op
                op(self.S, lambda e: e.activation(spsc[:, 0:16], src_ap, AF.Abs),
                   reads=[srcb], writes=[spscb])
                op(self.S, lambda e: e.activation(spsc[:, 16:32], spsc[:, 0:16],
                                                  AF.Exp, scale=-1.0),
                   reads=[spscb], writes=[spscb])
                op(self.S, lambda e: e.activation(spsc[:, 16:32], spsc[:, 16:32],
                                                  AF.Ln, bias=1.0),
                   reads=[spscb], writes=[spscb])
                op(self.S, lambda e, sc=-1.0 if neg else 1.0: e.activation(
                    spsc[:, 32:48], src_ap, AF.Relu, scale=sc),
                   reads=[srcb], writes=[spscb])
                op(self.V, lambda e: e.tensor_add(dst_ap, spsc[:, 16:32],
                                                  spsc[:, 32:48]),
                   reads=[spscb], writes=[dstb])
            egcrow, egcrowb = sb("egcrow", [1, NT * 16])
            egc, egcb = sb("egc", [128, NT * 16])
            tril_i = np.triu(np.ones((128, 128), np.float32))
            ctril, ctrilb = self.input_sbuf("c_tril", tril_i, stack=ph)
            sel16 = np.zeros((16, 16 * 128), np.float32)
            for h in range(16):
                sel16[h, h * 128:(h + 1) * 128] = 1.0
            csel, cselb = self.input_sbuf("c_sel16", sel16, stack=ph)
            ones1f, ones1fb = self.input_sbuf("c_ones1f",
                                              np.ones((1, 128), np.float32), stack=ph)
            cmst, cmstb = self.input_sbuf(
                "c_mst", 1e30 * np.triu(np.ones((128, 128), np.float32), 0), stack=ph)
            cminc, cmincb = self.input_sbuf(
                "c_minc", -1e30 * np.tril(np.ones((128, 128), np.float32), -1),
                stack=ph)
            sel2 = np.zeros((128, 2), np.float32)
            sel2[:64, 0] = 1.0
            sel2[64:, 1] = 1.0
            csel2, csel2b = self.input_sbuf("c_sel2", sel2, stack=ph)
            ex2 = np.zeros((2, 128), np.float32)
            ex2[0, :64] = 1.0
            ex2[1, 64:] = 1.0
            cex2, cex2b = self.input_sbuf("c_ex2", ex2.astype(BF), stack=ph)
            cex2q, cex2qb = self.input_sbuf(
                "c_ex2q", (ex2 * (DK ** -0.5)).astype(BF), stack=ph)
            cpar, cparb = self.cpar, self.cparb

            Sst, Sstb = sb("Sst", [128, 2 * 8 * 64], BF16)
            NI = SCW * 16
            dO0, dO0b = sb("dO0", [128, NI * 64], BF16)
            dWY, dWYb = sb("dWY", [128, (NI // 2) * 128], BF16)
            dQg, dQgb = sb("dQg", [128, (NI // 2) * 128], BF16)
            dMn, dMnb = sb("dMn", [128, (NI // 2) * 64], BF16)
            dIc, dIcb = sb("dIc", [128, (NI // 2) * 64], BF16)
            spl = {}
            splb = {}
            for n, w in (("O0", NI * 64), ("WY", (NI // 2) * 128),
                         ("Qg", (NI // 2) * 128), ("Mn", (NI // 2) * 64),
                         ("Ic", (NI // 2) * 64)):
                spl[n] = nc.dram_tensor(tag + "sp" + n, [128, NSC * w], BF16,
                                        kind="Internal").ap()
                splb[n] = Buf("sp" + n)
            dlocal = {"O0": (dO0, dO0b, NI * 64), "WY": (dWY, dWYb, (NI // 2) * 128),
                      "Qg": (dQg, dQgb, (NI // 2) * 128),
                      "Mn": (dMn, dMnb, (NI // 2) * 64),
                      "Ic": (dIc, dIcb, (NI // 2) * 64)}

            def pk(base, hv, w):
                return (64 * (hv % 2), base * w + 0)

            op(S, lambda e: e.activation(nexpa[:], alog[:], AF.Exp),
               reads=[alogb], writes=[nexpab])
            op(V, lambda e: e.tensor_scalar_mul(nexpa[:], nexpa[:], -1.0),
               reads=[nexpab], writes=[nexpab])
            for i in range(NT):
                tl = i + 1
                for k in range(8):
                    op(T, lambda e, k=k, tl=tl: e.matmul(
                        ps[6][:, 0:32], self.lnT_ap(k, tl),
                        wba[:, k * 32:(k + 1) * 32],
                        start=(k == 0), stop=(k == 7)),
                        reads=[self.lnb, wbab], writes=[pb[6]])
                op(V, lambda e, i=i: e.tensor_copy(bg[:, i * 32:(i + 1) * 32],
                                                   ps[6][:, 0:32]),
                   reads=[pb[6]], writes=[bgb])
                op(S, lambda e, i=i: e.activation(
                    beta[:, i * 16:(i + 1) * 16], bg[:, i * 32:i * 32 + 16],
                    AF.Sigmoid), reads=[bgb], writes=[betab])
                softplus(lnbta[:, i * 16:(i + 1) * 16], lnbtab,
                         bg[:, i * 32:i * 32 + 16], bgb, neg=True)
                op(V, lambda e, i=i: e.tensor_scalar_mul(
                    lnbta[:, i * 16:(i + 1) * 16], lnbta[:, i * 16:(i + 1) * 16],
                    -1.0), reads=[lnbtab], writes=[lnbtab])
                op(V, lambda e, i=i: e.tensor_add(
                    gbuf[:, i * 16:(i + 1) * 16], bg[:, i * 32 + 16:i * 32 + 32],
                    dtb[:]), reads=[bgb, dtbb], writes=[gbufb])
                softplus(gbuf[:, i * 16:(i + 1) * 16], gbufb,
                         gbuf[:, i * 16:(i + 1) * 16], gbufb, neg=False)
                op(V, lambda e, i=i: e.tensor_mul(
                    gbuf[:, i * 16:(i + 1) * 16], gbuf[:, i * 16:(i + 1) * 16],
                    nexpa[:]), reads=[gbufb, nexpab], writes=[gbufb])
                op(T, lambda e, i=i: e.matmul(
                    ps[6][:, 0:16], ctril[:], gbuf[:, i * 16:(i + 1) * 16],
                    start=True, stop=True),
                    reads=[gbufb, ctrilb], writes=[pb[6]])
                op(V, lambda e, i=i: e.tensor_copy(Gc[:, i * 16:(i + 1) * 16],
                                                   ps[6][:, 0:16]),
                   reads=[pb[6]], writes=[Gcb])
            op(V, lambda e: e.tensor_scalar_mul(negG[:], Gc[:], -1.0),
               reads=[Gcb], writes=[negGb])
            op(V, lambda e: e.tensor_add(glb[:], Gc[:], lnbta[:]),
               reads=[Gcb, lnbtab], writes=[glbb])
            op(S, lambda e: e.activation(eglb[:], glb[:], AF.Exp),
               reads=[glbb], writes=[eglbb])
            self.dma(SY, lambda e: e.dma_start(out=egcrow[:],
                                               in_=Gc[127:128, :]), egcrowb,
                     reads=[Gcb])
            op(S, lambda e: e.activation(egcrow[:], egcrow[:], AF.Exp),
               reads=[egcrowb], writes=[egcrowb])
            for i in range(NT):
                op(T, lambda e, i=i: e.matmul(
                    ps[6][:, 0:16], ones1f[:],
                    egcrow[:, i * 16:(i + 1) * 16], start=True, stop=True),
                    reads=[egcrowb, ones1fb], writes=[pb[6]])
                op(V, lambda e, i=i: e.tensor_copy(egc[:, i * 16:(i + 1) * 16],
                                                   ps[6][:, 0:16]),
                   reads=[pb[6]], writes=[egcb])
            op(V, lambda e: e.memset(Sst[:], 0.0), writes=[Sstb])
            self.barrier()

            with contextlib.ExitStack() as pa:
                TW = SCW * 128
                pre, preb = sb("pre", [128, 16 * (TW + 3)], BF16, st=pa)
                qkv, qkvb = pre, preb
                acc, accb = sb("acc", [128, TW], st=pa)
                tmp7, tmp7b = sb("tmp7", [128, 16 * 8], BF16, st=pa)
                ssq, ssqb = sb("ssq", [2, 8 * SCW * 128], BF16, st=pa)
                ktok, ktokb = sb("ktok", [128, SCW * 512], BF16, st=pa)
                vtok, vtokb = sb("vtok", [128, SCW * 1024], BF16, st=pa)
                GT, GTb = sb("GT", [16, 128], st=pa)
                eGT, eGTb = sb("eGT", [16, 128], st=pa)
                arg1, arg1b = sb("arg1", [128, 128], st=pa)
                prex, prexb = sb("prex", [128, 128], st=pa)
                Anb_, Anbb = sb("An", [128, 128], BF16, st=pa)
                Atb_, Atbb = sb("At", [128, 128], BF16, st=pa)
                Wtb_, Wtbb = sb("Wt", [128, 128], BF16, st=pa)
                wpre, wpreb = sb("wpre", [128, 128], st=pa)
                kbar, kbarb = sb("kbar", [128, 64], BF16, st=pa)
                rhs, rhsb = sb("rhs", [128, 128], BF16, st=pa)
                X, Xb = sb("X", [128, 128], BF16, st=pa)
                pw = {}
                pwb = {}
                for pname in ("t2", "t4", "t8", "t16", "t32", "t64", "n_a", "n_b"):
                    pw[pname], pwb[pname] = sb("P" + pname, [128, 128], BF16, st=pa)
                for m in range(16):
                    op(V, lambda e, m=m: e.memset(
                        pre[:, m * (TW + 3):m * (TW + 3) + 3], 0.0), writes=[preb])

                for sc in range(NSC):
                    c0 = sc * SCW
                    nch = min(SCW, NT - c0)
                    tw = nch * 128
                    PW = TW + 3
                    for m in range(16):
                        for k in range(8):
                            op(T, lambda e, m=m, k=k, tw=tw, c0=c0: e.matmul(
                                ps[6][:, 0:tw],
                                wslot[:, k * 2048 + m * 128:k * 2048 + (m + 1) * 128],
                                self.lnT_cols(k, 128 + c0 * 128,
                                              128 + c0 * 128 + tw),
                                start=(k == 0), stop=(k == 7)),
                                reads=[self.lnb, wslotb], writes=[pb[6]])
                        op(V, lambda e, m=m, tw=tw, PW=PW: e.tensor_copy(
                            pre[:, m * PW + 3:m * PW + 3 + tw], ps[6][:, 0:tw]),
                           reads=[pb[6]], writes=[preb])
                    for m in range(16):
                        for k in range(4):
                            fn = (lambda e, m=m, k=k, tw=tw, PW=PW:
                                  e.tensor_scalar_mul(
                                      acc[:, 3:tw],
                                      pre[:, m * PW + 3 + k:m * PW + k + tw],
                                      cvw[:, m * 4 + k:m * 4 + k + 1]))                                 if k == 0 else                                 (lambda e, m=m, k=k, tw=tw, PW=PW:
                                 e.scalar_tensor_tensor(
                                     acc[:, 3:tw],
                                     pre[:, m * PW + 3 + k:m * PW + k + tw],
                                     cvw[:, m * 4 + k:m * 4 + k + 1], acc[:, 3:tw],
                                     op0=OP.mult, op1=OP.add))
                            op(V, fn, reads=[preb, cvwb, accb], writes=[accb])
                        op(V, lambda e, m=m, PW=PW: e.tensor_copy(
                            tmp7[:, m * 8:m * 8 + 3], pre[:, m * PW:m * PW + 3]),
                            reads=[preb], writes=[tmp7b])
                        op(V, lambda e, m=m, PW=PW: e.tensor_copy(
                            tmp7[:, m * 8 + 3:m * 8 + 6],
                            pre[:, m * PW + 3:m * PW + 6]),
                            reads=[preb], writes=[tmp7b])
                        for k in range(4):
                            fn = (lambda e, m=m, k=k: e.tensor_scalar_mul(
                                acc[:, 0:3], tmp7[:, m * 8 + k:m * 8 + k + 3],
                                cvw[:, m * 4 + k:m * 4 + k + 1])) if k == 0 else                                 (lambda e, m=m, k=k: e.scalar_tensor_tensor(
                                    acc[:, 0:3], tmp7[:, m * 8 + k:m * 8 + k + 3],
                                    cvw[:, m * 4 + k:m * 4 + k + 1], acc[:, 0:3],
                                    op0=OP.mult, op1=OP.add))
                            op(V, fn, reads=[tmp7b, cvwb, accb], writes=[accb])
                        op(V, lambda e, m=m, tw=tw, PW=PW: e.tensor_copy(
                            pre[:, m * PW:m * PW + 3],
                            pre[:, m * PW + tw:m * PW + tw + 3]),
                            reads=[preb], writes=[preb])
                        op(S, lambda e, m=m, tw=tw: e.activation(
                            pre[:, m * PW + 3:m * PW + 3 + tw], acc[:, 0:tw], AF.Silu),
                            reads=[accb], writes=[qkvb])
                    for m in range(8):
                        op(S, lambda e, m=m, tw=tw: e.activation(
                            acc[:, 0:tw], pre[:, m * PW + 3:m * PW + 3 + tw], AF.Square),
                            reads=[qkvb], writes=[accb])
                        for ch in range(nch):
                            op(T, lambda e, ch=ch: e.matmul(
                                ps[6][:2, 0:128], csel2[:],
                                acc[:, ch * 128:(ch + 1) * 128],
                                start=True, stop=True),
                                reads=[accb, csel2b], writes=[pb[6]])
                            op(V, lambda e, m=m, ch=ch: e.tensor_copy(
                                ssq[:, (m * SCW + ch) * 128:
                                    (m * SCW + ch) * 128 + 128],
                                ps[6][:2, 0:128]),
                               reads=[pb[6]], writes=[ssqb])
                    nss = 8 * tw
                    op(S, lambda e, nss=nss: e.activation(
                        ssq[:, 0:nss], ssq[:, 0:nss], AF.Ln,
                        bias=self.cst[0:2, 0:1]),
                       reads=[ssqb, self.cstb], writes=[ssqb])
                    op(S, lambda e, nss=nss: e.activation(
                        ssq[:, 0:nss], ssq[:, 0:nss], AF.Exp, scale=-0.5),
                       reads=[ssqb], writes=[ssqb])
                    for m in range(8):
                        exc, excb = (cex2q, cex2qb) if m < 4 else (cex2, cex2b)
                        for ch in range(nch):
                            op(T, lambda e, m=m, ch=ch, exc=exc: e.matmul(
                                ps[6][:, 0:128], exc[:],
                                ssq[:, (m * SCW + ch) * 128:
                                    (m * SCW + ch) * 128 + 128],
                                start=True, stop=True),
                                reads=[ssqb, excb], writes=[pb[6]])
                            op(V, lambda e, m=m, ch=ch: e.tensor_mul(
                                pre[:, m * PW + 3 + ch * 128:m * PW + 3 + (ch + 1) * 128],
                                pre[:, m * PW + 3 + ch * 128:m * PW + 3 + (ch + 1) * 128],
                                ps[6][:, 0:128]),
                               reads=[pb[6], qkvb], writes=[qkvb])
                    for ch in range(nch):
                        for m in range(4, 16):
                            dst, dstb, doff = (ktok, ktokb,
                                               ch * 512 + (m - 4) * 128)                                 if m < 8 else (vtok, vtokb,
                                               ch * 1024 + (m - 8) * 128)
                            op(T, lambda e, m=m, ch=ch: e.transpose(
                                self.pt8[:, 0:128],
                                pre[:, m * PW + 3 + ch * 128:m * PW + 3 + (ch + 1) * 128],
                                self.identb[:]),
                                reads=[qkvb, self.identb_b], writes=[self.pt8b])
                            op(V, lambda e, dst=dst, doff=doff: e.tensor_copy(
                                dst[:, doff:doff + 128], self.pt8[:, 0:128]),
                               reads=[self.pt8b], writes=[dstb])
                    for ch in range(nch):
                        ci = c0 + ch
                        op(T, lambda e, ci=ci: e.transpose(
                            ps[6][:16, 0:128], Gc[:, ci * 16:(ci + 1) * 16],
                            self.identf[:]),
                            reads=[Gcb, self.identf_b], writes=[pb[6]])
                        op(V, lambda e: e.tensor_copy(GT[:], ps[6][:16, 0:128]),
                           reads=[pb[6]], writes=[GTb])
                        op(S, lambda e: e.activation(eGT[:], GT[:], AF.Exp),
                           reads=[GTb], writes=[eGTb])
                        for hv in range(16):
                            hk = hv // 2
                            kq_m = 4 + hk // 2
                            q_m = hk // 2
                            hko = 64 * (hk % 2)
                            kd = pre[hko:hko + 64,
                                     kq_m * PW + 3 + ch * 128:
                                     kq_m * PW + 3 + (ch + 1) * 128]
                            qd = pre[hko:hko + 64,
                                     q_m * PW + 3 + ch * 128:
                                     q_m * PW + 3 + (ch + 1) * 128]
                            gof = ci * 16 + hv
                            gc1 = lambda b_, o=gof: b_[:, o:o + 1]
                            op(T, lambda e, kd=kd: e.matmul(
                                ps[0][:, 0:128], kd, kd, start=True, stop=True),
                                reads=[qkvb], writes=[pb[0]])
                            op(T, lambda e, kd=kd, qd=qd: e.matmul(
                                ps[1][:, 0:128], kd, qd, start=True, stop=True),
                                reads=[qkvb], writes=[pb[1]])
                            op(T, lambda e, hv=hv: e.matmul(
                                ps[2][:, 0:128], csel[:, hv * 128:(hv + 1) * 128],
                                GT[:], start=True, stop=True),
                                reads=[GTb, cselb], writes=[pb[2]])
                            op(V, lambda e: e.tensor_add(arg1[:], ps[2][:, 0:128],
                                                         cmst[:]),
                               reads=[pb[2], cmstb], writes=[arg1b])
                            op(S, lambda e, gof=gof: e.activation(
                                prex[:], arg1[:], AF.Exp,
                                bias=glb[:, gof:gof + 1], scale=-1.0),
                                reads=[arg1b, glbb], writes=[prexb])
                            op(V, lambda e: e.tensor_mul(Anb_[:], prex[:],
                                                         ps[0][:, 0:128]),
                               reads=[prexb, pb[0]], writes=[Anbb])
                            op(V, lambda e: e.tensor_add(arg1[:], ps[2][:, 0:128],
                                                         cminc[:]),
                               reads=[pb[2], cmincb], writes=[arg1b])
                            op(S, lambda e, gof=gof: e.activation(
                                wpre[:], arg1[:], AF.Exp,
                                bias=negG[:, gof:gof + 1], scale=1.0),
                                reads=[arg1b, negGb], writes=[wpreb])
                            op(V, lambda e: e.tensor_mul(Wtb_[:], wpre[:],
                                                         ps[1][:, 0:128]),
                               reads=[wpreb, pb[1]], writes=[Wtbb])
                            op(T, lambda e: e.transpose(self.pt8[:, 0:128],
                                                        Anb_[:], self.identb[:]),
                               reads=[Anbb, self.identb_b], writes=[self.pt8b])
                            op(V, lambda e: e.tensor_copy(Atb_[:],
                                                          self.pt8[:, 0:128]),
                               reads=[self.pt8b], writes=[Atbb])
                            op(V, lambda e, ch=ch, hk=hk: e.tensor_scalar_mul(
                                kbar[:], ktok[:, ch * 512 + hk * 64:
                                              ch * 512 + (hk + 1) * 64],
                                wpre[:, 127:128]),
                               reads=[ktokb, wpreb], writes=[kbarb])
                            op(V, lambda e, ch=ch, hv=hv, gof=gof:
                               e.tensor_scalar_mul(
                                rhs[:, 0:64], vtok[:, ch * 1024 + hv * 64:
                                                   ch * 1024 + (hv + 1) * 64],
                                beta[:, gof:gof + 1]),
                               reads=[vtokb, betab], writes=[rhsb])
                            op(V, lambda e, ch=ch, hk=hk, gof=gof:
                               e.tensor_scalar_mul(
                                rhs[:, 64:128], ktok[:, ch * 512 + hk * 64:
                                                     ch * 512 + (hk + 1) * 64],
                                eglb[:, gof:gof + 1]),
                               reads=[ktokb, eglbb], writes=[rhsb])
                            op(V, lambda e: e.tensor_copy(X[:], rhs[:]),
                               reads=[rhsb], writes=[Xb])
                            cn, ct, cnt_, ctt_ = Anbb, Atbb, Anb_, Atb_
                            for r, pnm in enumerate(("t2", "t4", "t8", "t16",
                                                     "t32", "t64")):
                                op(T, lambda e, cnt_=cnt_, ctt_=ctt_: e.matmul(
                                    ps[4][:, 0:128], cnt_[:], ctt_[:],
                                    start=True, stop=True),
                                    reads=[cn, ct], writes=[pb[4]])
                                op(V, lambda e, pnm=pnm: e.tensor_copy(
                                    pw[pnm][:], ps[4][:, 0:128]),
                                   reads=[pb[4]], writes=[pwb[pnm]])
                                if pnm != "t64":
                                    nb = "n_a" if r % 2 == 0 else "n_b"
                                    op(T, lambda e, cnt_=cnt_, ctt_=ctt_: e.matmul(
                                        ps[5][:, 0:128], ctt_[:], cnt_[:],
                                        start=True, stop=True),
                                        reads=[cn, ct], writes=[pb[5]])
                                    op(V, lambda e, nb=nb: e.tensor_copy(
                                        pw[nb][:], ps[5][:, 0:128]),
                                       reads=[pb[5]], writes=[pwb[nb]])
                                    cn, ct = pwb[nb], pwb[pnm]
                                    cnt_, ctt_ = pw[nb], pw[pnm]
                            for pnm in ("t64", "t32", "t16", "t8", "t4", "t2"):
                                op(T, lambda e, pnm=pnm: e.matmul(
                                    ps[4][:, 0:128], pw[pnm][:], X[:],
                                    start=True, stop=True),
                                    reads=[pwb[pnm], Xb], writes=[pb[4]])
                                op(V, lambda e: e.tensor_add(X[:], X[:],
                                                             ps[4][:, 0:128]),
                                   reads=[pb[4], Xb], writes=[Xb])
                            op(T, lambda e: e.matmul(ps[4][:, 0:128], Atb_[:],
                                                     X[:], start=True, stop=True),
                               reads=[Atbb, Xb], writes=[pb[4]])
                            op(V, lambda e: e.tensor_sub(X[:], X[:],
                                                         ps[4][:, 0:128]),
                               reads=[pb[4], Xb], writes=[Xb])
                            po = 64 * ((hv // 2) % 2)
                            ii = ch * 16 + hv
                            ip = ch * 8 + (hv // 4) * 2 + hv % 2
                            op(T, lambda e: e.matmul(ps[3][:, 0:64], Wtb_[:],
                                                     X[:, 0:64],
                                                     start=True, stop=True),
                               reads=[Wtbb, Xb], writes=[pb[3]])
                            op(V, lambda e, ii=ii: e.tensor_copy(
                                dO0[:, ii * 64:(ii + 1) * 64], ps[3][:, 0:64]),
                               reads=[pb[3]], writes=[dO0b])
                            op(T, lambda e, po=po: e.matmul(
                                ps[4][po:po + 64, 0:128], X[:, 64:128], Wtb_[:],
                                start=True, stop=True),
                               reads=[Wtbb, Xb], writes=[pb[4]])
                            op(V, lambda e, po=po, ip=ip: e.tensor_scalar_mul(
                                dWY[po:po + 64, ip * 128:(ip + 1) * 128],
                                ps[4][po:po + 64, 0:128], -1.0),
                               reads=[pb[4]], writes=[dWYb])
                            op(T, lambda e, hv=hv, po=po: e.matmul(
                                ps[3][po:po + 64, 0:128],
                                csel[:, hv * 128:hv * 128 + 64], eGT[:],
                                start=True, stop=True),
                                reads=[eGTb, cselb], writes=[pb[3]])
                            op(V, lambda e, po=po, ip=ip, qd=qd: e.tensor_mul(
                                dQg[po:po + 64, ip * 128:(ip + 1) * 128], qd,
                                ps[3][po:po + 64, 0:128]),
                               reads=[pb[3], qkvb], writes=[dQgb])
                            op(T, lambda e, po=po: e.matmul(
                                ps[4][po:po + 64, 0:64], kbar[:], X[:, 64:128],
                                start=True, stop=True),
                               reads=[kbarb, Xb], writes=[pb[4]])
                            op(V, lambda e, po=po, ip=ip: e.tensor_scalar_mul(
                                dMn[po:po + 64, ip * 64:(ip + 1) * 64],
                                ps[4][po:po + 64, 0:64], -1.0),
                               reads=[pb[4]], writes=[dMnb])
                            op(T, lambda e, po=po: e.matmul(
                                ps[3][po:po + 64, 0:64], kbar[:], X[:, 0:64],
                                start=True, stop=True),
                               reads=[kbarb, Xb], writes=[pb[3]])
                            op(V, lambda e, po=po, ip=ip: e.tensor_copy(
                                dIc[po:po + 64, ip * 64:(ip + 1) * 64],
                                ps[3][po:po + 64, 0:64]),
                               reads=[pb[3]], writes=[dIcb])
                    for n in ("O0", "WY", "Qg", "Mn", "Ic"):
                        dloc, dlb, w = dlocal[n]
                        self.dma(SY, lambda e, n=n, sc=sc, dloc=dloc, w=w:
                                 e.dma_start(out=spl[n][:, sc * w:(sc + 1) * w],
                                             in_=dloc[:, 0:w]),
                                 splb[n], reads=[dlb])
                    for ch in range(nch):
                        ci = c0 + ch
                        for hv in range(16):
                            po = 64 * ((hv // 2) % 2)
                            ip = ch * 8 + (hv // 4) * 2 + hv % 2
                            gof = ci * 16 + hv
                            sof = ((hv // 4) * 2 + hv % 2) * 64
                            op(T, lambda e, po=po, ip=ip, sof=sof: e.matmul(
                                ps[5][po:po + 64, 0:64],
                                dMn[po:po + 64, ip * 64:(ip + 1) * 64],
                                Sst[po:po + 64, sof:sof + 64],
                                start=True, stop=False),
                                reads=[dMnb, Sstb], writes=[pb[5]])
                            op(T, lambda e, po=po, ip=ip: e.matmul(
                                ps[5][po:po + 64, 0:64],
                                self.identb[po:po + 64, po:po + 64],
                                dIc[po:po + 64, ip * 64:(ip + 1) * 64],
                                start=False, stop=True),
                                reads=[dIcb, self.identb_b], writes=[pb[5]])
                            op(V, lambda e, po=po, gof=gof, sof=sof:
                               e.scalar_tensor_tensor(
                                Sst[po:po + 64, sof:sof + 64],
                                Sst[po:po + 64, sof:sof + 64],
                                egc[po:po + 64, gof:gof + 1],
                                ps[5][po:po + 64, 0:64], op0=OP.mult, op1=OP.add),
                               reads=[pb[5], egcb, Sstb], writes=[Sstb])
                self.barrier()

            recvS, recvSb = sb("recvS", [128, 8 * 64], BF16)
            if use_cc:
                cin = nc.dram_tensor(tag + "ccin", [128 * 512], BF16,
                                     kind="Internal").ap()
                cout = nc.dram_tensor(tag + "ccout", [2 * 128 * 512], BF16,
                                      kind="Internal").ap()
                cinb, coutb = Buf("ccin"), Buf("ccout")
                self.dma(SY, lambda e: e.dma_start(
                    out=cin.rearrange("(p f) -> p f", p=128),
                    in_=Sst[:, 0:512]), cinb, reads=[Sstb])
                op(G, lambda e: e.collective_compute(
                    "AllGather", OP.bypass, ins=[cin[:]], outs=[cout[:]],
                    replica_groups=[[2 * i, 2 * i + 1]
                                    for i in range(self.n_cores // 2)]),
                   reads=[cinb], writes=[coutb])
                self.dma(SY, lambda e: e.dma_start(
                    out=recvS[:, 0:512], in_=cout[0:128 * 512].rearrange(
                        "(p f) -> p f", p=128)), recvSb, reads=[coutb])
                op(V, lambda e: e.tensor_scalar_mul(
                    recvS[:, 0:512], recvS[:, 0:512], cpar[:, 0:1]),
                   reads=[recvSb, cparb], writes=[recvSb])
            else:
                op(V, lambda e: e.memset(recvS[:], 0.0), writes=[recvSb])
            op(V, lambda e: e.tensor_copy(Sst[:, 0:512], recvS[:, 0:512]),
               reads=[recvSb], writes=[Sstb])

            self.dma(GD, lambda e: e.dma_start(
                out=wslot[:, 0:8 * 1024], in_=self.aa_w["aa_wz"][:]), wslotb)
            self.dma(GD, lambda e: e.dma_start(
                out=wslot[:, 8 * 1024:16 * 1024], in_=self.aa_w["aa_wout"][:]),
                wslotb)
            nrmw, nrmwb = sb("nrmw", [128, 1024])
            self.dma(SY, lambda e: e.dma_start(out=nrmw[:],
                                               in_=self.aa_w["aa_normw"][:]), nrmwb)
            with contextlib.ExitStack() as pbx:
                o_t, o_tb = sb("o_t", [128, 1024], st=pbx)
                zb_, zbb = sb("zsil", [128, 1024], st=pbx)
                on, onb = sb("on", [128, 1024], BF16, st=pbx)
                rmsb, rmsbb = sb("rms", [128, 16], st=pbx)
                oT, oTb = sb("oT", [128, 8 * 128], BF16, st=pbx)
                for sc in range(NSC):
                    c0 = sc * SCW
                    nch = min(SCW, NT - c0)
                    for n in ("O0", "WY", "Qg", "Mn", "Ic"):
                        dloc, dlb, w = dlocal[n]
                        self.dma(SY, lambda e, n=n, sc=sc, dloc=dloc, w=w:
                                 e.dma_start(out=dloc[:, 0:w],
                                             in_=spl[n][:, sc * w:(sc + 1) * w]),
                                 dlb, reads=[splb[n]])
                    for ch in range(nch):
                        ci = c0 + ch
                        pcur, pnxt = ci % 2, 1 - (ci % 2)
                        for half in range(2):
                            pj = half
                            for hh in range(8):
                                hv = half * 8 + hh
                                po = 64 * ((hv // 2) % 2)
                                ip = ch * 8 + (hv // 4) * 2 + hv % 2
                                ii = ch * 16 + hv
                                sof = pcur * 512 + ((hv // 4) * 2 + hv % 2) * 64
                                op(T, lambda e, po=po, ip=ip, hh=hh, pj=pj,
                                   sof=sof: e.matmul(
                                    ps[pj][:, hh * 64:(hh + 1) * 64],
                                    dQg[po:po + 64, ip * 128:(ip + 1) * 128],
                                    Sst[po:po + 64, sof:sof + 64],
                                    start=True, stop=False),
                                   reads=[dQgb, Sstb], writes=[pb[pj]])
                                op(T, lambda e, po=po, ip=ip, hh=hh, pj=pj,
                                   sof=sof: e.matmul(
                                    ps[pj][:, hh * 64:(hh + 1) * 64],
                                    dWY[po:po + 64, ip * 128:(ip + 1) * 128],
                                    Sst[po:po + 64, sof:sof + 64],
                                    start=False, stop=True),
                                   reads=[dWYb, Sstb], writes=[pb[pj]])
                            op(V, lambda e, half=half, ch=ch, pj=pj: e.tensor_add(
                                o_t[:, half * 512:(half + 1) * 512],
                                ps[pj][:, 0:512],
                                dO0[:, (ch * 16 + half * 8) * 64:
                                    (ch * 16 + half * 8 + 8) * 64]),
                               reads=[pb[pj], dO0b], writes=[o_tb])
                        for hv in range(16):
                            po = 64 * ((hv // 2) % 2)
                            ip = ch * 8 + (hv // 4) * 2 + hv % 2
                            gof = ci * 16 + hv
                            scur = pcur * 512 + ((hv // 4) * 2 + hv % 2) * 64
                            snxt = pnxt * 512 + ((hv // 4) * 2 + hv % 2) * 64
                            op(T, lambda e, po=po, ip=ip, scur=scur: e.matmul(
                                ps[5][po:po + 64, 0:64],
                                dMn[po:po + 64, ip * 64:(ip + 1) * 64],
                                Sst[po:po + 64, scur:scur + 64],
                                start=True, stop=False),
                                reads=[dMnb, Sstb], writes=[pb[5]])
                            op(T, lambda e, po=po, ip=ip: e.matmul(
                                ps[5][po:po + 64, 0:64],
                                self.identb[po:po + 64, po:po + 64],
                                dIc[po:po + 64, ip * 64:(ip + 1) * 64],
                                start=False, stop=True),
                                reads=[dIcb, self.identb_b], writes=[pb[5]])
                            op(V, lambda e, po=po, gof=gof, scur=scur, snxt=snxt:
                               e.scalar_tensor_tensor(
                                Sst[po:po + 64, snxt:snxt + 64],
                                Sst[po:po + 64, scur:scur + 64],
                                egc[po:po + 64, gof:gof + 1],
                                ps[5][po:po + 64, 0:64], op0=OP.mult, op1=OP.add),
                               reads=[pb[5], egcb, Sstb], writes=[Sstb])
                        tl = ci + 1
                        op(S, lambda e: e.activation(on[:], o_t[:], AF.Square),
                           reads=[o_tb], writes=[onb])
                        for hv in range(16):
                            op(V, lambda e, hv=hv: e.tensor_reduce(
                                rmsb[:, hv:hv + 1], on[:, hv * 64:(hv + 1) * 64],
                                AX, OP.add),
                               reads=[onb], writes=[rmsbb])
                        op(V, lambda e: e.tensor_scalar(
                            rmsb[:], rmsb[:], 1.0 / 64.0, 1e-6,
                            op0=OP.mult, op1=OP.add),
                           reads=[rmsbb], writes=[rmsbb])
                        op(V, lambda e: e.reciprocal(rmsb[:], rmsb[:]),
                           reads=[rmsbb], writes=[rmsbb])
                        op(S, lambda e: e.activation(rmsb[:], rmsb[:], AF.Sqrt),
                           reads=[rmsbb], writes=[rmsbb])
                        for dc in range(2):
                            for k in range(8):
                                op(T, lambda e, k=k, dc=dc, tl=tl: e.matmul(
                                    ps[2 + dc][:, 0:512], self.lnT_ap(k, tl),
                                    wslot[:, k * 1024 + dc * 512:
                                          k * 1024 + (dc + 1) * 512],
                                    start=(k == 0), stop=(k == 7)),
                                    reads=[self.lnb, wslotb], writes=[pb[2 + dc]])
                            op(S, lambda e, dc=dc: e.activation(
                                zb_[:, dc * 512:(dc + 1) * 512],
                                ps[2 + dc][:, 0:512], AF.Silu),
                               reads=[pb[2 + dc]], writes=[zbb])
                        op(V, lambda e: e.tensor_mul(zb_[:], zb_[:], nrmw[:]),
                           reads=[zbb, nrmwb], writes=[zbb])
                        for hv in range(16):
                            op(V, lambda e, hv=hv: e.tensor_scalar_mul(
                                o_t[:, hv * 64:(hv + 1) * 64],
                                o_t[:, hv * 64:(hv + 1) * 64],
                                rmsb[:, hv:hv + 1]),
                               reads=[rmsbb, o_tb], writes=[o_tb])
                        op(V, lambda e: e.tensor_mul(on[:], o_t[:], zb_[:]),
                           reads=[o_tb, zbb], writes=[onb])
                        for half in range(2):
                            for kk in range(4):
                                k = half * 4 + kk
                                op(T, lambda e, k=k, kk=kk, half=half: e.transpose(
                                    self.pt8[:, half * 512 + kk * 128:
                                             half * 512 + (kk + 1) * 128],
                                    on[:, k * 128:(k + 1) * 128], self.identb[:]),
                                    reads=[onb, self.identb_b],
                                    writes=[self.pt8b])
                            for kk in range(4):
                                k = half * 4 + kk
                                op(S, lambda e, k=k, kk=kk, half=half: e.copy(
                                    oT[:, k * 128:(k + 1) * 128],
                                    self.pt8[:, half * 512 + kk * 128:
                                             half * 512 + (kk + 1) * 128]),
                                   reads=[self.pt8b], writes=[oTb])
                        for m in range(8):
                            for k in range(8):
                                op(T, lambda e, m=m, k=k: e.matmul(
                                    ps[6][:, 0:128],
                                    wslot[:, 8192 + k * 1024 + m * 128:
                                          8192 + k * 1024 + (m + 1) * 128],
                                    oT[:, k * 128:(k + 1) * 128],
                                    start=(k == 0), stop=(k == 7)),
                                    reads=[oTb, wslotb], writes=[pb[6]])
                            op(V, lambda e, m=m: e.tensor_copy(
                                on[:, m * 128:(m + 1) * 128], ps[6][:, 0:128]),
                               reads=[pb[6]], writes=[onb])
                        for half in range(2):
                            for kk in range(4):
                                m = half * 4 + kk
                                op(T, lambda e, m=m, kk=kk, half=half: e.transpose(
                                    self.pt8[:, half * 512 + kk * 128:
                                             half * 512 + (kk + 1) * 128],
                                    on[:, m * 128:(m + 1) * 128], self.identb[:]),
                                    reads=[onb, self.identb_b],
                                    writes=[self.pt8b])
                            op(V, lambda e, half=half, tl=tl: e.tensor_add(
                                self.x_ap(tl, half * 512, (half + 1) * 512),
                                self.x_ap(tl, half * 512, (half + 1) * 512),
                                self.pt8[:, half * 512:half * 512 + 512]),
                               reads=[self.pt8b], writes=[self.xb])
        self.barrier()

    def prep_attn_b_weights(self, p, base):
        g = lambda n: np.asarray(p[base + n], np.float32)
        self.inputs["ab_mux"] = np.ascontiguousarray(g("mu_x").reshape(8, 128).T)
        A = g("lora_A_w")
        self.inputs["ab_lA"] = tile_w(np.concatenate([A[i] for i in range(6)], 1))
        Ab = g("lora_A_b")
        lab = np.zeros((64, 3), np.float32)
        for j in range(3):
            lab[0:32, j] = Ab[2 * j]
            lab[32:64, j] = Ab[2 * j + 1]
        self.inputs["ab_lAb"] = lab
        B = g("lora_B_w")
        lb = np.zeros((64, 3 * 1024), np.float32)
        for j in range(3):
            lb[0:32, j * 1024:(j + 1) * 1024] = B[2 * j]
            lb[32:64, j * 1024:(j + 1) * 1024] = B[2 * j + 1]
        self.inputs["ab_lB"] = lb
        lam = np.zeros((128, 6 * 8), np.float32)
        for gi in range(6):
            lam[:, gi * 8:(gi + 1) * 8] += g("lora_lambda")[gi].reshape(8, 128).T
            lam[:, gi * 8:(gi + 1) * 8] += 0
        self.inputs["ab_lam"] = lam
        lbb = np.zeros((128, 6 * 8), np.float32)
        for gi in range(6):
            lbb[:, gi * 8:(gi + 1) * 8] = g("lora_B_b")[gi].reshape(8, 128).T
        self.inputs["ab_lbb"] = lbb
        mh = g("mhd_w")
        mimg = np.zeros((128, 5 * 16 * 64), np.float32)
        for sl in range(5):
            for h in range(16):
                blk = mh[sl, h]
                mimg[0:64, (sl * 16 + h) * 64:(sl * 16 + h + 1) * 64] = blk
                mimg[64:128, (sl * 16 + h) * 64:(sl * 16 + h + 1) * 64] = blk
        self.inputs["ab_mhd"] = mimg
        for nm, w in (("ab_wA", g("wkv_A")), ("ab_wB", g("wkv_B"))):
            img = np.zeros((128, 16 * 64), np.float32)
            for h in range(16):
                img[0:64, h * 64:(h + 1) * 64] = w[h]
                img[64:128, h * 64:(h + 1) * 64] = w[h]
            self.inputs[nm] = img
        lamw = np.zeros((128, 8), np.float32)
        for j in range(8):
            lamw[0:64, j] = g("wkv_lamb")[2 * j]
            lamw[64:128, j] = g("wkv_lamb")[2 * j + 1]
        self.inputs["ab_lamb"] = lamw
        self.inputs["ab_gatew"] = tile_w(g("gate_w"))
        self.inputs["ab_outw"] = tile_w(g("out_w"))
        self.inputs["ab_outb"] = np.ascontiguousarray(g("out_b").reshape(8, 128).T)
        self.inputs["ab_gng"] = np.tile(g("gn_gamma"), (128, 1))
        self.inputs["ab_gnb"] = np.tile(g("gn_beta"), (128, 1))
        msk = np.zeros((128, 4 * 64 * 0 + 4 * 64), np.float32)
        full = np.zeros((64, 4 * 64), np.float32)
        jj, tt = np.meshgrid(np.arange(64), np.arange(64), indexing="ij")
        lv0 = ((jj // 8 == tt // 8) & (jj < tt)).astype(np.float32)
        lv1 = ((jj // 16 == tt // 16) & (jj % 16 < 8) & (tt % 16 >= 8)).astype(np.float32)
        lv2 = ((jj // 32 == tt // 32) & (jj % 32 < 16) & (tt % 32 >= 16)).astype(np.float32)
        lv3 = ((jj < 32) & (tt >= 32)).astype(np.float32)
        for i, lv in enumerate((lv0, lv1, lv2, lv3)):
            full[:, i * 64:(i + 1) * 64 * 0 + (i + 1) * 64] = lv
        self.inputs["ab_msk"] = np.ascontiguousarray(
            np.vstack([full, full]).astype(BF))

    def declare_attn_b_weights(self):
        self.ab_w = {}
        for n, sh in (("ab_mux", [128, 8]), ("ab_lA", [128, 8 * 192]),
                      ("ab_lAb", [64, 3]), ("ab_lB", [64, 3 * 1024]),
                      ("ab_lam", [128, 48]), ("ab_lbb", [128, 48]),
                      ("ab_mhd", [128, 5 * 16 * 64]), ("ab_wA", [128, 16 * 64]),
                      ("ab_wB", [128, 16 * 64]), ("ab_lamb", [128, 8]),
                      ("ab_gatew", [128, 8 * 1024]), ("ab_outw", [128, 8 * 1024]),
                      ("ab_outb", [128, 8]), ("ab_gng", [128, 1024]),
                      ("ab_gnb", [128, 1024])):
            self.ab_w[n] = self.din(n, sh, F32)
        self.ab_w["ab_msk"] = self.din("ab_msk", [128, 256], BF16)

    def emit_attn_b(self, tag, use_cc):
        T, V, S, G, SY, GD = self.T, self.V, self.S, self.G, self.SY, self.GD
        NT, NTE, nc = self.NT, self.NTE, self.nc
        ps, pb = self.psum, self.psb
        op = self.op
        with contextlib.ExitStack() as ph:
            self.pt8 = ph.enter_context(
                nc.psum_tensor(tag + "pt8", [128, 1024], BF16))
            self.pt8b = Buf(tag + "pt8")
            pt8bs = [Buf(f"{tag}pt8_{i}") for i in range(8)]
            _ptc = [0]

            def nxt():
                i = _ptc[0] % 8
                _ptc[0] += 1
                return i * 128, self.pt8b
            sb = lambda n, sh, dt=F32, st=None: self.sbuf(tag + n, sh, dt,
                                                          stack=st or ph)
            shp = {"ab_mux": ([128, 8], 0), "ab_lA": ([128, 8 * 192], 1),
                   "ab_lAb": ([64, 3], 0), "ab_lB": ([64, 3 * 1024], 1),
                   "ab_lam": ([128, 48], 0), "ab_lbb": ([128, 48], 0),
                   "ab_mhd": ([128, 5 * 16 * 64], 1), "ab_wA": ([128, 16 * 64], 1),
                   "ab_wB": ([128, 16 * 64], 1), "ab_lamb": ([128, 8], 0),
                   "ab_outb": ([128, 8], 0), "ab_msk": ([128, 256], 2)}
            W, Wb = {}, {}
            for n, (sh, cast) in shp.items():
                dt = BF16 if cast else F32
                W[n], Wb[n] = sb(n[3:], sh, dt)
                eng = GD if cast == 1 else SY
                self.dma(eng, lambda e, n=n, t_=W[n]: e.dma_start(
                    out=t_[:], in_=self.ab_w[n][:]), Wb[n])
            wslot, wslotb = sb("wslot", [128, 8 * 1024], BF16)
            self.dma(GD, lambda e: e.dma_start(out=wslot[:],
                                               in_=self.ab_w["ab_gatew"][:]), wslotb)
            onesd, onesdb = self.input_sbuf(
                "c_onesd", np.ones((128, 128), np.float32).astype(BF), stack=ph)
            lamc, lamcb = sb("lamc", [128, 48])
            op(V, lambda e: e.tensor_add(lamc[:], W["ab_lam"][:], W["ab_lbb"][:]),
               reads=[Wb["ab_lam"], Wb["ab_lbb"]], writes=[lamcb])

            if use_cc:
                bin_ = nc.dram_tensor(tag + "bcin", [128 * 8], BF16,
                                      kind="Internal").ap()
                bout = nc.dram_tensor(tag + "bcout", [2 * 128 * 8], BF16,
                                      kind="Internal").ap()
                binb, boutb = Buf("bcin"), Buf("bcout")
                lastc = 128 + NT * 128 - 1
                self.dma(SY, lambda e, lastc=lastc: e.dma_start(
                    out=bin_.rearrange("(p f) -> p f", p=128),
                    in_=self.lnT[:, lastc::NTE * 128],
                    allow_slow_non_contiguous=True), binb, reads=[self.lnb])
                op(G, lambda e: e.collective_compute(
                    "AllGather", OP.bypass, ins=[bin_[:]], outs=[bout[:]],
                    replica_groups=[[2 * i, 2 * i + 1]
                                    for i in range(self.n_cores // 2)]),
                   reads=[binb], writes=[boutb])
                bcol, bcolb = sb("bcol", [128, 8], BF16)
                self.dma(SY, lambda e: e.dma_start(
                    out=bcol[:], in_=bout[0:128 * 8].rearrange(
                        "(p f) -> p f", p=128)), bcolb, reads=[boutb])
                op(V, lambda e: e.tensor_scalar_mul(
                    bcol[:], bcol[:], self.cpar[:, 0:1]),
                   reads=[bcolb, self.cparb], writes=[bcolb])
                op(V, lambda e: e.tensor_copy(self.lnT[:, 127::NTE * 128],
                                              bcol[:]),
                   reads=[bcolb], writes=[self.lnb])
            else:
                op(V, lambda e: e.memset(self.lnT[:, 127::NTE * 128], 0.0),
                   writes=[self.lnb])

            Sst, Sstb = sb("Sst", [128, 2 * 512], BF16)
            op(V, lambda e: e.memset(Sst[:], 0.0), writes=[Sstb])
            spn = ("op", "rd", "ic", "ew", "ep", "gs")
            spl, splb = {}, {}
            for n in spn:
                spl[n] = nc.dram_tensor(tag + "sp" + n, [128, NT * 1024], BF16,
                                        kind="Internal").ap()
                splb[n] = Buf(tag + n)

            with contextlib.ExitStack() as pa:
                diff, diffb = sb("diff", [128, 1024], BF16, st=pa)
                zc, zcb = sb("zc", [128, 1024], BF16, st=pa)
                hdn, hdnb = sb("hdn", [64, 3 * 128], BF16, st=pa)
                dl, dlb = sb("dl", [128, 1024], BF16, st=pa)
                dd5, dd5b = sb("dd5", [128, 1024], BF16, st=pa)
                kv5, kv5b = {}, {}
                for n in ("kk", "vv", "rr", "ww", "uu"):
                    kv5[n], kv5b[n] = sb(n, [128, 1024], BF16, st=pa)
                wh, whb = sb("wh", [128, 1024], BF16, st=pa)
                lnw, lnwb = sb("lnw", [128, 1024], BF16, st=pa)
                bcum, bcumb = sb("bcum", [128, 1024], st=pa)
                ktk, ktkb = sb("ktk", [64, 16 * 128], BF16, st=pa)
                vtk, vtkb = sb("vtk", [64, 16 * 128], BF16, st=pa)
                utk, utkb = sb("utk", [64, 16 * 128], BF16, st=pa)
                lnwT, lnwTb = sb("lnwT", [64, 16 * 128], BF16, st=pa)
                tmpd, tmpdb = sb("tmpd", [64, 16 * 128], BF16, st=pa)
                khd, khdb = sb("khd", [128, 64], BF16, st=pa)
                khT, khTb = sb("khT", [64, 128], BF16, st=pa)
                sca, scab = sb("sca", [128, 64], st=pa)
                scl, sclb = sb("scl", [128, 4 * 64], st=pa)
                ptl, ptlb = sb("ptl", [64, 2 * 4 * 64], BF16, st=pa)
                opar, oparb = sb("opar", [128, 1024], st=pa)
                vrt, vrtb = sb("vrt", [64, 32], st=pa)
                icb_, icbb = sb("icb", [128, 2 * 512], BF16, st=pa)
                ewb_, ewbb = sb("ewb", [128, 2 * 512], BF16, st=pa)
                eprev, eprevb = sb("eprev", [128, 1024], BF16, st=pa)

                for ti in range(NT):
                    ec0 = 128 + ti * 128
                    for k in range(8):
                        op(V, lambda e, k=k, ec0=ec0: e.tensor_sub(
                            diff[:, k * 128:(k + 1) * 128],
                            self.lnT_cols(k, ec0 - 1, ec0 + 127),
                            self.lnT_cols(k, ec0, ec0 + 128)),
                           reads=[self.lnb], writes=[diffb])
                        op(V, lambda e, k=k, ec0=ec0: e.scalar_tensor_tensor(
                            zc[:, k * 128:(k + 1) * 128],
                            diff[:, k * 128:(k + 1) * 128],
                            W["ab_mux"][:, k:k + 1],
                            self.lnT_cols(k, ec0, ec0 + 128),
                            op0=OP.mult, op1=OP.add),
                           reads=[diffb, Wb["ab_mux"], self.lnb], writes=[zcb])
                        op(V, lambda e, k=k: e.tensor_mul(
                            zc[:, k * 128:(k + 1) * 128],
                            zc[:, k * 128:(k + 1) * 128],
                            diff[:, k * 128:(k + 1) * 128]),
                           reads=[diffb, zcb], writes=[zcb])
                        op(V, lambda e, k=k, ec0=ec0: e.tensor_add(
                            zc[:, k * 128:(k + 1) * 128],
                            zc[:, k * 128:(k + 1) * 128],
                            self.lnT_cols(k, ec0, ec0 + 128)),
                           reads=[zcb, self.lnb], writes=[zcb])
                    for p3 in range(3):
                        bA = (0, 2)[p3 % 2]
                        for k in range(8):
                            op(T, lambda e, p3=p3, k=k, bA=bA: e.matmul(
                                ps[bA][:64, 0:128],
                                W["ab_lA"][:, k * 192 + p3 * 64:
                                           k * 192 + (p3 + 1) * 64],
                                zc[:, k * 128:(k + 1) * 128],
                                start=(k == 0), stop=(k == 7)),
                                reads=[Wb["ab_lA"], zcb], writes=[pb[bA]])
                        op(S, lambda e, p3=p3, bA=bA: e.activation(
                            hdn[:, p3 * 128:(p3 + 1) * 128], ps[bA][:64, 0:128],
                            AF.Tanh, bias=W["ab_lAb"][:, p3:p3 + 1]),
                           reads=[pb[bA], Wb["ab_lAb"]], writes=[hdnb])
                    for g in range(6):
                        p3, gh = g // 2, g % 2
                        for k in range(8):
                            bB = (0, 2)[k % 2]
                            op(T, lambda e, p3=p3, gh=gh, k=k, bB=bB: e.matmul(
                                ps[bB][:, 0:128],
                                W["ab_lB"][gh * 32:gh * 32 + 32,
                                           p3 * 1024 + k * 128:
                                           p3 * 1024 + (k + 1) * 128],
                                hdn[gh * 32:gh * 32 + 32,
                                    p3 * 128:(p3 + 1) * 128],
                                start=True, stop=True),
                                reads=[Wb["ab_lB"], hdnb], writes=[pb[bB]])
                            op(V, lambda e, g=g, k=k, bB=bB: e.scalar_tensor_tensor(
                                dl[:, k * 128:(k + 1) * 128], ps[bB][:, 0:128],
                                lamc[:, g * 8 + k:g * 8 + k + 1],
                                diff[:, k * 128:(k + 1) * 128],
                                op0=OP.add, op1=OP.mult),
                               reads=[pb[bB], lamcb, diffb], writes=[dlb])
                        if g == 5:
                            for k in range(8):
                                op(V, lambda e, k=k, ec0=ec0: e.tensor_add(
                                    dd5[:, k * 128:(k + 1) * 128],
                                    dl[:, k * 128:(k + 1) * 128],
                                    self.lnT_cols(k, ec0, ec0 + 128)),
                                   reads=[dlb, self.lnb], writes=[dd5b])
                            continue
                        dstn = ("kk", "vv", "rr", "ww", "uu")[g]
                        dst, dstb = kv5[dstn], kv5b[dstn]
                        for pr in range(8):
                            bC = (1, 3)[pr % 2]
                            for hh in range(2):
                                h = pr * 2 + hh
                                po = 64 * hh
                                kt = h // 2
                                mof = (g * 16 + h) * 64
                                op(T, lambda e, po=po, kt=kt, mof=mof, ec0=ec0,
                                   bC=bC: e.matmul(
                                    ps[bC][po:po + 64, 0:128],
                                    W["ab_mhd"][po:po + 64, mof:mof + 64],
                                    self.lnT[po:po + 64,
                                             kt * NTE * 128 + ec0:
                                             kt * NTE * 128 + ec0 + 128],
                                    start=True, stop=False),
                                   reads=[Wb["ab_mhd"], self.lnb], writes=[pb[bC]])
                                op(T, lambda e, po=po, kt=kt, mof=mof, bC=bC:
                                   e.matmul(
                                    ps[bC][po:po + 64, 0:128],
                                    W["ab_mhd"][po:po + 64, mof:mof + 64],
                                    dl[po:po + 64, kt * 128:kt * 128 + 128],
                                    start=False, stop=True),
                                   reads=[Wb["ab_mhd"], dlb], writes=[pb[bC]])
                            op(V, lambda e, pr=pr, dst=dst, bC=bC: e.tensor_copy(
                                dst[:, pr * 128:(pr + 1) * 128], ps[bC][:, 0:128]),
                               reads=[pb[bC]], writes=[dstb])
                    for pr in range(8):
                        bD = (0, 2)[pr % 2]
                        bE = (1, 3)[pr % 2]
                        for hh in range(2):
                            h = pr * 2 + hh
                            po = 64 * hh
                            op(T, lambda e, po=po, h=h, pr=pr, bD=bD: e.matmul(
                                ps[bD][po:po + 64, 0:128],
                                W["ab_wA"][po:po + 64, h * 64:(h + 1) * 64],
                                kv5["ww"][po:po + 64, pr * 128:(pr + 1) * 128],
                                start=True, stop=True),
                                reads=[Wb["ab_wA"], kv5b["ww"]], writes=[pb[bD]])
                        op(S, lambda e, pr=pr, bD=bD: e.activation(
                            wh[:, pr * 128:(pr + 1) * 128], ps[bD][:, 0:128],
                            AF.Tanh),
                           reads=[pb[bD]], writes=[whb])
                        for hh in range(2):
                            h = pr * 2 + hh
                            po = 64 * hh
                            op(T, lambda e, po=po, h=h, pr=pr, bE=bE: e.matmul(
                                ps[bE][po:po + 64, 0:128],
                                W["ab_wB"][po:po + 64, h * 64:(h + 1) * 64],
                                wh[po:po + 64, pr * 128:(pr + 1) * 128],
                                start=True, stop=True),
                                reads=[Wb["ab_wB"], whb], writes=[pb[bE]])
                        op(S, lambda e, pr=pr, bE=bE: e.activation(
                            lnw[:, pr * 128:(pr + 1) * 128], ps[bE][:, 0:128],
                            AF.Exp, bias=W["ab_lamb"][:, pr:pr + 1]),
                           reads=[pb[bE], Wb["ab_lamb"]], writes=[lnwb])
                        op(V, lambda e, pr=pr: e.tensor_scalar(
                            lnw[:, pr * 128:(pr + 1) * 128],
                            lnw[:, pr * 128:(pr + 1) * 128], -1.0, -30.0,
                            op0=OP.mult, op1=OP.max),
                           reads=[lnwb], writes=[lnwb])
                        for c2 in range(2):
                            cof = pr * 128 + c2 * 64
                            op(V, lambda e, cof=cof: e.tensor_tensor_scan(
                                bcum[:, cof:cof + 64], lnw[:, cof:cof + 64],
                                lnw[:, cof:cof + 64], 0.0,
                                op0=OP.add, op1=OP.bypass),
                               reads=[lnwb], writes=[bcumb])
                    for pr in range(8):
                        for c2 in range(2):
                            dof = (c2 * 8 + pr) * 128
                            for src, srcb, dst, dstb2 in (
                                    (lnw, lnwb, lnwT, lnwTb),
                                    (kv5["kk"], kv5b["kk"], ktk, ktkb),
                                    (kv5["vv"], kv5b["vv"], vtk, vtkb),
                                    (kv5["uu"], kv5b["uu"], utk, utkb)):
                                pof, pob = nxt()
                                op(T, lambda e, pr=pr, c2=c2, src=src, pof=pof:
                                   e.transpose(
                                    self.pt8[:64, pof:pof + 128],
                                    src[:, pr * 128 + c2 * 64:
                                        pr * 128 + c2 * 64 + 64],
                                    self.identb[:]),
                                    reads=[srcb, self.identb_b],
                                    writes=[pob])
                                op(V, lambda e, dof=dof, dst=dst, pof=pof:
                                   e.tensor_copy(
                                    dst[:, dof:dof + 128],
                                    self.pt8[:64, pof:pof + 128]),
                                   reads=[pob], writes=[dstb2])
                    for pr in range(8):
                        for c2 in range(2):
                            dof = (c2 * 8 + pr) * 128
                            pof, pob = nxt()
                            op(T, lambda e, pr=pr, c2=c2, pof=pof: e.transpose(
                                self.pt8[:64, pof:pof + 128],
                                kv5["rr"][:, pr * 128 + c2 * 64:
                                          pr * 128 + c2 * 64 + 64],
                                self.identb[:]),
                                reads=[kv5b["rr"], self.identb_b],
                                writes=[pob])
                            op(V, lambda e, dof=dof, pof=pof: e.tensor_mul(
                                tmpd[:, dof:dof + 128],
                                self.pt8[:64, pof:pof + 128],
                                vtk[:, dof:dof + 128]),
                               reads=[pob, vtkb], writes=[tmpdb])
                    for h in range(16):
                        pr, hh = h // 2, h % 2
                        for c2 in range(2):
                            dof = (c2 * 8 + pr) * 128 + hh * 64
                            op(V, lambda e, dof=dof, h=h, c2=c2: e.tensor_reduce(
                                vrt[:64, h:h + 1] if c2 == 0 else
                                vrt[:64, h:h + 1],
                                tmpd[:, dof:dof + 64], AX, OP.add),
                               reads=[tmpdb], writes=[vrtb]) if False else None
                    for h in range(16):
                        pr, hh = h // 2, h % 2
                        for c2 in range(2):
                            dof = (c2 * 8 + pr) * 128 + hh * 64
                            op(V, lambda e, dof=dof, h=h, c2=c2: e.tensor_reduce(
                                vrt[:64, h * 2 + c2:h * 2 + c2 + 1],
                                tmpd[:, dof:dof + 64], AX, OP.add),
                               reads=[tmpdb], writes=[vrtb])
                    op(V, lambda e: e.tensor_mul(tmpd[:], utk[:], ktk[:]),
                       reads=[utkb, ktkb], writes=[tmpdb])
                    for h in range(16):
                        pr, hh = h // 2, h % 2
                        for c2 in range(2):
                            dof = (c2 * 8 + pr) * 128 + hh * 64
                            op(V, lambda e, dof=dof, h=h, c2=c2:
                               e.tensor_scalar_mul(
                                tmpd[:, dof:dof + 64], tmpd[:, dof:dof + 64],
                                vrt[:64, h * 2 + c2:h * 2 + c2 + 1]),
                               reads=[vrtb, tmpdb], writes=[tmpdb])
                    for pr in range(8):
                        for c2 in range(2):
                            dof = (c2 * 8 + pr) * 128
                            pof, pob = nxt()
                            op(T, lambda e, dof=dof, pof=pof: e.transpose(
                                self.pt8[:, pof:pof + 64],
                                tmpd[:64, dof:dof + 128],
                                self.identb[0:64, 0:64]),
                                reads=[tmpdb, self.identb_b], writes=[pob])
                            op(V, lambda e, pr=pr, c2=c2, pof=pof: e.tensor_copy(
                                opar[:, pr * 128 + c2 * 64:
                                     pr * 128 + c2 * 64 + 64],
                                self.pt8[:, pof:pof + 64]),
                               reads=[pob], writes=[oparb])
                    for c2 in range(2):
                        poc = 0
                        for pr in range(8):
                            cof = pr * 128 + c2 * 64
                            tof = (c2 * 8 + pr) * 128
                            for hh in range(2):
                                op(T, lambda e, hh=hh, cof=cof: e.matmul(
                                    ps[2][0:64, hh * 64:hh * 64 + 64],
                                    kv5["vv"][64 * hh:64 * hh + 64, cof:cof + 64],
                                    kv5["rr"][64 * hh:64 * hh + 64, cof:cof + 64],
                                    start=True, stop=True),
                                   reads=[kv5b["vv"], kv5b["rr"]], writes=[pb[2]])
                            for hh in range(2):
                                op(V, lambda e, hh=hh: e.tensor_mul(
                                    ptl[0:64, hh * 256:(hh + 1) * 256]
                                    .rearrange("p (l f) -> p l f", l=4),
                                    ps[2][0:64, hh * 64:hh * 64 + 64]
                                    .unsqueeze(1).broadcast_to([64, 4, 64]),
                                    W["ab_msk"][0:64, :]
                                    .rearrange("p (l f) -> p l f", l=4)),
                                   reads=[pb[2], Wb["ab_msk"]], writes=[ptlb])
                            for lv in range(4):
                                sblk = 8 * (2 ** lv)
                                if lv == 0:
                                    op(V, lambda e, cof=cof: e.tensor_scalar_mul(
                                        sca[:, 0:8], bcum[:, cof:cof + 8], -1.0),
                                       reads=[bcumb], writes=[scab])
                                    op(V, lambda e, cof=cof: e.tensor_sub(
                                        sca[:, 8:64].rearrange(
                                            "p (b r) -> p b r", b=7, r=8),
                                        bcum[:, cof + 7:cof + 63:8]
                                        .unsqueeze(2).broadcast_to([128, 7, 8]),
                                        bcum[:, cof + 8:cof + 64].rearrange(
                                            "p (b r) -> p b r", b=7, r=8)),
                                       reads=[bcumb], writes=[scab])
                                else:
                                    nb = 64 // sblk
                                    half = sblk // 2
                                    op(V, lambda e, cof=cof, nb=nb, half=half,
                                       sblk=sblk: e.tensor_sub(
                                        sca[:, 0:64].rearrange(
                                            "p (b r) -> p b r", b=nb, r=sblk),
                                        bcum[:, cof + half - 1:
                                             cof + half - 1 + (nb - 1) * sblk + 1:
                                             sblk]
                                        .unsqueeze(2).broadcast_to([128, nb, sblk]),
                                        bcum[:, cof:cof + 64].rearrange(
                                            "p (b r) -> p b r", b=nb, r=sblk)),
                                       reads=[bcumb], writes=[scab])
                                op(V, lambda e: e.tensor_scalar_min(
                                    sca[:, 0:64], sca[:, 0:64], 60.0),
                                   reads=[scab], writes=[scab])
                                op(S, lambda e: e.activation(
                                    khd[:, 0:64], sca[:, 0:64], AF.Exp),
                                   reads=[scab], writes=[khdb])
                                op(V, lambda e, cof=cof: e.tensor_mul(
                                    khd[:, 0:64], khd[:, 0:64],
                                    kv5["kk"][:, cof:cof + 64]),
                                   reads=[khdb, kv5b["kk"]], writes=[khdb])
                                pof, pob = nxt()
                                op(T, lambda e, pof=pof: e.transpose(
                                    self.pt8[:64, pof:pof + 128], khd[:, 0:64],
                                    self.identb[:]),
                                    reads=[khdb, self.identb_b],
                                    writes=[pob])
                                op(V, lambda e, pof=pof: e.tensor_copy(
                                    khT[0:64, 0:128],
                                    self.pt8[:64, pof:pof + 128]),
                                   reads=[pob], writes=[khTb])
                                for hh in range(2):
                                    op(T, lambda e, hh=hh, lv=lv: e.matmul(
                                        ps[3][hh * 64:hh * 64 + 64,
                                              lv * 64:lv * 64 + 64],
                                        khT[0:64, hh * 64:hh * 64 + 64],
                                        ptl[0:64,
                                            hh * 256 + lv * 64:
                                            hh * 256 + lv * 64 + 64],
                                        start=True, stop=True),
                                       reads=[khTb, ptlb], writes=[pb[3]])
                                op(V, lambda e, cof=cof: e.tensor_sub(
                                    sca[:, 0:64], bcum[:, cof:cof + 64],
                                    lnw[:, cof:cof + 64]),
                                   reads=[bcumb, lnwb], writes=[scab])
                                if lv == 0:
                                    op(V, lambda e, cof=cof: e.tensor_sub(
                                        sca[:, 8:64].rearrange(
                                            "p (b r) -> p b r", b=7, r=8),
                                        sca[:, 8:64].rearrange(
                                            "p (b r) -> p b r", b=7, r=8),
                                        bcum[:, cof + 7:cof + 63:8]
                                        .unsqueeze(2).broadcast_to([128, 7, 8])),
                                       reads=[bcumb, scab], writes=[scab])
                                else:
                                    nb = 64 // sblk
                                    half = sblk // 2
                                    op(V, lambda e, cof=cof, nb=nb, half=half,
                                       sblk=sblk: e.tensor_sub(
                                        sca[:, 0:64].rearrange(
                                            "p (b r) -> p b r", b=nb, r=sblk),
                                        sca[:, 0:64].rearrange(
                                            "p (b r) -> p b r", b=nb, r=sblk),
                                        bcum[:, cof + half - 1:
                                             cof + half - 1 + (nb - 1) * sblk + 1:
                                             sblk]
                                        .unsqueeze(2).broadcast_to([128, nb, sblk])),
                                       reads=[bcumb, scab], writes=[scab])
                                op(V, lambda e: e.tensor_scalar_min(
                                    sca[:, 0:64], sca[:, 0:64], 0.0),
                                   reads=[scab], writes=[scab])
                                op(S, lambda e, lv=lv: e.activation(
                                    scl[:, lv * 64:(lv + 1) * 64], sca[:, 0:64],
                                    AF.Exp),
                                   reads=[scab], writes=[sclb])
                            op(V, lambda e: e.tensor_mul(
                                scl[:].rearrange("p (l f) -> p l f", l=4),
                                scl[:].rearrange("p (l f) -> p l f", l=4),
                                ps[3][:, 0:256].rearrange(
                                    "p (l f) -> p l f", l=4)),
                               reads=[pb[3], sclb], writes=[sclb])
                            op(V, lambda e: e.tensor_add(
                                scl[:, 0:128], scl[:, 0:128], scl[:, 128:256]),
                               reads=[sclb], writes=[sclb])
                            op(V, lambda e, cof=cof: e.scalar_tensor_tensor(
                                opar[:, cof:cof + 64], scl[:, 0:64], 1.0,
                                scl[:, 64:128], op0=OP.mult, op1=OP.add),
                               reads=[sclb], writes=[sclb]) if False else None
                            op(V, lambda e: e.tensor_add(
                                scl[:, 0:64], scl[:, 0:64], scl[:, 64:128]),
                               reads=[sclb], writes=[sclb])
                            op(V, lambda e, cof=cof: e.tensor_add(
                                opar[:, cof:cof + 64], opar[:, cof:cof + 64],
                                scl[:, 0:64]),
                               reads=[sclb, oparb], writes=[oparb])
                            op(V, lambda e, cof=cof: e.tensor_sub(
                                sca[:, 0:64],
                                bcum[:, cof + 63:cof + 64].broadcast_to([128, 64]),
                                bcum[:, cof:cof + 64]),
                               reads=[bcumb], writes=[scab])
                            op(S, lambda e: e.activation(khd[:, 0:64],
                                                         sca[:, 0:64], AF.Exp),
                               reads=[scab], writes=[khdb])
                            op(V, lambda e, cof=cof: e.tensor_mul(
                                khd[:, 0:64], khd[:, 0:64],
                                kv5["kk"][:, cof:cof + 64]),
                               reads=[khdb, kv5b["kk"]], writes=[khdb])
                            pof, pob = nxt()
                            op(T, lambda e, pof=pof: e.transpose(
                                self.pt8[:64, pof:pof + 128], khd[:, 0:64],
                                self.identb[:]),
                                reads=[khdb, self.identb_b], writes=[pob])
                            op(V, lambda e, pof=pof: e.tensor_copy(
                                khT[0:64, 0:128], self.pt8[:64, pof:pof + 128]),
                               reads=[pob], writes=[khTb])
                            for hh in range(2):
                                op(T, lambda e, hh=hh, tof=tof: e.matmul(
                                    ps[4][hh * 64:hh * 64 + 64, 0:64],
                                    vtk[0:64, tof + hh * 64:tof + hh * 64 + 64],
                                    khT[0:64, hh * 64:hh * 64 + 64],
                                    start=True, stop=True),
                                   reads=[vtkb, khTb], writes=[pb[4]])
                            op(V, lambda e, c2=c2, pr=pr: e.tensor_copy(
                                icb_[:, c2 * 512 + pr * 64:c2 * 512 + pr * 64 + 64],
                                ps[4][:, 0:64]),
                               reads=[pb[4]], writes=[icbb])
                            for hh in range(2):
                                op(T, lambda e, hh=hh, tof=tof: e.matmul(
                                    ps[4][hh * 64:hh * 64 + 64, 0:64],
                                    onesd[0:64, 0:64],
                                    lnwT[0:64, tof + hh * 64:tof + hh * 64 + 64],
                                    start=True, stop=True),
                                   reads=[lnwTb, onesdb], writes=[pb[4]])
                            op(S, lambda e, c2=c2, pr=pr: e.activation(
                                ewb_[:, c2 * 512 + pr * 64:
                                     c2 * 512 + pr * 64 + 64],
                                ps[4][:, 0:64], AF.Exp),
                               reads=[pb[4]], writes=[ewbb])

                    for pr in range(8):
                        op(V, lambda e, pr=pr: e.tensor_sub(
                            scl[:, 0:128], bcum[:, pr * 128:(pr + 1) * 128],
                            lnw[:, pr * 128:(pr + 1) * 128]),
                           reads=[bcumb, lnwb], writes=[sclb])
                        op(S, lambda e, pr=pr: e.activation(
                            eprev[:, pr * 128:(pr + 1) * 128], scl[:, 0:128],
                            AF.Exp),
                           reads=[sclb], writes=[eprevb])
                    for dc in range(2):
                        for k in range(8):
                            op(T, lambda e, k=k, dc=dc: e.matmul(
                                ps[4 + dc][:, 0:512],
                                dd5[:, k * 128:(k + 1) * 128],
                                wslot[:, k * 1024 + dc * 512:
                                      k * 1024 + (dc + 1) * 512],
                                start=(k == 0), stop=(k == 7)),
                                reads=[dd5b, wslotb], writes=[pb[4 + dc]])
                    for dc in range(2):
                        op(S, lambda e, dc=dc: e.activation(
                            dd5[:, dc * 512:(dc + 1) * 512],
                            ps[4 + dc][:, 0:512], AF.Silu),
                           reads=[pb[4 + dc]], writes=[dd5b])
                    for n, src, srcb in (("op", opar, oparb),
                                         ("rd", kv5["rr"], kv5b["rr"]),
                                         ("ic", icb_, icbb), ("ew", ewb_, ewbb),
                                         ("ep", eprev, eprevb),
                                         ("gs", dd5, dd5b)):
                        if n == "op":
                            self.dma(GD, lambda e, ti=ti, src=src: e.dma_start(
                                out=spl["op"][:, ti * 1024:(ti + 1) * 1024],
                                in_=src[:]), splb["op"], reads=[srcb])
                        else:
                            self.dma(SY, lambda e, n=n, ti=ti, src=src:
                                     e.dma_start(
                                out=spl[n][:, ti * 1024:(ti + 1) * 1024],
                                in_=src[:]), splb[n], reads=[srcb])
                    for c2 in range(2):
                        op(V, lambda e, c2=c2: e.tensor_mul(
                            Sst[:, 0:512], Sst[:, 0:512],
                            ewb_[:, c2 * 512:(c2 + 1) * 512]),
                           reads=[ewbb, Sstb], writes=[Sstb])
                        op(V, lambda e, c2=c2: e.tensor_add(
                            Sst[:, 0:512], Sst[:, 0:512],
                            icb_[:, c2 * 512:(c2 + 1) * 512]),
                           reads=[icbb, Sstb], writes=[Sstb])
                self.barrier()

            recvS, recvSb = sb("recvS", [128, 512], BF16)
            if use_cc:
                cin = nc.dram_tensor(tag + "ccin", [128 * 512], BF16,
                                     kind="Internal").ap()
                cout = nc.dram_tensor(tag + "ccout", [2 * 128 * 512], BF16,
                                      kind="Internal").ap()
                cinb, coutb = Buf(tag + "ccin"), Buf(tag + "ccout")
                self.dma(SY, lambda e: e.dma_start(
                    out=cin.rearrange("(p f) -> p f", p=128),
                    in_=Sst[:, 0:512]), cinb, reads=[Sstb])
                op(G, lambda e: e.collective_compute(
                    "AllGather", OP.bypass, ins=[cin[:]], outs=[cout[:]],
                    replica_groups=[[2 * i, 2 * i + 1]
                                    for i in range(self.n_cores // 2)]),
                   reads=[cinb], writes=[coutb])
                self.dma(SY, lambda e: e.dma_start(
                    out=recvS[:], in_=cout[0:128 * 512].rearrange(
                        "(p f) -> p f", p=128)), recvSb, reads=[coutb])
                op(V, lambda e: e.tensor_scalar_mul(
                    recvS[:], recvS[:], self.cpar[:, 0:1]),
                   reads=[recvSb, self.cparb], writes=[recvSb])
            else:
                op(V, lambda e: e.memset(recvS[:], 0.0), writes=[recvSb])
            op(V, lambda e: e.tensor_copy(Sst[:, 0:512], recvS[:]),
               reads=[recvSb], writes=[Sstb])

            self.dma(GD, lambda e: e.dma_start(
                out=wslot[:], in_=self.ab_w["ab_outw"][:]), wslotb)

            with contextlib.ExitStack() as pbx:
                for n in ("ab_gng", "ab_gnb"):
                    W[n], Wb[n] = sb(n[3:], [128, 1024], BF16, st=pbx)
                    self.dma(GD, lambda e, n=n, t_=W[n]: e.dma_start(
                        out=t_[:], in_=self.ab_w[n][:]), Wb[n])
                ro, rob = sb("ro", [128, 1024], BF16, st=pbx)
                oprr, oprrb = sb("oprr", [128, 1024], BF16, st=pbx)
                icr, icrb = sb("icr", [128, 1024], BF16, st=pbx)
                ewr, ewrb = sb("ewr", [128, 1024], BF16, st=pbx)
                epr, eprb = sb("epr", [128, 1024], BF16, st=pbx)
                gsr, gsrb = sb("gsr", [128, 1024], BF16, st=pbx)
                od, odb = sb("od", [128, 1024], st=pbx)
                otk, otkb = sb("otk", [128, 1024], st=pbx)
                on2, on2b = sb("on2", [128, 1024], BF16, st=pbx)
                st2, st2b = sb("st2", [128, 32], st=pbx)
                oT2, oT2b = sb("oT2", [128, 1024], BF16, st=pbx)
                for ti in range(NT):
                    tl = ti + 1
                    for n, dst, dstb2 in (("op", oprr, oprrb), ("rd", ro, rob),
                                          ("ic", icr, icrb), ("ew", ewr, ewrb),
                                          ("ep", epr, eprb), ("gs", gsr, gsrb)):
                        self.dma(SY, lambda e, n=n, ti=ti, dst=dst: e.dma_start(
                            out=dst[:], in_=spl[n][:, ti * 1024:(ti + 1) * 1024]),
                            dstb2, reads=[splb[n]])
                    gci = ti * 2
                    for c2 in range(2):
                        ci = gci + c2
                        pcur, pnxt = ci % 2, 1 - ci % 2
                        bI = c2 % 2
                        for pr in range(8):
                            for hh in range(2):
                                op(T, lambda e, pr=pr, hh=hh, c2=c2, pcur=pcur,
                                   bI=bI: e.matmul(
                                    ps[bI][hh * 64:hh * 64 + 64,
                                           pr * 64:pr * 64 + 64],
                                    Sst[hh * 64:hh * 64 + 64,
                                        pcur * 512 + pr * 64:
                                        pcur * 512 + pr * 64 + 64],
                                    ro[hh * 64:hh * 64 + 64,
                                       pr * 128 + c2 * 64:pr * 128 + c2 * 64 + 64],
                                    start=True, stop=True),
                                   reads=[Sstb, rob], writes=[pb[bI]])
                        op(V, lambda e, c2=c2, bI=bI: e.tensor_mul(
                            od.rearrange("p (b r) -> p b r", b=8, r=128)
                            [:, :, c2 * 64:c2 * 64 + 64],
                            ps[bI][:, 0:512].rearrange("p (b r) -> p b r",
                                                       b=8, r=64),
                            epr.rearrange("p (b r) -> p b r", b=8, r=128)
                            [:, :, c2 * 64:c2 * 64 + 64]),
                           reads=[pb[bI], eprb], writes=[odb])
                        op(V, lambda e, c2=c2: e.tensor_add(
                            od.rearrange("p (b r) -> p b r", b=8, r=128)
                            [:, :, c2 * 64:c2 * 64 + 64],
                            od.rearrange("p (b r) -> p b r", b=8, r=128)
                            [:, :, c2 * 64:c2 * 64 + 64],
                            oprr.rearrange("p (b r) -> p b r", b=8, r=128)
                            [:, :, c2 * 64:c2 * 64 + 64]),
                           reads=[oprrb, odb], writes=[odb])
                        op(V, lambda e, c2=c2, pcur=pcur, pnxt=pnxt: e.tensor_mul(
                            Sst[:, pnxt * 512:pnxt * 512 + 512],
                            Sst[:, pcur * 512:pcur * 512 + 512],
                            ewr[:, c2 * 512:(c2 + 1) * 512]),
                           reads=[ewrb, Sstb], writes=[Sstb])
                        op(V, lambda e, c2=c2, pnxt=pnxt: e.tensor_add(
                            Sst[:, pnxt * 512:pnxt * 512 + 512],
                            Sst[:, pnxt * 512:pnxt * 512 + 512],
                            icr[:, c2 * 512:(c2 + 1) * 512]),
                           reads=[icrb, Sstb], writes=[Sstb])
                    for pr in range(8):
                        bJ = 6
                        op(T, lambda e, pr=pr, bJ=bJ: e.transpose(
                            ps[bJ][:, 0:128], od[:, pr * 128:(pr + 1) * 128],
                            self.identf[:]),
                            reads=[odb, self.identf_b], writes=[pb[bJ]])
                        op(V, lambda e, pr=pr, bJ=bJ: e.tensor_copy(
                            otk[:, pr * 128:(pr + 1) * 128], ps[bJ][:, 0:128]),
                           reads=[pb[bJ]], writes=[otkb])
                    for h in range(16):
                        pr, hh = h // 2, h % 2
                        hof = pr * 128 + hh * 64
                        op(V, lambda e, hof=hof, h=h: e.tensor_reduce(
                            st2[:, h:h + 1], otk[:, hof:hof + 64], AX, OP.add),
                           reads=[otkb], writes=[st2b])
                    op(V, lambda e: e.tensor_scalar_mul(st2[:, 0:16],
                                                        st2[:, 0:16], -1.0 / 64),
                       reads=[st2b], writes=[st2b])
                    for h in range(16):
                        pr, hh = h // 2, h % 2
                        hof = pr * 128 + hh * 64
                        op(S, lambda e, hof=hof, h=h: e.activation(
                            on2[:, hof:hof + 64], otk[:, hof:hof + 64],
                            AF.Square, bias=st2[:, h:h + 1],
                            accum_out=st2[:, 16 + h:17 + h]),
                           reads=[otkb, st2b], writes=[on2b, st2b])
                    op(V, lambda e: e.tensor_scalar(
                        st2[:, 16:32], st2[:, 16:32], 1.0 / 64, 0.001,
                        op0=OP.mult, op1=OP.add),
                       reads=[st2b], writes=[st2b])
                    op(S, lambda e: e.activation(st2[:, 16:32], st2[:, 16:32],
                                                 AF.Ln),
                       reads=[st2b], writes=[st2b])
                    op(S, lambda e: e.activation(st2[:, 16:32], st2[:, 16:32],
                                                 AF.Exp, scale=-0.5),
                       reads=[st2b], writes=[st2b])
                    for h in range(16):
                        pr, hh = h // 2, h % 2
                        hof = pr * 128 + hh * 64
                        op(V, lambda e, hof=hof, h=h: e.tensor_scalar(
                            otk[:, hof:hof + 64], otk[:, hof:hof + 64],
                            st2[:, h:h + 1], st2[:, 16 + h:17 + h],
                            op0=OP.add, op1=OP.mult),
                           reads=[st2b, otkb], writes=[otkb])
                    op(V, lambda e: e.scalar_tensor_tensor(
                        otk[:], otk[:], 1.0, W["ab_gng"][:],
                        op0=OP.mult, op1=OP.mult),
                       reads=[Wb["ab_gng"], otkb], writes=[otkb])
                    op(V, lambda e: e.tensor_add(otk[:], otk[:], W["ab_gnb"][:]),
                       reads=[Wb["ab_gnb"], otkb], writes=[otkb])
                    op(V, lambda e: e.tensor_mul(on2[:], otk[:], gsr[:]),
                       reads=[otkb, gsrb], writes=[on2b])
                    for half in range(2):
                        for kk in range(4):
                            k = half * 4 + kk
                            op(T, lambda e, k=k, kk=kk, half=half: e.transpose(
                                self.pt8[:, half * 512 + kk * 128:
                                         half * 512 + (kk + 1) * 128],
                                on2[:, k * 128:(k + 1) * 128], self.identb[:]),
                                reads=[on2b, self.identb_b],
                                writes=[self.pt8b])
                        for kk in range(4):
                            k = half * 4 + kk
                            op(S, lambda e, k=k, half=half, kk=kk: e.copy(
                                oT2[:, k * 128:(k + 1) * 128],
                                self.pt8[:, half * 512 + kk * 128:
                                         half * 512 + (kk + 1) * 128]),
                               reads=[self.pt8b], writes=[oT2b])
                    for m in range(8):
                        bK = (6, 5)[m % 2]
                        for k in range(8):
                            op(T, lambda e, m=m, k=k, bK=bK: e.matmul(
                                ps[bK][:, 0:128],
                                wslot[:, k * 1024 + m * 128:
                                      k * 1024 + (m + 1) * 128],
                                oT2[:, k * 128:(k + 1) * 128],
                                start=(k == 0), stop=(k == 7)),
                                reads=[oT2b, wslotb], writes=[pb[bK]])
                        op(V, lambda e, m=m, bK=bK: e.tensor_single_scalar(
                            on2[:, m * 128:(m + 1) * 128], ps[bK][:, 0:128],
                            W["ab_outb"][:, m:m + 1], OP.add),
                           reads=[pb[bK], Wb["ab_outb"]], writes=[on2b])
                    for half in range(2):
                        for kk in range(4):
                            m = half * 4 + kk
                            op(T, lambda e, m=m, kk=kk, half=half: e.transpose(
                                self.pt8[:, half * 512 + kk * 128:
                                         half * 512 + (kk + 1) * 128],
                                on2[:, m * 128:(m + 1) * 128], self.identb[:]),
                                reads=[on2b, self.identb_b],
                                writes=[self.pt8b])
                        op(V, lambda e, half=half, tl=tl: e.tensor_add(
                            self.x_ap(tl, half * 512, (half + 1) * 512),
                            self.x_ap(tl, half * 512, (half + 1) * 512),
                            self.pt8[:, half * 512:half * 512 + 512]),
                           reads=[self.pt8b],
                           writes=[self.xb])
        self.barrier()

    def program(self):
        if "ffn1" in self.stages or "ffn2" in self.stages:
            self.declare_ffn_weights()
            self.emit_ffn_cache()
        if "ln1" in self.stages:
            self.emit_ln(0, self.NTE)
        if "attn_a" in self.stages:
            self.free_xov()
            self.declare_attn_a_weights()
            self.emit_attn_a("aa", use_cc="aa_cc" in self.stages)
        if "ffn1" in self.stages:
            if "ln2" in self.stages:
                self.emit_ln(1, self.NTE)
            self.emit_ffn("f1")
        if "attn_b" in self.stages:
            if "ln3" in self.stages:
                self.emit_ln(1, self.NTE)
            self.declare_attn_b_weights()
            self.emit_attn_b("ab", use_cc="ab_cc" in self.stages)
        if "ffn2" in self.stages:
            if "ln4" in self.stages:
                self.emit_ln(1, self.NTE)
            self.emit_ffn("f2")
        if "out_x" in self.stages:
            xo = self.dout("x_out", [128, self.NT * 1024], F32)
            ob = Buf("xout")
            self.dma(self.SY, lambda e: e.dma_start(
                out=xo[:], in_=self.x_tm[:, 0:self.NT * 1024]),
                ob, reads=[self.xb])
            self.SY.dep(ob.wtoks()[0])


def nc_psum(nc, name):
    return nc.psum_tensor(name, [128, 512], F32)


def tile_w(w):
    w = np.ascontiguousarray(np.asarray(w, np.float32))
    din, dout = w.shape
    assert din % 128 == 0, din
    return np.ascontiguousarray(
        w.reshape(din // 128, 128, dout).transpose(1, 0, 2).reshape(128, -1))


def pad_rows(w, rows):
    w = np.asarray(w, np.float32)
    out = np.zeros((rows, w.shape[1]), np.float32)
    out[:w.shape[0]] = w
    return out


def make_x0ext(x_shard, prev3):
    TPC = x_shard.shape[0]
    NTE = TPC // 128 + 1
    full = np.zeros((NTE * 128, 1024), np.float32)
    full[125:128] = prev3
    full[128:] = x_shard
    return np.ascontiguousarray(
        full.reshape(NTE, 128, 1024).transpose(1, 0, 2).reshape(128, -1))


def unmake_tm(img, TPC):
    NT = TPC // 128
    return np.ascontiguousarray(img).reshape(128, NT, 1024).transpose(1, 0, 2
                                                                      ).reshape(TPC, 1024)


FULL_STAGES = ("ln1", "attn_a", "aa_cc", "ln2", "ffn1", "ln3", "attn_b",
               "ab_cc", "ln4", "ffn2", "out_x")


def kernel(**inputs):
    from concourse.bass_utils import run_bass_kernel_spmd
    TPC = 2048
    x = np.asarray(inputs["L0_attn_a_hidden_states"], np.float32)
    B_, T_, D_ = x.shape
    nc = bass.Bass("TRN2", target_bir_lowering=False, debug=False)
    b = Builder(nc, TPC, stages=FULL_STAGES, n_cores=8, parity=0)
    b.prep_attn_a_weights(inputs, "L0_attn_a_")
    b.prep_ffn_weights(inputs, "L0_ffn_")
    b.prep_attn_b_weights(inputs, "L0_attn_b_")
    nc = b.build()

    in_maps = []
    for c in range(8):
        bb, half = c // 2, c % 2
        shard = x[bb, half * TPC:(half + 1) * TPC]
        prev3 = (x[bb, half * TPC - 3:half * TPC] if half == 1
                 else np.zeros((3, 1024), np.float32))
        m = dict(b.inputs)
        m["x0ext"] = make_x0ext(np.ascontiguousarray(shard),
                                np.ascontiguousarray(prev3))
        m["c_par"] = np.full((128, 1), float(half), np.float32)
        in_maps.append(m)
    res = run_bass_kernel_spmd(nc, in_maps, core_ids=list(range(8)))
    out = np.zeros((B_, T_, D_), np.float32)
    for c in range(8):
        bb, half = c // 2, c % 2
        out[bb, half * TPC:(half + 1) * TPC] = unmake_tm(
            res.results[c]["x_out"], TPC)
    return out
